# version 01
# baseline (speedup 1.0000x reference)
import numpy as np
import ml_dtypes
import concourse.bass as bass
import concourse.mybir as mybir
from concourse.bass_utils import run_bass_kernel_spmd

F32 = mybir.dt.float32
BF16 = mybir.dt.bfloat16
I32 = mybir.dt.int32
AF = mybir.ActivationFunctionType
ALU = mybir.AluOpType
AX = mybir.AxisListType

BF = ml_dtypes.bfloat16
NCORES = 8
TC = 2048
D = 1024
HD = 4096
NST = 16
FULLT = 16384

TWO_PI = float(2.0 * np.pi)
INV_2PI = float(1.0 / (2.0 * np.pi))



class Sem:
    def __init__(self, handle):
        self.h = handle
        self.v = 0

    def inc(self, k):
        self.v += k
        return self.v


class Ctx:
    ENGINES = ("sync", "tensor", "vector", "scalar", "gpsimd")

    def __init__(self, nc):
        self.nc = nc
        self.q = {e: [] for e in self.ENGINES}
        self.sems = {}
        self._exit = []
        for name in ("dsy", "dgp", "dve2", "mm", "dv", "ac", "fsy", "fgp", "fve"):
            self.sem(name)

    def sem(self, name):
        if name not in self.sems:
            cm = self.nc.semaphore(name)
            self.sems[name] = Sem(cm.__enter__())
            self._exit.append(cm)
        return self.sems[name]

    def sbuf(self, name, shape, dtype):
        cm = self.nc.sbuf_tensor(name, shape, dtype)
        t = cm.__enter__()
        self._exit.append(cm)
        return t

    def psum(self, name, shape, dtype):
        cm = self.nc.psum_tensor(name, shape, dtype)
        t = cm.__enter__()
        self._exit.append(cm)
        return t

    def emit(self, eng, fn, waits=(), inc=None):
        waits = tuple(waits)
        if inc is not None:
            inc[0].inc(inc[1])

        def thunk(e, fn=fn, waits=waits, inc=inc):
            for s, v in waits:
                e.wait_ge(s.h, v)
            r = fn(e)
            if inc is not None:
                r.then_inc(inc[0].h, inc[1])
        self.q[eng].append(thunk)
        return inc[0].v if inc is not None else None

    def dma(self, eng, out, in_, waits=()):
        sem = {"sync": self.sems["dsy"], "gpsimd": self.sems["dgp"],
               "vector": self.sems["dve2"]}[eng]
        return self.emit(eng, lambda e, o=out, i=in_: e.dma_start(out=o, in_=i),
                         waits=waits, inc=(sem, 16))

    def fence(self, eng):
        q = {"sync": "dsy", "gpsimd": "dgp", "vector": "dve2"}[eng]
        f = {"sync": "fsy", "gpsimd": "fgp", "vector": "fve"}[eng]
        qs, fs = self.sems[q], self.sems[f]
        v = qs.v
        fs.inc(1)
        fv = fs.v

        def thunk(e, qs=qs, v=v, fs=fs):
            e.wait_ge(qs.h, v)
            e.nop().then_inc(fs.h, 1)
        self.q[eng].append(thunk)
        return (fs, fv)

    def mm(self, fn, waits=(), n=1):
        return self.emit("tensor", fn, waits=waits, inc=(self.sems["mm"], n))

    def dv(self, fn, waits=(), n=1):
        return self.emit("vector", fn, waits=waits, inc=(self.sems["dv"], n))

    def ac(self, fn, waits=(), n=1):
        return self.emit("scalar", fn, waits=waits, inc=(self.sems["ac"], n))

    def finish(self):
        with self.nc.Block() as block:
            for e in self.ENGINES:
                fns = self.q[e]
                getattr(block, e)(lambda eng, fns=fns: [f(eng) for f in fns])
        for cm in reversed(self._exit):
            cm.__exit__(None, None, None)

    @property
    def S(self):
        return self.sems



def emit_layernorm(ctx, pref, nchunks, T, getx, apply_fn, in_wait, ones_x, ones_sq,
                   ones_wait, ps_a, ps_b, eps_ap=None):
    S = ctx.S
    scr = getattr(ctx, "_ln_scratch", None)
    if scr is None or scr["T"] < T:
        rstdb_t = ctx.sbuf(pref + "_rstdb", [128, T], F32)
        pb_t = ctx.sbuf(pref + "_pb", [128, T], F32)
        if T >= 2048:
            rows = [rstdb_t[0:1, k * 512:k * 512 + 512] for k in range(4)]
        else:
            rows = [ctx.sbuf(pref + f"_r{k}", [1, 512], F32)[0:1, :] for k in range(4)]
        scr = dict(T=T, sq=pb_t[:, 0:512], rows=rows,
                   hbm=ctx.nc.dram_tensor(pref + "_bchbm", [2, T], F32),
                   rstdb=rstdb_t, pb=pb_t)
        ctx._ln_scratch = scr
    sq = scr["sq"]
    r_sum, r_var, r_std, r_p = scr["rows"]
    hbm, rstdb, pb = scr["rstdb"] * 0 if False else scr["hbm"], scr["rstdb"], scr["pb"]
    nseg = T // 512
    invD = float(1.0 / (nchunks * 128))

    for seg in range(nseg):
        c0 = seg * 512
        for i in range(nchunks):
            w = [in_wait, (S["mm"], S["mm"].v)]
            ctx.ac(lambda e, x=getx(i), c0=c0: e.activation(sq[:, :], x[:, c0:c0 + 512],
                                                            AF.Square), waits=w)
            acv = S["ac"].v
            mw = [(S["ac"], acv)]
            if seg == 0 and i == 0:
                mw += [in_wait]
            if i == 0:
                mw.append((S["dv"], S["dv"].v))
                mw.append((S["dgp"], S["dgp"].v * 0) if False else (S["ac"], acv))
            ctx.mm(lambda e, x=getx(i), c0=c0, i=i:
                   e.matmul(ps_a, ones_x, x[:, c0:c0 + 512],
                            start=(i == 0), stop=(i == nchunks - 1)), waits=mw)
            ctx.mm(lambda e, i=i:
                   e.matmul(ps_b, ones_sq, sq[:, :], start=(i == 0), stop=(i == nchunks - 1)))
        mmv = S["mm"].v
        evw = [(S["mm"], mmv)]
        if seg > 0:
            evw.append(hbf)
        ctx.dv(lambda e: e.tensor_copy(r_sum, ps_a), waits=evw)
        ctx.dv(lambda e: e.tensor_copy(r_var, ps_b))
        ctx.dv(lambda e: e.tensor_single_scalar(r_sum, r_sum, invD, ALU.mult),
               waits=[(S["dv"], S["dv"].v)])
        ctx.dv(lambda e: e.scalar_tensor_tensor(
            r_std, r_sum, -1.0, r_sum, ALU.mult, ALU.mult),
            waits=[(S["dv"], S["dv"].v)])
        ctx.dv(lambda e: e.scalar_tensor_tensor(
            r_var, r_var, invD, r_std, ALU.mult, ALU.add),
            waits=[(S["dv"], S["dv"].v)])
        dvv = S["dv"].v
        ctx.ac(lambda e: e.activation(r_std, r_var, AF.Ln, bias=eps_ap),
               waits=[(S["dv"], dvv)])
        ctx.ac(lambda e: e.activation(r_std, r_std, AF.Exp, scale=-0.5))
        acv = S["ac"].v
        ctx.dv(lambda e: e.tensor_mul(r_p, r_sum, r_std),
               waits=[(S["ac"], acv)])
        dvv = S["dv"].v
        ctx.dma("gpsimd", hbm[0:1, c0:c0 + 512], r_std, waits=[(S["dv"], dvv)])
        ctx.dma("gpsimd", hbm[1:2, c0:c0 + 512], r_p)
        hbf = ctx.fence("gpsimd")
    ctx.dma("gpsimd", rstdb[0:128, 0:T], hbm[0:1, 0:T].broadcast_to([128, T]),
            waits=[hbf, (S["dv"], S["dv"].v)])
    ctx.dma("gpsimd", pb[0:128, 0:T], hbm[1:2, 0:T].broadcast_to([128, T]))
    bcf = ctx.fence("gpsimd")
    for i in range(nchunks):
        apply_fn(i, rstdb[0:128, 0:T], pb[0:128, 0:T], bcf)
    return rstdb, pb


def ln_apply_simple(ctx, i, getx, getout, rstdb, pb, bcf, tmp, wcol=None, bcol=None,
                    extra_wait=()):
    S = ctx.S
    ctx.dv(lambda e: e.tensor_mul(tmp[:, :], getx(i), rstdb[:, :]),
           waits=[bcf] + list(extra_wait))
    if wcol is None:
        ctx.dv(lambda e: e.tensor_sub(getout(i), tmp[:, :], pb[:, :]),
               waits=[(S["dv"], S["dv"].v)])
    else:
        ctx.dv(lambda e: e.tensor_sub(tmp[:, :], tmp[:, :], pb[:, :]),
               waits=[(S["dv"], S["dv"].v)])
        ctx.dv(lambda e: e.tensor_scalar(getout(i), tmp[:, :], wcol, bcol,
                                         ALU.mult, ALU.add),
               waits=[(S["dv"], S["dv"].v)])



def build_la1():
    nc = bass.Bass("TRN2", target_bir_lowering=False)
    ctx = Ctx(nc)
    S = ctx.S

    xt = nc.dram_tensor("xt", [D, TC], F32, kind="ExternalInput").ap()
    inproj = nc.dram_tensor("inproj", [32, 128, 8, 128], F32, kind="ExternalInput").ap()
    bcw = nc.dram_tensor("bcw", [32, 128, 64], F32, kind="ExternalInput").ap()
    dtp = nc.dram_tensor("dtp", [128, 32], F32, kind="ExternalInput").ap()
    phases = nc.dram_tensor("phases", [128, 32], F32, kind="ExternalInput").ap()
    posrow = nc.dram_tensor("posrow", [1, TC], F32, kind="ExternalInput").ap()
    ones32 = nc.dram_tensor("ones32", [128, 1], F32, kind="ExternalInput").ap()
    cvecd = nc.dram_tensor("cvec", [128, 2], F32, kind="ExternalInput").ap()
    hdt_o = nc.dram_tensor("hdt", [32, 128, TC], BF16, kind="ExternalOutput").ap()
    encdt_o = nc.dram_tensor("encdt", [32, 128, TC], BF16, kind="ExternalOutput").ap()
    bcrows_o = nc.dram_tensor("bcrows", [64, TC], F32, kind="ExternalOutput").ap()
    z0dbg = nc.dram_tensor("z0dbg", [8, 128, TC], BF16, kind="ExternalOutput").ap()
    st0dbg = nc.dram_tensor("st0dbg", [4, TC], F32, kind="ExternalOutput").ap()

    xtt = [ctx.sbuf(f"xtt{i}", [128, TC], F32) for i in range(8)]
    z0 = [ctx.sbuf(f"z0{i}", [128, TC], BF16) for i in range(8)]
    onesc = ctx.sbuf("onesc", [128, 1], F32)
    cvec = ctx.sbuf("cvect", [128, 2], F32)
    dtc = ctx.sbuf("dtc", [128, 32], F32)
    sp1 = ctx.sbuf("sp1", [128, 32], F32)
    phc = ctx.sbuf("phc", [128, 32], F32)
    posb = ctx.sbuf("posb", [128, TC], F32)
    wip = [ctx.sbuf(f"wip{j}", [128, 1024], BF16) for j in range(2)]
    wbc = [ctx.sbuf(f"wbc{j}", [128, 64], BF16) for j in range(2)]
    hdch = ctx.sbuf("hdch", [128, TC], BF16)
    encch = ctx.sbuf("encch", [128, TC], BF16)
    costt = ctx.sbuf("costt", [128, TC], F32)
    red = ctx.sbuf("red", [128, TC], F32)
    redi = ctx.sbuf("redi", [128, TC], I32)
    lntmp = ctx.sbuf("lntmp", [128, TC], F32)
    bcev = ctx.sbuf("bcev", [64, TC], F32)

    ps_hd = [ctx.psum(f"ps_hd{j}", [128, 512], F32) for j in range(2)]
    ps_ln = [ctx.psum(f"ps_ln{j}", [1, 512], F32) for j in range(2)]
    ps_bc = ctx.psum("ps_bc", [64, TC], F32)

    xv = [ctx.dma("sync", xtt[i][:, :], xt.rearrange("(c p) t -> c p t", p=128)[i])
          for i in range(8)]
    ov = ctx.dma("sync", onesc[:, :], ones32)
    ctx.dma("sync", cvec[:, :], cvecd)
    dv_ = ctx.dma("sync", dtc[:, :], dtp)
    phv = ctx.dma("sync", phc[:, :], phases)
    psv = ctx.dma("sync", posb[:, :], posrow.broadcast_to([128, TC]))
    ldall = ctx.fence("sync")

    ctx.ac(lambda e: e.activation(sp1[:, :], dtc[:, :], AF.Abs), waits=[ldall])
    ctx.ac(lambda e: e.activation(sp1[:, :], sp1[:, :], AF.Exp, scale=-1.0))
    ctx.ac(lambda e: e.activation(sp1[:, :], sp1[:, :], AF.Ln, bias=cvec[0:128, 1:2]))
    acv = S["ac"].v
    ctx.dv(lambda e: e.tensor_single_scalar(dtc[:, :], dtc[:, :], 0.0, ALU.max),
           waits=[(S["ac"], acv)])
    ctx.dv(lambda e: e.tensor_add(dtc[:, :], dtc[:, :], sp1[:, :]),
           waits=[(S["dv"], S["dv"].v)])
    spv = S["dv"].v

    def apply0(i, rstdb, pb, bcf):
        ln_apply_simple(ctx, i, lambda k: xtt[k][:, :], lambda k: z0[k][:, :],
                        rstdb, pb, bcf, lntmp)

    emit_layernorm(ctx, "ln0", 8, TC, lambda i: xtt[i][:, :], apply0,
                   ldall, onesc[:, 0:1], onesc[:, 0:1], ldall,
                   ps_ln[0][0:1, :], ps_ln[1][0:1, :], eps_ap=cvec[0:1, 0:1])
    z0done = (S["dv"], S["dv"].v)

    evac_dv = [0, 0]
    wip_mm = [0, 0]
    bc_mm = [0, 0]
    prev_hdst = None
    prev_encst = None
    for g in range(32):
        j = g % 2
        ctx.dma("gpsimd", wip[j][:, :], inproj[g],
                waits=[(S["mm"], wip_mm[j])] if wip_mm[j] else [])
        ctx.dma("gpsimd", wbc[j][:, :], bcw[g],
                waits=[(S["mm"], bc_mm[j])] if bc_mm[j] else [])
        wf = ctx.fence("gpsimd")
        for seg in range(4):
            pj = seg % 2
            mw = [wf, z0done, (S["dv"], evac_dv[pj])]
            for kc in range(8):
                ctx.mm(lambda e, j=j, kc=kc, seg=seg, pj=pj:
                       e.matmul(ps_hd[pj][:, :], wip[j][:, kc * 128:kc * 128 + 128],
                                z0[kc][:, seg * 512:seg * 512 + 512],
                                start=(kc == 0), stop=(kc == 7)),
                       waits=(mw if kc == 0 else ()))
            mmv = S["mm"].v
            ew = [(S["mm"], mmv)]
            if seg == 0 and prev_hdst is not None:
                ew.append(prev_hdst)
            ctx.dv(lambda e, seg=seg, pj=pj: e.tensor_copy(
                hdch[:, seg * 512:seg * 512 + 512], ps_hd[pj][:, :]), waits=ew)
            evac_dv[pj] = S["dv"].v
        wip_mm[j] = S["mm"].v
        hdv = S["dv"].v
        ctx.dma("sync", hdt_o[g], hdch[:, :], waits=[(S["dv"], hdv)])
        prev_hdst = ctx.fence("sync")
        ctx.dv(lambda e, g=g: e.tensor_single_scalar(costt[:, :], posb[:, :],
                                                     phc[:, g:g + 1], ALU.mult),
               waits=[(S["ac"], S["ac"].v)])
        ctx.dv(lambda e: e.tensor_scalar(red[:, :], costt[:, :], INV_2PI, 0.25,
                                         ALU.mult, ALU.add),
               waits=[(S["dv"], S["dv"].v)])
        ctx.dv(lambda e: e.tensor_copy(redi[:, :], red[:, :]),
               waits=[(S["dv"], S["dv"].v)])
        ctx.dv(lambda e: e.tensor_copy(costt[:, :], redi[:, :]),
               waits=[(S["dv"], S["dv"].v)])
        ctx.dv(lambda e: e.tensor_sub(red[:, :], red[:, :], costt[:, :]),
               waits=[(S["dv"], S["dv"].v)])
        rdv = S["dv"].v
        ctx.ac(lambda e: e.activation(costt[:, :], red[:, :], AF.Sin, scale=TWO_PI),
               waits=[(S["dv"], rdv)])
        csv = S["ac"].v
        ew = [(S["ac"], csv)]
        if prev_encst is not None:
            ew.append(prev_encst)
        ctx.dv(lambda e: e.tensor_mul(encch[:, :], hdch[:, :], costt[:, :]), waits=ew)
        encv = S["dv"].v
        for seg in range(4):
            ctx.mm(lambda e, j=j, seg=seg, g=g: e.matmul(
                ps_bc[:, seg * 512:seg * 512 + 512], wbc[j][:, :],
                encch[:, seg * 512:seg * 512 + 512],
                start=(g == 0), stop=(g == 31)),
                waits=([(S["dv"], encv)] if seg == 0 else ()))
        bcmm = S["mm"].v
        bc_mm[j] = bcmm
        ctx.dv(lambda e, g=g: e.tensor_single_scalar(encch[:, :], encch[:, :],
                                                     dtc[:, g:g + 1], ALU.mult),
               waits=[(S["mm"], bcmm)] + ([(S["dv"], spv)] if g == 0 else []))
        ctx.dma("sync", encdt_o[g], encch[:, :], waits=[(S["dv"], S["dv"].v)])
        prev_encst = ctx.fence("sync")
    mmv = S["mm"].v
    ctx.dv(lambda e: e.tensor_copy(bcev[0:16, :], ps_bc[0:16, :]), waits=[(S["mm"], mmv)])
    ctx.dv(lambda e: e.tensor_copy(bcev[32:48, :], ps_bc[32:48, :]))
    ctx.dma("sync", bcrows_o[0:16, :], bcev[0:16, :], waits=[(S["dv"], S["dv"].v)])
    ctx.dma("sync", bcrows_o[32:48, :], bcev[32:48, :])
    ctx.q["vector"].append(lambda e, v1=S["dgp"].v, v2=S["dsy"].v:
                           (e.wait_ge(S["dgp"].h, v1), e.wait_ge(S["dsy"].h, v2)))
    ctx.finish()
    return nc



def _prep_la1_inputs(P):
    x = P["L0_attn_x"]
    inproj = np.ascontiguousarray(
        P["L0_attn_input_proj"].reshape(8, 128, 32, 128).transpose(2, 1, 0, 3))
    bcwf = np.zeros((32, 128, 64), np.float32)
    bcwf[:, :, 0:16] = P["L0_attn_b_proj"].reshape(32, 128, NST)
    bcwf[:, :, 32:48] = P["L0_attn_c_proj"].reshape(32, 128, NST)
    dtp = np.ascontiguousarray(P["L0_attn_dt_proj"].reshape(32, 128).T)
    phases = np.ascontiguousarray(P["L0_attn_floquet_phases"].reshape(32, 128).T)
    ones32 = np.ones((128, 1), np.float32)
    maps = []
    for c in range(NCORES):
        b, h = c >> 1, c & 1
        xs = np.ascontiguousarray(x[b, h * TC:(h + 1) * TC, :].T)
        pos = np.arange(h * TC, (h + 1) * TC, dtype=np.float32)[None, :]
        cv = np.zeros((128, 2), np.float32); cv[:, 0] = 1e-5; cv[:, 1] = 1.0
        maps.append(dict(xt=xs, inproj=inproj, bcw=bcwf, dtp=dtp, phases=phases,
                         posrow=np.ascontiguousarray(pos), ones32=ones32, cvec=cv))
    return maps


def kernel(**inputs):
    raise NotImplementedError("under construction")



def build_la2():
    nc = bass.Bass("TRN2", target_bir_lowering=False)
    ctx = Ctx(nc)
    S = ctx.S

    encdt_r = nc.dram_tensor("encdt_r", [4, 128, FULLT], BF16, kind="ExternalInput").ap()
    bcb = nc.dram_tensor("bcb", [4, 64, 4096], F32, kind="ExternalInput").ap()
    dtp_sl = nc.dram_tensor("dtp_sl", [128, 4], F32, kind="ExternalInput").ap()
    alog = nc.dram_tensor("alog", [1, NST], F32, kind="ExternalInput").ap()
    idend = nc.dram_tensor("idenb", [128, 128], BF16, kind="ExternalInput").ap()
    cvecd = nc.dram_tensor("cvec", [128, 2], F32, kind="ExternalInput").ap()
    yt_o = nc.dram_tensor("yt", [4, 128, FULLT], BF16, kind="ExternalOutput").ap()

    enc = [ctx.sbuf(f"enc{j}", [128, FULLT], BF16) for j in range(2)]
    bb = [ctx.sbuf(f"bb{j}", [128, 4096], BF16) for j in range(2)]
    cb = [ctx.sbuf(f"cb{j}", [128, 4096], BF16) for j in range(2)]
    ubuf = ctx.sbuf("ubuf", [128, 1024], BF16)
    gbuf = [ctx.sbuf(f"gbuf{j}", [128, 1024], BF16) for j in range(2)]
    tbuf = [ctx.sbuf(f"tbuf{j}", [128, 1024], BF16) for j in range(2)]
    ystg = [ctx.sbuf(f"ystg{j}", [128, 4096], BF16) for j in range(2)]
    idt = ctx.sbuf("idt", [128, 128], BF16)
    dtc = ctx.sbuf("dtc", [128, 4], F32)
    sp1 = ctx.sbuf("sp2", [128, 4], F32)
    abc = ctx.sbuf("abc", [128, NST], F32)
    dec = ctx.sbuf("dec", [128, 4 * NST], F32)
    cvec = ctx.sbuf("cvect", [128, 2], F32)
    ps_y = [ctx.psum(f"ps_y{j}", [128, 2048], F32) for j in range(2)]

    ctx.dma("sync", idt[:, :], idend)
    ctx.dma("sync", dtc[:, :], dtp_sl)
    ctx.dma("sync", abc[:, :], alog.broadcast_to([128, NST]))
    ctx.dma("sync", cvec[:, :], cvecd)
    ldf = ctx.fence("sync")

    ctx.ac(lambda e: e.activation(sp1[:, :], dtc[:, :], AF.Abs), waits=[ldf])
    ctx.ac(lambda e: e.activation(sp1[:, :], sp1[:, :], AF.Exp, scale=-1.0))
    ctx.ac(lambda e: e.activation(sp1[:, :], sp1[:, :], AF.Ln, bias=cvec[0:128, 1:2]))
    ctx.ac(lambda e: e.activation(abc[:, :], abc[:, :], AF.Exp))
    acv = S["ac"].v
    ctx.dv(lambda e: e.tensor_single_scalar(dtc[:, :], dtc[:, :], 0.0, ALU.max),
           waits=[(S["ac"], acv)])
    ctx.dv(lambda e: e.tensor_add(dtc[:, :], dtc[:, :], sp1[:, :]),
           waits=[(S["dv"], S["dv"].v)])
    dtcv = S["dv"].v
    for dg in range(4):
        ctx.dv(lambda e, dg=dg: e.tensor_single_scalar(
            dec[:, dg * NST:(dg + 1) * NST], abc[:, :], dtc[:, dg:dg + 1], ALU.mult),
            waits=[(S["dv"], dtcv)] if dg == 0 else [])
    dvv = S["dv"].v
    ctx.ac(lambda e: e.activation(dec[:, :], dec[:, :], AF.Exp, scale=-1.0),
           waits=[(S["dv"], dvv)])
    decdone = (S["ac"], S["ac"].v)

    encf = [None, None]
    enc_lastuse = [0, 0]
    bcuse = [0, 0]
    tmm = [0, 0]
    yst = [None, None]
    ps_free = 0
    for dg in range(4):
        je = dg % 2
        ctx.dma("sync", enc[je][:, :], encdt_r[dg],
                waits=[(S["dv"], enc_lastuse[je])] if enc_lastuse[je] else [])
        encf[je] = ctx.fence("sync")
        for b in range(4):
            for n in range(NST):
                jb = n % 2
                bw = [(S["dv"], bcuse[jb])] if bcuse[jb] else []
                ctx.dma("gpsimd", bb[jb][:, :],
                        bcb[b][n:n + 1, :].broadcast_to([128, 4096]), waits=bw)
                ctx.dma("gpsimd", cb[jb][:, :],
                        bcb[b][32 + n:33 + n, :].broadcast_to([128, 4096]))
                bcf = ctx.fence("gpsimd")
                for tq in range(4):
                    jg, jt = tq % 2, tq % 2
                    c0 = b * 4096 + tq * 1024
                    t0 = tq * 1024
                    w = [bcf, encf[je]] if tq == 0 else []
                    if tq == 0 and n == 0 and b == 0 and dg == 0:
                        w.append(decdone)
                    ctx.dv(lambda e, je=je, jb=jb, c0=c0, t0=t0: e.tensor_mul(
                        ubuf[:, :], enc[je][:, c0:c0 + 1024], bb[jb][:, t0:t0 + 1024]),
                        waits=w)
                    init = 0.0 if tq == 0 else gbuf[(tq - 1) % 2][:, 1023:1024]
                    sw = [(S["dv"], S["dv"].v)] if tq > 0 else []
                    ctx.dv(lambda e, jg=jg, init=init, dg=dg, n=n: e.tensor_tensor_scan(
                        gbuf[jg][:, :],
                        dec[:, dg * NST + n:dg * NST + n + 1].broadcast_to([128, 1024]),
                        ubuf[:, :], init, ALU.mult, ALU.add), waits=sw)
                    tw = [(S["mm"], tmm[jt])] if tmm[jt] else []
                    ctx.dv(lambda e, jg=jg, jt=jt, jb=jb, t0=t0: e.tensor_mul(
                        tbuf[jt][:, :], gbuf[jg][:, :], cb[jb][:, t0:t0 + 1024]),
                        waits=tw)
                    dvv = S["dv"].v
                    for half in range(2):
                        mw = []
                        if half == 0:
                            mw = [(S["dv"], dvv), ldf]
                            if n == 0 and tq == 0 and ps_free:
                                mw.append((S["dv"], ps_free))
                        ctx.mm(lambda e, jt=jt, tq=tq, half=half, n=n: e.matmul(
                            ps_y[tq // 2][:, (tq % 2) * 1024 + half * 512:
                                          (tq % 2) * 1024 + half * 512 + 512],
                            idt[:, :], tbuf[jt][:, half * 512:half * 512 + 512],
                            start=(n == 0), stop=(n == NST - 1)), waits=mw)
                    tmm[jt] = S["mm"].v
                bcuse[jb] = S["dv"].v
            jy = b % 2
            mmv = S["mm"].v
            yw = [(S["mm"], mmv)]
            if yst[jy] is not None:
                yw.append(yst[jy])
            ctx.dv(lambda e, jy=jy: e.tensor_copy(ystg[jy][:, 0:2048], ps_y[0][:, :]),
                   waits=yw)
            ctx.dv(lambda e, jy=jy: e.tensor_copy(ystg[jy][:, 2048:4096], ps_y[1][:, :]))
            ps_free = S["dv"].v
            ctx.dma("sync", yt_o[dg][:, b * 4096:(b + 1) * 4096], ystg[jy][:, :],
                    waits=[(S["dv"], ps_free)])
            yst[jy] = ctx.fence("sync")
        enc_lastuse[je] = S["dv"].v
    ctx.q["vector"].append(lambda e, v1=S["dgp"].v, v2=S["dsy"].v:
                           (e.wait_ge(S["dgp"].h, v1), e.wait_ge(S["dsy"].h, v2)))
    ctx.finish()
    return nc



def build_la3():
    nc = bass.Bass("TRN2", target_bir_lowering=False)
    ctx = Ctx(nc)
    S = ctx.S
    TQ = 512

    hdt = nc.dram_tensor("hdt", [32, 128, TC], BF16, kind="ExternalInput").ap()
    ytc = nc.dram_tensor("ytc", [32, 128, TC], BF16, kind="ExternalInput").ap()
    skipw = nc.dram_tensor("skipw", [32, 128, 32, 128], F32, kind="ExternalInput").ap()
    outw = nc.dram_tensor("outw", [8, 128, 32, 128], F32, kind="ExternalInput").ap()
    xt = nc.dram_tensor("xt", [D, TC], F32, kind="ExternalInput").ap()
    ones32 = nc.dram_tensor("ones32", [128, 1], F32, kind="ExternalInput").ap()
    ones16d = nc.dram_tensor("ones16b", [128, 1], BF16, kind="ExternalInput").ap()
    cvecd = nc.dram_tensor("cvec", [128, 2], F32, kind="ExternalInput").ap()
    r1t_o = nc.dram_tensor("r1t", [D, TC], F32, kind="ExternalOutput").ap()
    zt_o = nc.dram_tensor("zt", [D, TC], F32, kind="ExternalOutput").ap()

    hdq = ctx.sbuf("hdq", [128, 32 * TQ], BF16)
    outhd = ctx.sbuf("outhd", [128, 32 * TQ], BF16)
    aout = ctx.sbuf("aout", [128, 8 * TQ], BF16)
    xq = [ctx.sbuf(f"xq{i}", [128, TQ], F32) for i in range(8)]
    wsk = [ctx.sbuf(f"wsk{j}", [128, 4096], BF16) for j in range(2)]
    yec = [ctx.sbuf(f"yec{j}", [128, TQ], BF16) for j in range(2)]
    onesc = ctx.sbuf("onesc", [128, 1], F32)
    ones16 = ctx.sbuf("ones16t", [128, 1], BF16)
    cvec = ctx.sbuf("cvect", [128, 2], F32)
    lntmp = ctx.sbuf("lntmp", [128, TQ], F32)
    zstg = [ctx.sbuf(f"zstg{j}", [128, TQ], F32) for j in range(2)]
    ps = [ctx.psum(f"ps{j}", [128, 512], F32) for j in range(2)]
    ps_ln = [ctx.psum(f"ps_ln{j}", [1, 512], F32) for j in range(2)]

    ctx.dma("sync", onesc[:, :], ones32)
    ctx.dma("sync", ones16[:, :], ones16d)
    ctx.dma("sync", cvec[:, :], cvecd)
    consts = ctx.fence("sync")

    qend_mm = 0
    qend_dv = 0
    qend_ac = 0
    zst_f = [None, None]
    for q in range(4):
        t0 = q * TQ
        lw = [(S["mm"], qend_mm), (S["dv"], qend_dv), (S["ac"], qend_ac)] if q else []
        for g in range(32):
            ctx.dma("sync", hdq[:, g * TQ:(g + 1) * TQ], hdt[g][:, t0:t0 + TQ],
                    waits=lw if g == 0 else [])
        for i in range(8):
            ctx.dma("sync", xq[i][:, :], xt.rearrange("(c p) t -> c p t", p=128)[i][:, t0:t0 + TQ])
        hdf = ctx.fence("sync")
        wsk_mm = [0, 0]
        yv = [None, None]
        for ec in range(32):
            j = ec % 2
            ctx.dma("gpsimd", wsk[j][:, :], skipw[ec],
                    waits=[(S["mm"], wsk_mm[j])] if wsk_mm[j] else [])
            wf = ctx.fence("gpsimd")
            ctx.dma("sync", yec[j][:, :], ytc[ec][:, t0:t0 + TQ],
                    waits=[(S["dv"], yv[j])] if yv[j] is not None else [])
            yf = ctx.fence("sync")
            jp = ec % 2
            for kc in range(32):
                ctx.mm(lambda e, j=j, kc=kc, jp=jp: e.matmul(
                    ps[jp][:, :], wsk[j][:, kc * 128:kc * 128 + 128],
                    hdq[:, kc * TQ:(kc + 1) * TQ],
                    start=(kc == 0), stop=(kc == 31)),
                    waits=([wf, hdf, (S["dv"], qend_dv)] if kc == 0 else ()))
            mmv = S["mm"].v
            wsk_mm[j] = mmv
            ctx.dv(lambda e, ec=ec, j=j, jp=jp: e.scalar_tensor_tensor(
                outhd[:, ec * TQ:(ec + 1) * TQ], ps[jp][:, :], 1.0, yec[j][:, :],
                ALU.mult, ALU.add), waits=[(S["mm"], mmv), yf])
            yv[j] = S["dv"].v
        ow_mm = [0, 0]
        for do in range(8):
            j = do % 2
            ctx.dma("gpsimd", wsk[j][:, :], outw[do],
                    waits=[(S["mm"], max(ow_mm[j], wsk_mm[j]))])
            wf = ctx.fence("gpsimd")
            jp = do % 2
            dvv = S["dv"].v
            for ec in range(32):
                ctx.mm(lambda e, j=j, ec=ec, jp=jp: e.matmul(
                    ps[jp][:, :], wsk[j][:, ec * 128:ec * 128 + 128],
                    outhd[:, ec * TQ:(ec + 1) * TQ],
                    start=(ec == 0), stop=(ec == 31)),
                    waits=([wf, (S["dv"], dvv)] if ec == 0 else ()))
            mmv = S["mm"].v
            ow_mm[j] = mmv
            aw = [(S["mm"], mmv)]
            if do == 0 and qend_ac:
                aw.append((S["ac"], qend_ac))
            ctx.dv(lambda e, do=do, jp=jp: e.tensor_copy(
                aout[:, do * TQ:(do + 1) * TQ], ps[jp][:, :]), waits=aw)
        aoutv = (S["dv"], S["dv"].v)

        def apply1(i, rstdb, pb, bcf):
            ctx.dv(lambda e, i=i: e.tensor_mul(lntmp[:, :], aout[:, i * TQ:(i + 1) * TQ],
                                               rstdb), waits=[bcf])
            ctx.dv(lambda e, i=i: e.tensor_sub(lntmp[:, :], lntmp[:, :], pb))
            ctx.dv(lambda e, i=i: e.tensor_add(xq[i][:, :], xq[i][:, :], lntmp[:, :]))

        emit_layernorm(ctx, f"lnA{q}", 8, TQ, lambda i: aout[:, i * TQ:(i + 1) * TQ],
                       apply1, aoutv, ones16[:, 0:1], onesc[:, 0:1], consts,
                       ps_ln[0][0:1, :], ps_ln[1][0:1, :], eps_ap=cvec[0:1, 0:1])
        r1v = (S["dv"], S["dv"].v)
        for i in range(8):
            ctx.dma("sync", r1t_o.rearrange("(c p) t -> c p t", p=128)[i][:, t0:t0 + TQ],
                    xq[i][:, :], waits=[r1v] if i == 0 else [])
        r1stf = ctx.fence("sync")

        def apply2(i, rstdb, pb, bcf):
            j = i % 2
            w = [bcf]
            if zst_f[j] is not None:
                w.append(zst_f[j])
            ctx.dv(lambda e, i=i, j=j: e.tensor_mul(zstg[j][:, :], xq[i][:, :], rstdb),
                   waits=w)
            ctx.dv(lambda e, i=i, j=j: e.tensor_sub(zstg[j][:, :], zstg[j][:, :], pb),
                   waits=[(S["dv"], S["dv"].v)])
            zv = S["dv"].v
            ctx.dma("sync", zt_o.rearrange("(c p) t -> c p t", p=128)[i][:, t0:t0 + TQ],
                    zstg[j][:, :], waits=[(S["dv"], zv)])
            zst_f[j] = ctx.fence("sync")

        emit_layernorm(ctx, f"lnB{q}", 8, TQ, lambda i: xq[i][:, :],
                       apply2, r1v, onesc[:, 0:1], onesc[:, 0:1], consts,
                       ps_ln[0][0:1, :], ps_ln[1][0:1, :], eps_ap=cvec[0:1, 0:1])
        qend_mm = S["mm"].v
        qend_dv = S["dv"].v
    ctx.q["vector"].append(lambda e, v1=S["dgp"].v, v2=S["dsy"].v:
                           (e.wait_ge(S["dgp"].h, v1), e.wait_ge(S["dsy"].h, v2)))
    ctx.finish()
    return nc



def build_lb1(own_half):
    nc = bass.Bass("TRN2", target_bir_lowering=False)
    ctx = Ctx(nc)
    S = ctx.S
    TF = 2 * TC
    o0 = own_half * TC

    zfull = nc.dram_tensor("zfull", [D, TF], F32, kind="ExternalInput").ap()
    wqkv = nc.dram_tensor("wqkv", [16, 128, 3, 8, 64], F32, kind="ExternalInput").ap()
    wproj = nc.dram_tensor("wproj", [8, 128, 8, 128], F32, kind="ExternalInput").ap()
    ln1wb = nc.dram_tensor("ln1wb", [2, 128, 8], F32, kind="ExternalInput").ap()
    idend = nc.dram_tensor("iden16", [128, 128], BF16, kind="ExternalInput").ap()
    ones32d = nc.dram_tensor("ones32", [128, 1], F32, kind="ExternalInput").ap()
    ones16d = nc.dram_tensor("ones16b", [128, 1], BF16, kind="ExternalInput").ap()
    cvecd = nc.dram_tensor("cvec", [128, 2], F32, kind="ExternalInput").ap()
    z1t_o = nc.dram_tensor("z1t", [D, TC], BF16, kind="ExternalOutput").ap()

    zb = [ctx.sbuf(f"zb{i}", [128, TF], BF16) for i in range(8)]
    pvt = [ctx.sbuf(f"pvt{i}", [128, TC], BF16) for i in range(8)]
    sc = ctx.sbuf("sc", [128, TF], F32)
    psb = ctx.sbuf("psb", [128, TF], BF16)
    pbt = [ctx.sbuf(f"pbt{j}", [128, 1024], BF16) for j in range(2)]
    qt = [ctx.sbuf(f"qt{j}", [64, TC], BF16) for j in range(2)]
    kt = [ctx.sbuf(f"kt{j}", [64, TF], BF16) for j in range(2)]
    vt = [ctx.sbuf(f"vt{j}", [128, (TF // 128) * 64], BF16) for j in range(2)]
    wq = [ctx.sbuf(f"wq{j}", [128, 3 * 512], BF16) for j in range(2)]
    wpj = [ctx.sbuf(f"wpj{j}", [128, 1024], BF16) for j in range(2)]
    z1s = [ctx.sbuf(f"z1s{j}", [128, 512], BF16) for j in range(2)]
    zstage = [ctx.sbuf(f"zst{j}", [128, 512], F32) for j in range(2)]
    idt = ctx.sbuf("idt", [128, 128], BF16)
    onesc = ctx.sbuf("onesc", [128, 1], F32)
    ones16 = ctx.sbuf("ones16t", [128, 1], BF16)
    cvec = ctx.sbuf("cvect", [128, 2], F32)
    lnw = ctx.sbuf("lnw", [128, 8], F32)
    lnb = ctx.sbuf("lnb", [128, 8], F32)
    nmx = ctx.sbuf("nmx", [128, 1], F32)
    zrow = ctx.sbuf("zrow", [128, 1], F32)
    rz = ctx.sbuf("rz", [128, 1], F32)
    pvq = ctx.sbuf("pvq", [128, 64], BF16)

    ps_s = [ctx.psum(f"ps_s{j}", [128, 512], F32) for j in range(2)]
    ps_qv = [ctx.psum(f"ps_qv{j}", [128, 512], F32) for j in range(2)]
    ps_pv = ctx.psum("ps_pv", [128, 64], F32)
    ps_t = [ctx.psum(f"ps_t{j}", [128, 1024], BF16) for j in range(2)]

    for i in range(8):
        ctx.dma("gpsimd", zb[i][:, :], zfull.rearrange("(c p) t -> c p t", p=128)[i])
    ctx.dma("sync", idt[:, :], idend)
    ctx.dma("sync", onesc[:, :], ones32d)
    ctx.dma("sync", ones16[:, :], ones16d)
    ctx.dma("sync", cvec[:, :], cvecd)
    ctx.dma("sync", lnw[:, :], ln1wb[0])
    ctx.dma("sync", lnb[:, :], ln1wb[1])
    consts = ctx.fence("sync")
    zf = ctx.fence("gpsimd")

    for part in range(2):
        c0 = part * TC

        def applyp(i, rstdb, pb, bcf, c0=c0):
            ctx.dv(lambda e, i=i, c0=c0: e.tensor_mul(
                sc[:, 0:TC], zb[i][:, c0:c0 + TC], rstdb), waits=[bcf, consts])
            ctx.dv(lambda e: e.tensor_sub(sc[:, 0:TC], sc[:, 0:TC], pb))
            ctx.dv(lambda e, i=i, c0=c0: e.tensor_scalar(
                zb[i][:, c0:c0 + TC], sc[:, 0:TC], lnw[:, i:i + 1], lnb[:, i:i + 1],
                ALU.mult, ALU.add))

        emit_layernorm(ctx, f"ln1_{part}", 8, TC, lambda i, c0=c0: zb[i][:, c0:c0 + TC],
                       applyp, zf, ones16[:, 0:1], onesc[:, 0:1], consts,
                       ps_s[0][0:1, 0:512], ps_s[1][0:1, 0:512], eps_ap=cvec[0:1, 0:1])
    lndone = (S["dv"], S["dv"].v)

    ntc = TF // 128
    wq_mm = [0, 0]
    qkv_dv = [0, 0]
    head_attn_mm = [0, 0]
    sc_ev = [0, 0]
    pbt_mm = [0, 0]
    pbt_ac = [0, 0]
    pvps_free = 0
    psb_mm = 0
    nm_ac = 0
    pvt_mm = 0
    for n in range(16):
        j = n % 2
        ctx.dma("gpsimd", wq[j][:, :], wqkv[n],
                waits=[(S["mm"], wq_mm[j])] if wq_mm[j] else [])
        wf = ctx.fence("gpsimd")
        for seg in range(4):
            jp = seg % 2
            for kc in range(8):
                ctx.mm(lambda e, j=j, kc=kc, seg=seg, jp=jp: e.matmul(
                    ps_qv[jp][0:64, :], wq[j][:, kc * 64:kc * 64 + 64],
                    zb[kc][:, o0 + seg * 512:o0 + seg * 512 + 512],
                    start=(kc == 0), stop=(kc == 7)),
                    waits=([wf, lndone, (S["dv"], qkv_dv[jp])] if kc == 0 else ()))
            mmv = S["mm"].v
            ew = [(S["mm"], mmv)]
            if seg < 2 and head_attn_mm[j]:
                ew.append((S["mm"], head_attn_mm[j]))
            ctx.dv(lambda e, j=j, seg=seg, jp=jp: e.tensor_single_scalar(
                qt[j][:, seg * 512:seg * 512 + 512], ps_qv[jp][0:64, :], 0.125, ALU.mult),
                waits=ew)
            qkv_dv[jp] = S["dv"].v
        for seg in range(8):
            jp = seg % 2
            for kc in range(8):
                ctx.mm(lambda e, j=j, kc=kc, seg=seg, jp=jp: e.matmul(
                    ps_qv[jp][0:64, :], wq[j][:, 512 + kc * 64:512 + kc * 64 + 64],
                    zb[kc][:, seg * 512:seg * 512 + 512],
                    start=(kc == 0), stop=(kc == 7)),
                    waits=([(S["dv"], qkv_dv[jp])] if kc == 0 else ()))
            mmv = S["mm"].v
            ctx.dv(lambda e, j=j, seg=seg, jp=jp: e.tensor_copy(
                kt[j][:, seg * 512:seg * 512 + 512], ps_qv[jp][0:64, :]),
                waits=[(S["mm"], mmv)])
            qkv_dv[jp] = S["dv"].v
        for tc4 in range(ntc // 4):
            jp = tc4 % 2
            for sub in range(4):
                tc = tc4 * 4 + sub
                for kc in range(8):
                    ctx.mm(lambda e, j=j, kc=kc, tc=tc, sub=sub, jp=jp: e.matmul(
                        ps_qv[jp][:, sub * 64:sub * 64 + 64],
                        zb[kc][:, tc * 128:tc * 128 + 128],
                        wq[j][:, 1024 + kc * 64:1024 + kc * 64 + 64],
                        start=(kc == 0 and sub == 0), stop=(kc == 7 and sub == 3)),
                        waits=([(S["dv"], qkv_dv[jp])] if (kc == 0 and sub == 0) else ()))
            mmv = S["mm"].v
            ctx.dv(lambda e, j=j, tc4=tc4, jp=jp: e.tensor_copy(
                vt[j][:, tc4 * 256:tc4 * 256 + 256], ps_qv[jp][:, 0:256]),
                waits=[(S["mm"], mmv)])
            qkv_dv[jp] = S["dv"].v
        qkvf = (S["dv"], S["dv"].v)
        wq_mm[j] = S["mm"].v

        for qtl in range(16):
            for seg in range(8):
                jp = seg % 2
                mw = [(S["ac"], sc_ev[jp])] if sc_ev[jp] else []
                if seg == 0:
                    mw.append(qkvf)
                ctx.mm(lambda e, j=j, qtl=qtl, seg=seg, jp=jp: e.matmul(
                    ps_s[jp][:, :], qt[j][:, qtl * 128:qtl * 128 + 128],
                    kt[j][:, seg * 512:seg * 512 + 512], start=True, stop=True),
                    waits=mw)
                mmv = S["mm"].v
                sw = [(S["mm"], mmv)]
                if seg == 0 and nm_ac:
                    sw.append((S["dv"], nm_dv))
                ctx.ac(lambda e, seg=seg, jp=jp: e.copy(
                    sc[:, seg * 512:seg * 512 + 512], ps_s[jp][:, :]), waits=sw)
                sc_ev[jp] = S["ac"].v
            acv = S["ac"].v
            rw = [(S["ac"], acv)]
            if nm_ac:
                rw.append((S["ac"], nm_ac))
            ctx.dv(lambda e: e.reduce_max(nmx[:, :], sc[:, :], axis=AX.X, negate=True),
                   waits=rw)
            nm_dv = S["dv"].v
            ew = [(S["dv"], nm_dv)]
            if psb_mm:
                ew.append((S["mm"], psb_mm))
            ctx.ac(lambda e: e.activation(psb[:, :], sc[:, :], AF.Exp,
                                          bias=nmx[:, 0:1], accum_out=zrow[:, 0:1]),
                   waits=ew)
            expv = S["ac"].v
            nm_ac = expv
            ctx.dv(lambda e: e.reciprocal(rz[:, :], zrow[:, 0:1]), waits=[(S["ac"], expv)])
            rzv = S["dv"].v
            for grp in range(4):
                jt = grp % 2
                tw = [(S["ac"], expv)]
                if pbt_ac[jt]:
                    tw.append((S["ac"], pbt_ac[jt]))
                for s8 in range(8):
                    kk = grp * 8 + s8
                    ctx.mm(lambda e, kk=kk, s8=s8, jt=jt: e.transpose(
                        ps_t[jt][:, s8 * 128:s8 * 128 + 128],
                        psb[:, kk * 128:kk * 128 + 128], idt[:, :]),
                        waits=(tw if s8 == 0 else ()))
                mmv = S["mm"].v
                ctx.ac(lambda e, jt=jt: e.copy(pbt[jt][:, :], ps_t[jt][:, :]),
                       waits=[(S["mm"], mmv)])
                pbt_ac[jt] = S["ac"].v
                pw = [(S["ac"], pbt_ac[jt])]
                if grp == 0 and pvps_free:
                    pw.append((S["dv"], pvps_free))
                for s8 in range(8):
                    kk = grp * 8 + s8
                    ctx.mm(lambda e, kk=kk, s8=s8, j=j, jt=jt: e.matmul(
                        ps_pv[:, :], pbt[jt][:, s8 * 128:s8 * 128 + 128],
                        vt[j][:, kk * 64:kk * 64 + 64],
                        start=(kk == 0), stop=(kk == 31)),
                        waits=(pw if s8 == 0 else ()))
                pbt_mm[jt] = S["mm"].v
            psb_mm = S["mm"].v
            mmv = S["mm"].v
            ctx.dv(lambda e: e.tensor_single_scalar(pvq[:, :], ps_pv[:, :], rz[:, 0:1],
                                                    ALU.mult),
                   waits=[(S["mm"], mmv), (S["dv"], rzv)])
            pvps_free = S["dv"].v
            ctx.mm(lambda e: e.transpose(ps_t[0][0:64, 0:128], pvq[:, :], idt[:, :]),
                   waits=[(S["dv"], pvps_free), (S["ac"], pbt_ac[0]),
                          (S["mm"], pbt_mm[0])])
            mmv = S["mm"].v
            ctx.ac(lambda e, n=n, qtl=qtl: e.copy(
                pvt[n // 2][(n % 2) * 64:(n % 2) * 64 + 64, qtl * 128:qtl * 128 + 128],
                ps_t[0][0:64, 0:128]), waits=[(S["mm"], mmv)])
            pbt_ac[0] = S["ac"].v
            pbt_mm[0] = S["mm"].v + 0
        head_attn_mm[j] = S["mm"].v

    pvtdone = (S["ac"], S["ac"].v)
    wp_mm = [0, 0]
    z1use = [None, None]
    zsf = [None, None]
    for do in range(8):
        j = do % 2
        ctx.dma("gpsimd", wpj[j][:, :], wproj[do],
                waits=[(S["mm"], wp_mm[j])] if wp_mm[j] else [])
        wf = ctx.fence("gpsimd")
        for tseg in range(4):
            jp = tseg % 2
            ctx.dma("sync", zstage[jp][:, :],
                    zfull.rearrange("(c p) t -> c p t", p=128)[do][:, o0 + tseg * 512:o0 + tseg * 512 + 512],
                    waits=[zsf[jp]] if zsf[jp] is not None else [])
            zstf = ctx.fence("sync")
            for kc in range(8):
                ctx.mm(lambda e, j=j, kc=kc, tseg=tseg, jp=jp: e.matmul(
                    ps_s[jp][:, :], wpj[j][:, kc * 128:kc * 128 + 128],
                    pvt[kc][:, tseg * 512:tseg * 512 + 512],
                    start=(kc == 0), stop=(kc == 7)),
                    waits=([wf, pvtdone, (S["dv"], z1use[jp])] if kc == 0 and z1use[jp] is not None
                           else [wf, pvtdone] if kc == 0 else ()))
            mmv = S["mm"].v
            zw = [(S["mm"], mmv), zstf]
            if zsf[jp] is not None:
                zw.append(zsf[jp])
            ctx.dv(lambda e, jp=jp: e.scalar_tensor_tensor(
                z1s[jp][:, :], ps_s[jp][:, :], 1.0, zstage[jp][:, :], ALU.mult, ALU.add),
                waits=zw)
            z1use[jp] = S["dv"].v
            ctx.dma("sync", z1t_o.rearrange("(c p) t -> c p t", p=128)[do][:, tseg * 512:tseg * 512 + 512],
                    z1s[jp][:, :], waits=[(S["dv"], z1use[jp])])
            zsf[jp] = ctx.fence("sync")
        wp_mm[j] = S["mm"].v
    ctx.q["vector"].append(lambda e, v1=S["dgp"].v, v2=S["dsy"].v:
                           (e.wait_ge(S["dgp"].h, v1), e.wait_ge(S["dsy"].h, v2)))
    ctx.finish()
    return nc



def build_lb2():
    nc = bass.Bass("TRN2", target_bir_lowering=False)
    ctx = Ctx(nc)
    S = ctx.S

    z1t = nc.dram_tensor("z1t", [D, TC], BF16, kind="ExternalInput").ap()
    r1t = nc.dram_tensor("r1t", [D, TC], F32, kind="ExternalInput").ap()
    ln2wb = nc.dram_tensor("ln2wb", [2, 128, 8], F32, kind="ExternalInput").ap()
    gatew = nc.dram_tensor("gatew", [8, 128, 8], F32, kind="ExternalInput").ap()
    gateb = nc.dram_tensor("gateb", [1, 8], F32, kind="ExternalInput").ap()
    w1r = nc.dram_tensor("w1r", [8, 2, 128, 8, TC], F32, kind="ExternalInput").ap()
    w2r = nc.dram_tensor("w2r", [8, 2, 128, 16, D], F32, kind="ExternalInput").ap()
    idend = nc.dram_tensor("iden16", [128, 128], BF16, kind="ExternalInput").ap()
    ones32d = nc.dram_tensor("ones32", [128, 1], F32, kind="ExternalInput").ap()
    ones16d = nc.dram_tensor("ones16b", [128, 1], BF16, kind="ExternalInput").ap()
    cvecd = nc.dram_tensor("cvec", [128, 2], F32, kind="ExternalInput").ap()
    r2t_o = nc.dram_tensor("r2t", [D, TC], F32, kind="ExternalOutput").ap()

    z1b = [ctx.sbuf(f"z1b{i}", [128, TC], BF16) for i in range(8)]
    ln2b = [ctx.sbuf(f"ln2b{i}", [128, TC], BF16) for i in range(8)]
    acc = ctx.sbuf("acc", [128, 16 * 1024], BF16)
    w1buf = ctx.sbuf("w1buf", [128, 8 * TC], BF16)
    w2buf = ctx.sbuf("w2buf", [128, 16 * 1024], BF16)
    hbuf = ctx.sbuf("hbuf", [128, 16 * 512], BF16)
    msk = ctx.sbuf("msk", [128, 16 * 8], F32)
    gwb = ctx.sbuf("gwb", [128, 8 * 8], BF16)
    gbb = ctx.sbuf("gbb", [128, 8], F32)
    glog = ctx.sbuf("glog", [128, 8], F32)
    gm1 = ctx.sbuf("gm1", [128, 1], F32)
    lnw = ctx.sbuf("lnw", [128, 8], F32)
    lnb = ctx.sbuf("lnb", [128, 8], F32)
    idt = ctx.sbuf("idt", [128, 128], BF16)
    onesc = ctx.sbuf("onesc", [128, 1], F32)
    ones16 = ctx.sbuf("ones16t", [128, 1], BF16)
    cvec = ctx.sbuf("cvect", [128, 2], F32)
    r1stg = [ctx.sbuf("r1stg0", [128, TC], F32)] * 2
    ps_h = [ctx.psum(f"ps_h{j}", [128, 512], F32) for j in range(2)]
    ps_y = [ctx.psum(f"ps_y{j}", [128, 512], F32) for j in range(2)]
    ps_t = [ctx.psum(f"ps_t{j}", [128, 1024], BF16) for j in range(2)]
    ps_ln = [ctx.psum(f"ps_ln{j}", [1, 512], F32) for j in range(2)]

    for i in range(8):
        ctx.dma("sync", z1b[i][:, :], z1t.rearrange("(c p) t -> c p t", p=128)[i])
    ctx.dma("sync", idt[:, :], idend)
    ctx.dma("sync", onesc[:, :], ones32d)
    ctx.dma("sync", ones16[:, :], ones16d)
    ctx.dma("sync", cvec[:, :], cvecd)
    ctx.dma("sync", lnw[:, :], ln2wb[0])
    ctx.dma("sync", lnb[:, :], ln2wb[1])
    zf = ctx.fence("sync")
    for kc in range(8):
        ctx.dma("gpsimd", gwb[:, kc * 8:kc * 8 + 8], gatew[kc])
    ctx.dma("gpsimd", gbb[:, :], gateb.broadcast_to([128, 8]))
    gwf = ctx.fence("gpsimd")

    def applyp(i, rstdb, pb, bcf):
        ctx.dv(lambda e, i=i: e.tensor_mul(ln2b[i][:, :], z1b[i][:, :], rstdb),
               waits=[bcf, zf])
        ctx.dv(lambda e, i=i: e.tensor_sub(ln2b[i][:, :], ln2b[i][:, :], pb),
               waits=[(S["dv"], S["dv"].v)])
        ctx.dv(lambda e, i=i: e.tensor_scalar(ln2b[i][:, :], ln2b[i][:, :],
                                              lnw[:, i:i + 1], lnb[:, i:i + 1],
                                              ALU.mult, ALU.add),
               waits=[(S["dv"], S["dv"].v)])

    emit_layernorm(ctx, "ln2", 8, TC, lambda i: z1b[i][:, :], applyp, zf,
                   ones16[:, 0:1], onesc[:, 0:1], zf,
                   ps_ln[0][0:1, :], ps_ln[1][0:1, :], eps_ap=cvec[0:1, 0:1])
    lndone = (S["dv"], S["dv"].v)

    gu = 0
    for tc in range(16):
        jp = tc % 2
        for kc in range(8):
            ctx.mm(lambda e, tc=tc, kc=kc, jp=jp: e.matmul(
                ps_h[jp][:, 0:8], ln2b[kc][:, tc * 128:tc * 128 + 128],
                gwb[:, kc * 8:kc * 8 + 8], start=(kc == 0), stop=(kc == 7)),
                waits=([lndone, gwf, (S["dv"], gu)] if kc == 0 else ()))
        mmv = S["mm"].v
        ctx.dv(lambda e, tc=tc, jp=jp: e.tensor_add(glog[:, :], ps_h[jp][:, 0:8],
                                                    gbb[:, :]), waits=[(S["mm"], mmv)])
        dvv = S["dv"].v
        ctx.dv(lambda e: e.reduce_max(gm1[:, :], glog[:, :], axis=AX.X),
               waits=[(S["dv"], dvv)])
        dvv = S["dv"].v
        ctx.dv(lambda e, tc=tc: e.tensor_single_scalar(
            msk[:, tc * 8:tc * 8 + 8], glog[:, :], gm1[:, 0:1], ALU.is_ge),
            waits=[(S["dv"], dvv)])
        gu = S["dv"].v
    gdone = (S["dv"], S["dv"].v)

    ctx.dv(lambda e: e.memset(acc[:, :], 0.0))
    accz = S["dv"].v
    w1_mm = 0
    w2_mm = 0
    acc_use = 0
    for e_ in range(8):
        for half in range(2):
            ctx.dma("gpsimd", w1buf[:, :], w1r[e_, half],
                    waits=[(S["mm"], w1_mm)] if w1_mm else [])
            w1f = ctx.fence("gpsimd")
            ctx.dma("gpsimd", w2buf[:, :], w2r[e_, half],
                    waits=[(S["mm"], w2_mm)] if w2_mm else [])
            w2f = ctx.fence("gpsimd")
            for tsub in range(4):
                for hc in range(16):
                    jp = hc % 2
                    for kc in range(8):
                        ctx.mm(lambda e, hc=hc, kc=kc, tsub=tsub, jp=jp: e.matmul(
                            ps_h[jp][:, :],
                            w1buf[:, kc * TC + hc * 128:kc * TC + hc * 128 + 128],
                            ln2b[kc][:, tsub * 512:tsub * 512 + 512],
                            start=(kc == 0), stop=(kc == 7)),
                            waits=([w1f, lndone, (S["ac"], getattr(ctx, "_hev", [0, 0])[jp])]
                                   if kc == 0 else ()))
                    mmv = S["mm"].v
                    hw = [(S["mm"], mmv)]
                    if tsub == 0 and hc < 2 and w2_mm:
                        hw.append((S["mm"], w2_mm))
                    ctx.ac(lambda e, hc=hc, jp=jp: e.activation(
                        hbuf[:, hc * 512:hc * 512 + 512], ps_h[jp][:, :],
                        AF.Gelu_apprx_tanh), waits=hw)
                    if not hasattr(ctx, "_hev"):
                        ctx._hev = [0, 0]
                    ctx._hev[jp] = S["ac"].v
                hdone = (S["ac"], S["ac"].v)
                for tcl in range(4):
                    tc = tsub * 4 + tcl
                    for ds in range(2):
                        jp = (tcl * 2 + ds) % 2
                        for hc in range(16):
                            ctx.mm(lambda e, hc=hc, tcl=tcl, ds=ds, jp=jp: e.matmul(
                                ps_y[jp][:, :],
                                hbuf[:, hc * 512 + tcl * 128:hc * 512 + tcl * 128 + 128],
                                w2buf[:, hc * 1024 + ds * 512:hc * 1024 + ds * 512 + 512],
                                start=(hc == 0), stop=(hc == 15)),
                                waits=([hdone, w2f, (S["dv"], acc_use)] if hc == 0 else ()))
                        mmv = S["mm"].v
                        ctx.dv(lambda e, e_=e_, tc=tc, ds=ds, jp=jp: e.scalar_tensor_tensor(
                            acc[:, tc * 1024 + ds * 512:tc * 1024 + ds * 512 + 512],
                            ps_y[jp][:, :], msk[:, tc * 8 + e_:tc * 8 + e_ + 1],
                            acc[:, tc * 1024 + ds * 512:tc * 1024 + ds * 512 + 512],
                            ALU.mult, ALU.add),
                            waits=[(S["mm"], mmv), gdone] + ([(S["dv"], accz)] if e_ == 0 else []))
                        acc_use = S["dv"].v
                w2_mm = S["mm"].v
            w1_mm = S["mm"].v
    moedone = (S["dv"], S["dv"].v)

    z2u = 0
    for tc in range(16):
        for kc in range(8):
            jt = kc % 2
            tw = [moedone]
            if getattr(ctx, "_ztev", [0, 0])[jt]:
                tw = [(S["dv"], ctx._ztev[jt])]
            ctx.mm(lambda e, tc=tc, kc=kc, jt=jt: e.transpose(
                ps_t[jt][:, 0:128], acc[:, tc * 1024 + kc * 128:tc * 1024 + kc * 128 + 128],
                idt[:, :]), waits=tw)
            mmv = S["mm"].v
            ctx.dv(lambda e, tc=tc, kc=kc, jt=jt: e.tensor_add(
                z1b[kc][:, tc * 128:tc * 128 + 128],
                z1b[kc][:, tc * 128:tc * 128 + 128], ps_t[jt][:, 0:128]),
                waits=[(S["mm"], mmv)])
            if not hasattr(ctx, "_ztev"):
                ctx._ztev = [0, 0]
            ctx._ztev[jt] = S["dv"].v
    z2done = (S["dv"], S["dv"].v)

    rsf = [None, None]

    def applyr2(i, rstdb, pb, bcf):
        lw = [rsf[0]] if rsf[0] is not None else []
        ctx.dma("sync", r1stg[0][:, :], r1t.rearrange("(c p) t -> c p t", p=128)[i], waits=lw)
        rf = ctx.fence("sync")
        tmpb = acc[:, 0:2048]
        ctx.dv(lambda e, i=i, tmpb=tmpb: e.tensor_mul(tmpb, z1b[i][:, :], rstdb), waits=[bcf])
        ctx.dv(lambda e, tmpb=tmpb: e.tensor_sub(tmpb, tmpb, pb))
        ctx.dv(lambda e, tmpb=tmpb: e.tensor_add(r1stg[0][:, :], r1stg[0][:, :], tmpb),
               waits=[rf])
        dvv = S["dv"].v
        ctx.dma("sync", r2t_o.rearrange("(c p) t -> c p t", p=128)[i], r1stg[0][:, :],
                waits=[(S["dv"], dvv)])
        rsf[0] = ctx.fence("sync")

    emit_layernorm(ctx, "lnR", 8, TC, lambda i: z1b[i][:, :], applyr2, z2done,
                   ones16[:, 0:1], onesc[:, 0:1], zf,
                   ps_ln[0][0:1, :], ps_ln[1][0:1, :], eps_ap=cvec[0:1, 0:1])
    ctx.q["vector"].append(lambda e, v1=S["dgp"].v, v2=S["dsy"].v:
                           (e.wait_ge(S["dgp"].h, v1), e.wait_ge(S["dsy"].h, v2)))
    ctx.finish()
    return nc



def build_lc1(own_half):
    nc = bass.Bass("TRN2", target_bir_lowering=False)
    ctx = Ctx(nc)
    S = ctx.S
    TF = 2 * TC
    o0 = own_half * TC
    KMAXT = o0 + TC

    r2full = nc.dram_tensor("r2full", [D, TF], F32, kind="ExternalInput").ap()
    wqkv2 = nc.dram_tensor("wqkv2", [16, 128, 3, 8, 64], F32, kind="ExternalInput").ap()
    mskcd = nc.dram_tensor("mskc", [4, 128, 512], F32, kind="ExternalInput").ap()
    idend = nc.dram_tensor("iden16", [128, 128], BF16, kind="ExternalInput").ap()
    ones32d = nc.dram_tensor("ones32", [128, 1], F32, kind="ExternalInput").ap()
    ones16d = nc.dram_tensor("ones16b", [128, 1], BF16, kind="ExternalInput").ap()
    cvecd = nc.dram_tensor("cvec", [128, 2], F32, kind="ExternalInput").ap()
    ao_o = nc.dram_tensor("ao", [16, TC, 64], BF16, kind="ExternalOutput").ap()

    nkc = KMAXT // 128
    zb = [ctx.sbuf(f"zb{i}", [128, TF], BF16) for i in range(8)]
    sc = ctx.sbuf("sc", [128, KMAXT], F32)
    psb = ctx.sbuf("psb", [128, KMAXT], BF16)
    pbt = [ctx.sbuf(f"pbt{j}", [128, 1024], BF16) for j in range(2)]
    qt = [ctx.sbuf(f"qt{j}", [64, TC], BF16) for j in range(2)]
    kt = [ctx.sbuf(f"kt{j}", [64, KMAXT], BF16) for j in range(2)]
    vt = [ctx.sbuf(f"vt{j}", [128, nkc * 64], BF16) for j in range(2)]
    wq = [ctx.sbuf(f"wq{j}", [128, 3 * 512], BF16) for j in range(2)]
    mskt = ctx.sbuf("mskt", [128, 4 * 512], F32)
    idt = ctx.sbuf("idt", [128, 128], BF16)
    onesc = ctx.sbuf("onesc", [128, 1], F32)
    ones16 = ctx.sbuf("ones16t", [128, 1], BF16)
    cvec = ctx.sbuf("cvect", [128, 2], F32)
    nmx = ctx.sbuf("nmx", [128, 1], F32)
    zrow = ctx.sbuf("zrow", [128, 1], F32)
    rz = ctx.sbuf("rz", [128, 1], F32)
    pvq = [ctx.sbuf(f"pvq{j}", [128, 64], BF16) for j in range(2)]
    ps_s = [ctx.psum(f"ps_s{j}", [128, 512], F32) for j in range(2)]
    ps_qv = [ctx.psum(f"ps_qv{j}", [128, 512], F32) for j in range(2)]
    ps_pv = ctx.psum("ps_pv", [128, 64], F32)
    ps_t = [ctx.psum(f"ps_t{j}", [128, 1024], BF16) for j in range(2)]

    for i in range(8):
        ctx.dma("gpsimd", zb[i][:, :], r2full.rearrange("(c p) t -> c p t", p=128)[i])
    zf = ctx.fence("gpsimd")
    ctx.dma("sync", idt[:, :], idend)
    ctx.dma("sync", onesc[:, :], ones32d)
    ctx.dma("sync", ones16[:, :], ones16d)
    ctx.dma("sync", cvec[:, :], cvecd)
    for r in range(4):
        ctx.dma("sync", mskt[:, r * 512:r * 512 + 512], mskcd[r])
    consts = ctx.fence("sync")

    for part in range(2):
        c0 = part * TC

        def applyp(i, rstdb, pb, bcf, c0=c0):
            ctx.dv(lambda e, i=i, c0=c0: e.tensor_mul(
                sc[:, 0:TC], zb[i][:, c0:c0 + TC], rstdb), waits=[bcf, consts])
            ctx.dv(lambda e, i=i, c0=c0: e.tensor_sub(
                zb[i][:, c0:c0 + TC], sc[:, 0:TC], pb), waits=[(S["dv"], S["dv"].v)])

        emit_layernorm(ctx, f"ln0_{part}", 8, TC, lambda i, c0=c0: zb[i][:, c0:c0 + TC],
                       applyp, zf, ones16[:, 0:1], onesc[:, 0:1], consts,
                       ps_s[0][0:1, 0:512], ps_s[1][0:1, 0:512], eps_ap=cvec[0:1, 0:1])
    lndone = (S["dv"], S["dv"].v)

    wq_mm = [0, 0]
    qkv_dv = [0, 0]
    head_attn_mm = [0, 0]
    sc_ev = [0, 0]
    sc_ev_dv = [0, 0]
    pbt_mm = [0, 0]
    pbt_ac = [0, 0]
    pvps_free = 0
    psb_mm = 0
    nm_ac = 0
    nm_dv = 0
    pvq_dma = [None, None]
    for n in range(16):
        j = n % 2
        ctx.dma("gpsimd", wq[j][:, :], wqkv2[n],
                waits=[(S["mm"], wq_mm[j])] if wq_mm[j] else [])
        wf = ctx.fence("gpsimd")
        for seg in range(4):
            jp = seg % 2
            for kc in range(8):
                ctx.mm(lambda e, j=j, kc=kc, seg=seg, jp=jp: e.matmul(
                    ps_qv[jp][0:64, :], wq[j][:, kc * 64:kc * 64 + 64],
                    zb[kc][:, o0 + seg * 512:o0 + seg * 512 + 512],
                    start=(kc == 0), stop=(kc == 7)),
                    waits=([wf, lndone, (S["dv"], qkv_dv[jp])] if kc == 0 else ()))
            mmv = S["mm"].v
            ew = [(S["mm"], mmv)]
            if seg < 2 and head_attn_mm[j]:
                ew.append((S["mm"], head_attn_mm[j]))
            ctx.dv(lambda e, j=j, seg=seg, jp=jp: e.tensor_single_scalar(
                qt[j][:, seg * 512:seg * 512 + 512], ps_qv[jp][0:64, :], 0.125, ALU.mult),
                waits=ew)
            qkv_dv[jp] = S["dv"].v
        for seg in range(KMAXT // 512):
            jp = seg % 2
            for kc in range(8):
                ctx.mm(lambda e, j=j, kc=kc, seg=seg, jp=jp: e.matmul(
                    ps_qv[jp][0:64, :], wq[j][:, 512 + kc * 64:512 + kc * 64 + 64],
                    zb[kc][:, seg * 512:seg * 512 + 512],
                    start=(kc == 0), stop=(kc == 7)),
                    waits=([(S["dv"], qkv_dv[jp])] if kc == 0 else ()))
            mmv = S["mm"].v
            ctx.dv(lambda e, j=j, seg=seg, jp=jp: e.tensor_copy(
                kt[j][:, seg * 512:seg * 512 + 512], ps_qv[jp][0:64, :]),
                waits=[(S["mm"], mmv)])
            qkv_dv[jp] = S["dv"].v
        for tc4 in range(nkc // 4):
            jp = tc4 % 2
            for sub in range(4):
                tc = tc4 * 4 + sub
                for kc in range(8):
                    ctx.mm(lambda e, j=j, kc=kc, tc=tc, sub=sub, jp=jp: e.matmul(
                        ps_qv[jp][:, sub * 64:sub * 64 + 64],
                        zb[kc][:, tc * 128:tc * 128 + 128],
                        wq[j][:, 1024 + kc * 64:1024 + kc * 64 + 64],
                        start=(kc == 0 and sub == 0), stop=(kc == 7 and sub == 3)),
                        waits=([(S["dv"], qkv_dv[jp])] if (kc == 0 and sub == 0) else ()))
            mmv = S["mm"].v
            ctx.dv(lambda e, j=j, tc4=tc4, jp=jp: e.tensor_copy(
                vt[j][:, tc4 * 256:tc4 * 256 + 256], ps_qv[jp][:, 0:256]),
                waits=[(S["mm"], mmv)])
            qkv_dv[jp] = S["dv"].v
        qkvf = (S["dv"], S["dv"].v)
        wq_mm[j] = S["mm"].v

        for qtl in range(16):
            q0g = o0 + qtl * 128
            nseg = (q0g + 128 + 511) // 512
            kw = nseg * 512
            for seg in range(nseg):
                jp = seg % 2
                mw = [(S["ac"], sc_ev[jp])] if sc_ev[jp] else []
                if sc_ev_dv[jp]:
                    mw.append((S["dv"], sc_ev_dv[jp]))
                if seg == 0:
                    mw.append(qkvf)
                ctx.mm(lambda e, j=j, qtl=qtl, seg=seg, jp=jp: e.matmul(
                    ps_s[jp][:, :], qt[j][:, qtl * 128:qtl * 128 + 128],
                    kt[j][:, seg * 512:seg * 512 + 512], start=True, stop=True),
                    waits=mw)
                mmv = S["mm"].v
                sw = [(S["mm"], mmv)]
                if seg == 0 and nm_dv:
                    sw.append((S["dv"], nm_dv))
                if seg == nseg - 1:
                    r = (q0g - seg * 512) // 128
                    ctx.dv(lambda e, seg=seg, jp=jp, r=r: e.scalar_tensor_tensor(
                        sc[:, seg * 512:seg * 512 + 512], ps_s[jp][:, :], 1.0,
                        mskt[:, r * 512:r * 512 + 512], ALU.mult, ALU.add), waits=sw)
                    sc_ev_dv[jp] = S["dv"].v
                else:
                    ctx.ac(lambda e, seg=seg, jp=jp: e.copy(
                        sc[:, seg * 512:seg * 512 + 512], ps_s[jp][:, :]), waits=sw)
                    sc_ev[jp] = S["ac"].v
            acv = S["ac"].v
            dvv0 = S["dv"].v
            rw = [(S["ac"], acv), (S["dv"], dvv0)]
            if nm_ac:
                rw.append((S["ac"], nm_ac))
            ctx.dv(lambda e, kw=kw: e.reduce_max(nmx[:, :], sc[:, 0:kw], axis=AX.X,
                                                 negate=True), waits=rw)
            nm_dv = S["dv"].v
            ew = [(S["dv"], nm_dv)]
            if psb_mm:
                ew.append((S["mm"], psb_mm))
            ctx.ac(lambda e, kw=kw: e.activation(psb[:, 0:kw], sc[:, 0:kw], AF.Exp,
                                                 bias=nmx[:, 0:1], accum_out=zrow[:, 0:1]),
                   waits=ew)
            expv = S["ac"].v
            nm_ac = expv
            ctx.dv(lambda e: e.reciprocal(rz[:, :], zrow[:, 0:1]), waits=[(S["ac"], expv)])
            rzv = S["dv"].v
            ngrp = (kw + 1023) // 1024
            for grp in range(ngrp):
                jt = grp % 2
                nsub = min(8, (kw - grp * 1024) // 128)
                tw = [(S["ac"], expv)]
                if pbt_ac[jt]:
                    tw.append((S["ac"], pbt_ac[jt]))
                for s8 in range(nsub):
                    kk = grp * 8 + s8
                    ctx.mm(lambda e, kk=kk, s8=s8, jt=jt: e.transpose(
                        ps_t[jt][:, s8 * 128:s8 * 128 + 128],
                        psb[:, kk * 128:kk * 128 + 128], idt[:, :]),
                        waits=(tw if s8 == 0 else ()))
                mmv = S["mm"].v
                ctx.ac(lambda e, jt=jt, nsub=nsub: e.copy(
                    pbt[jt][:, 0:nsub * 128], ps_t[jt][:, 0:nsub * 128]),
                    waits=[(S["mm"], mmv), (S["mm"], pbt_mm[jt])])
                pbt_ac[jt] = S["ac"].v
                pw = [(S["ac"], pbt_ac[jt])]
                if grp == 0 and pvps_free:
                    pw.append((S["dv"], pvps_free))
                for s8 in range(nsub):
                    kk = grp * 8 + s8
                    ctx.mm(lambda e, kk=kk, s8=s8, j=j, jt=jt, kw=kw: e.matmul(
                        ps_pv[:, :], pbt[jt][:, s8 * 128:s8 * 128 + 128],
                        vt[j][:, kk * 64:kk * 64 + 64],
                        start=(kk == 0), stop=(kk == kw // 128 - 1)),
                        waits=(pw if s8 == 0 else ()))
                pbt_mm[jt] = S["mm"].v
            psb_mm = S["mm"].v
            mmv = S["mm"].v
            jq = qtl % 2
            pq = [(S["mm"], mmv), (S["dv"], rzv)]
            if pvq_dma[jq] is not None:
                pq.append(pvq_dma[jq])
            ctx.dv(lambda e, jq=jq: e.tensor_single_scalar(
                pvq[jq][:, :], ps_pv[:, :], rz[:, 0:1], ALU.mult), waits=pq)
            pvps_free = S["dv"].v
            ctx.dma("sync", ao_o[n][qtl * 128:qtl * 128 + 128, :], pvq[jq][:, :],
                    waits=[(S["dv"], pvps_free)])
            pvq_dma[jq] = ctx.fence("sync")
        head_attn_mm[j] = S["mm"].v
    ctx.q["vector"].append(lambda e, v1=S["dgp"].v, v2=S["dsy"].v:
                           (e.wait_ge(S["dgp"].h, v1), e.wait_ge(S["dsy"].h, v2)))
    ctx.finish()
    return nc



def build_lc2():
    nc = bass.Bass("TRN2", target_bir_lowering=False)
    ctx = Ctx(nc)
    S = ctx.S

    aoresh = nc.dram_tensor("aoresh", [8, 128, TC], BF16, kind="ExternalInput").ap()
    woutf = nc.dram_tensor("woutf", [8, 128, 8, 128], F32, kind="ExternalInput").ap()
    r2t = nc.dram_tensor("r2t", [D, TC], F32, kind="ExternalInput").ap()
    ones32d = nc.dram_tensor("ones32", [128, 1], F32, kind="ExternalInput").ap()
    ones16d = nc.dram_tensor("ones16b", [128, 1], BF16, kind="ExternalInput").ap()
    cvecd = nc.dram_tensor("cvec", [128, 2], F32, kind="ExternalInput").ap()
    r3t_o = nc.dram_tensor("r3t", [D, TC], F32, kind="ExternalOutput").ap()

    aob = [ctx.sbuf(f"aob{i}", [128, TC], BF16) for i in range(8)]
    wob = [ctx.sbuf(f"wob{j}", [128, 1024], BF16) for j in range(2)]
    wsumc = ctx.sbuf("wsumc", [128, 8], F32)
    onesc = ctx.sbuf("onesc", [128, 1], F32)
    ones16 = ctx.sbuf("ones16t", [128, 1], BF16)
    cvec = ctx.sbuf("cvect", [128, 2], F32)
    lntmp = ctx.sbuf("lntmp", [128, TC], F32)
    rstg = ctx.sbuf("rstg", [128, TC], F32)
    ps = [ctx.psum(f"ps{j}", [128, 512], F32) for j in range(2)]
    ps_ln = [ctx.psum(f"ps_ln{j}", [1, 512], F32) for j in range(2)]

    for i in range(8):
        ctx.dma("sync", aob[i][:, :], aoresh[i])
    ctx.dma("sync", onesc[:, :], ones32d)
    ctx.dma("sync", ones16[:, :], ones16d)
    ctx.dma("sync", cvec[:, :], cvecd)
    aof = ctx.fence("sync")

    wu = 0
    wobf = [None, None]
    for rc in range(8):
        j = rc % 2
        w = [(S["mm"], wu)] if wu else []
        ctx.dma("gpsimd", wob[j][:, :], woutf[rc], waits=w)
        wobf[j] = ctx.fence("gpsimd")
        for dc in range(8):
            ctx.mm(lambda e, j=j, dc=dc, rc=rc: e.matmul(
                ps[0][0:128, dc:dc + 1], wob[j][:, dc * 128:dc * 128 + 128],
                ones16[:, 0:1], start=(rc == 0 and dc == 0), stop=(rc == 7 and dc == 7)),
                waits=([wobf[j], aof] if dc == 0 else ()))
        wu = S["mm"].v
    mmv = S["mm"].v
    ctx.dv(lambda e: e.tensor_copy(wsumc[:, :], ps[0][0:128, 0:8]), waits=[(S["mm"], mmv)])
    wsv = S["dv"].v

    for i in range(8):
        ctx.dv(lambda e, i=i: e.tensor_single_scalar(
            aob[i][:, :], aob[i][:, :], wsumc[:, i:i + 1], ALU.mult),
            waits=[(S["dv"], wsv), aof])
    sdone = (S["dv"], S["dv"].v)

    rsf = [None]

    def applyr(i, rstdb, pb, bcf):
        lw = [rsf[0]] if rsf[0] is not None else []
        ctx.dma("sync", rstg[:, :], r2t.rearrange("(c p) t -> c p t", p=128)[i], waits=lw)
        rf = ctx.fence("sync")
        ctx.dv(lambda e, i=i: e.tensor_mul(lntmp[:, :], aob[i][:, :], rstdb), waits=[bcf])
        ctx.dv(lambda e: e.tensor_sub(lntmp[:, :], lntmp[:, :], pb),
               waits=[(S["dv"], S["dv"].v)])
        ctx.dv(lambda e: e.tensor_add(rstg[:, :], rstg[:, :], lntmp[:, :]),
               waits=[rf, (S["dv"], S["dv"].v)])
        dvv = S["dv"].v
        ctx.dma("sync", r3t_o.rearrange("(c p) t -> c p t", p=128)[i], rstg[:, :],
                waits=[(S["dv"], dvv)])
        rsf[0] = ctx.fence("sync")

    emit_layernorm(ctx, "lnC", 8, TC, lambda i: aob[i][:, :], applyr, sdone,
                   ones16[:, 0:1], onesc[:, 0:1], aof,
                   ps_ln[0][0:1, :], ps_ln[1][0:1, :], eps_ap=cvec[0:1, 0:1])
    ctx.q["vector"].append(lambda e, v1=S["dgp"].v, v2=S["dsy"].v:
                           (e.wait_ge(S["dgp"].h, v1), e.wait_ge(S["dsy"].h, v2)))
    ctx.finish()
    return nc



def build_ld1(own_half):
    nc = bass.Bass("TRN2", target_bir_lowering=False)
    ctx = Ctx(nc)
    S = ctx.S
    TF = 2 * TC
    o0 = own_half * TC

    r3full = nc.dram_tensor("r3full", [D, TF], F32, kind="ExternalInput").ap()
    ln1wb = nc.dram_tensor("ln1wb", [2, 128, 8], F32, kind="ExternalInput").ap()
    gatewA = nc.dram_tensor("gatewA", [8, 128, 8], F32, kind="ExternalInput").ap()
    qw = nc.dram_tensor("qw", [8, 128, 8, 128], F32, kind="ExternalInput").ap()
    qbc = nc.dram_tensor("qbc", [128, 8], F32, kind="ExternalInput").ap()
    kw = nc.dram_tensor("kw", [128, 8, 128], F32, kind="ExternalInput").ap()
    kbc = nc.dram_tensor("kbc", [128, 1], F32, kind="ExternalInput").ap()
    vw = nc.dram_tensor("vw", [8, 128, 128], F32, kind="ExternalInput").ap()
    vbrow = nc.dram_tensor("vbrow", [1, 128], F32, kind="ExternalInput").ap()
    ow = nc.dram_tensor("ow", [8, 128, 8, 128], F32, kind="ExternalInput").ap()
    obb_d = nc.dram_tensor("obb_d", [8, D], F32, kind="ExternalInput").ap()
    reltT = nc.dram_tensor("reltT", [128, 132], F32, kind="ExternalInput").ap()
    bmasks = nc.dram_tensor("bmasks", [3, 128, 256], F32, kind="ExternalInput").ap()
    idend = nc.dram_tensor("iden16", [128, 128], BF16, kind="ExternalInput").ap()
    ones32d = nc.dram_tensor("ones32", [128, 1], F32, kind="ExternalInput").ap()
    ones16d = nc.dram_tensor("ones16b", [128, 1], BF16, kind="ExternalInput").ap()
    cvecd = nc.dram_tensor("cvec", [128, 2], F32, kind="ExternalInput").ap()
    z4t_o = nc.dram_tensor("z4t", [D, TC], BF16, kind="ExternalOutput").ap()
    relhbm = [nc.dram_tensor(f"relhbm{j}", [128, 129], F32) for j in range(2)]

    xnb = [ctx.sbuf(f"xnb{i}", [128, TF], BF16) for i in range(8)]
    z3o = [ctx.sbuf(f"z3o{i}", [128, TC], BF16) for i in range(8)]
    qtA = [ctx.sbuf(f"qtA{e}", [128, TC], BF16) for e in range(8)]
    kt = ctx.sbuf("kt", [128, TF], BF16)
    vtt = ctx.sbuf("vtt", [128, TF], BF16)
    sc = ctx.sbuf("sc", [128, TF], F32)
    psb = ctx.sbuf("psb", [128, TF], BF16)
    pbt = [ctx.sbuf(f"pbt{j}", [128, 1024], BF16) for j in range(2)]
    gA = ctx.sbuf("gA", [128, 16 * 8], F32)
    gTb = ctx.sbuf("gTb", [8, TC], BF16)
    relq = ctx.sbuf("relq", [128, 132], F32)
    skw = ctx.sbuf("skw", [128, 256], F32)
    bmt = ctx.sbuf("bmt", [128, 3 * 256], BF16)
    qwb = [ctx.sbuf(f"qwb{j}", [128, 1024], BF16) for j in range(2)]
    owb = [ctx.sbuf(f"owb{j}", [128, 1024], BF16) for j in range(2)]
    obb = ctx.sbuf("obb", [8, D], BF16)
    gwA = ctx.sbuf("gwA", [128, 64], BF16)
    qbt = ctx.sbuf("qbt", [128, 8], F32)
    kbt = ctx.sbuf("kbt", [128, 1], F32)
    vbb = ctx.sbuf("vbb", [128, 128], F32)
    glog = ctx.sbuf("glog", [128, 8], F32)
    gm1 = ctx.sbuf("gm1", [128, 1], F32)
    gm2 = ctx.sbuf("gm2", [128, 1], F32)
    geq = ctx.sbuf("geq", [128, 8], F32)
    gden = ctx.sbuf("gden", [128, 1], F32)
    idt = ctx.sbuf("idt", [128, 128], BF16)
    onesc = ctx.sbuf("onesc", [128, 1], F32)
    ones16 = ctx.sbuf("ones16t", [128, 1], BF16)
    cvec = ctx.sbuf("cvect", [128, 2], F32)
    nmx = ctx.sbuf("nmx", [128, 1], F32)
    zrow = ctx.sbuf("zrow", [128, 1], F32)
    rz = ctx.sbuf("rz", [128, 1], F32)
    rzg = ctx.sbuf("rzg", [128, 1], F32)
    cxb = ctx.sbuf("cxb", [128, 128], BF16)
    cxT = ctx.sbuf("cxT", [128, 128], BF16)
    gtb_s = ctx.sbuf("gtb_s", [128, 8], BF16)

    ps_s = [ctx.psum(f"ps_s{j}", [128, 512], F32) for j in range(2)]
    ps_qv = [ctx.psum(f"ps_qv{j}", [128, 512], F32) for j in range(2)]
    ps_pv = ctx.psum("ps_pv", [128, 128], F32)
    ps_moa = [ctx.psum(f"ps_moa{j}", [128, 512], F32) for j in range(2)]
    ps_t = ctx.psum("ps_t", [128, 1024], BF16)

    for i in range(8):
        ctx.dma("gpsimd", xnb[i][:, :], r3full.rearrange("(c p) t -> c p t", p=128)[i])
    for kc in range(8):
        ctx.dma("gpsimd", gwA[:, kc * 8:kc * 8 + 8], gatewA[kc])
    zf = ctx.fence("gpsimd")
    ctx.dma("sync", idt[:, :], idend)
    ctx.dma("sync", onesc[:, :], ones32d)
    ctx.dma("sync", ones16[:, :], ones16d)
    ctx.dma("sync", cvec[:, :], cvecd)
    ctx.dma("sync", qbt[:, :], qbc)
    ctx.dma("sync", kbt[:, :], kbc)
    ctx.dma("sync", vbb[:, :], vbrow.broadcast_to([128, 128]))
    consts = ctx.fence("sync")
    for r in range(3):
        ctx.dma("gpsimd", bmt[:, r * 256:r * 256 + 256], bmasks[r])
    bmf = ctx.fence("gpsimd")

    for part in range(2):
        c0 = part * TC

        def applyp(i, rstdb, pb, bcf, c0=c0):
            ctx.dv(lambda e, i=i, c0=c0: e.tensor_mul(
                sc[:, 0:TC], xnb[i][:, c0:c0 + TC], rstdb), waits=[bcf, consts])
            ctx.dv(lambda e, i=i, c0=c0: e.tensor_sub(
                xnb[i][:, c0:c0 + TC], sc[:, 0:TC], pb), waits=[(S["dv"], S["dv"].v)])

        emit_layernorm(ctx, f"lnz{part}", 8, TC, lambda i, c0=c0: xnb[i][:, c0:c0 + TC],
                       applyp, zf, ones16[:, 0:1], onesc[:, 0:1], consts,
                       ps_s[0][0:1, 0:512], ps_s[1][0:1, 0:512], eps_ap=cvec[0:1, 0:1])
    z3done = (S["dv"], S["dv"].v)
    for i in range(8):
        ctx.dv(lambda e, i=i: e.tensor_copy(z3o[i][:, :], xnb[i][:, o0:o0 + TC]),
               waits=[z3done] if i == 0 else [])
    z3ocp = (S["dv"], S["dv"].v)

    lnw = ctx.sbuf("lnw", [128, 8], F32)
    lnb = ctx.sbuf("lnb", [128, 8], F32)
    ctx.dma("sync", lnw[:, :], ln1wb[0])
    ctx.dma("sync", lnb[:, :], ln1wb[1])
    lnwf = ctx.fence("sync")
    for part in range(2):
        c0 = part * TC

        def applyq(i, rstdb, pb, bcf, c0=c0):
            ctx.dv(lambda e, i=i, c0=c0: e.tensor_mul(
                sc[:, 0:TC], xnb[i][:, c0:c0 + TC], rstdb), waits=[bcf, lnwf, z3ocp])
            ctx.dv(lambda e: e.tensor_sub(sc[:, 0:TC], sc[:, 0:TC], pb),
                   waits=[(S["dv"], S["dv"].v)])
            ctx.dv(lambda e, i=i, c0=c0: e.tensor_scalar(
                xnb[i][:, c0:c0 + TC], sc[:, 0:TC], lnw[:, i:i + 1], lnb[:, i:i + 1],
                ALU.mult, ALU.add), waits=[(S["dv"], S["dv"].v)])

        emit_layernorm(ctx, f"lnx{part}", 8, TC, lambda i, c0=c0: xnb[i][:, c0:c0 + TC],
                       applyq, z3ocp, ones16[:, 0:1], onesc[:, 0:1], consts,
                       ps_s[0][0:1, 0:512], ps_s[1][0:1, 0:512], eps_ap=cvec[0:1, 0:1])
    lndone = (S["dv"], S["dv"].v)

    gu = 0
    for tc in range(16):
        jp = tc % 2
        for kc in range(8):
            ctx.mm(lambda e, tc=tc, kc=kc, jp=jp: e.matmul(
                ps_qv[jp][:, 0:8], xnb[kc][:, o0 + tc * 128:o0 + tc * 128 + 128],
                gwA[:, kc * 8:kc * 8 + 8], start=(kc == 0), stop=(kc == 7)),
                waits=([lndone, zf, (S["dv"], gu)] if kc == 0 else ()))
        mmv = S["mm"].v
        ctx.dv(lambda e, jp=jp: e.tensor_copy(glog[:, :], ps_qv[jp][:, 0:8]),
               waits=[(S["mm"], mmv), (S["ac"], getattr(ctx, "_gprev_ac", 0))])
        dvv = S["dv"].v
        ctx.dv(lambda e: e.reduce_max(gm1[:, :], glog[:, :], axis=AX.X, negate=True),
               waits=[(S["dv"], dvv)])
        dvv = S["dv"].v
        ctx.dv(lambda e: e.tensor_single_scalar(geq[:, :], glog[:, :], gm1[:, 0:1],
                                                ALU.add), waits=[(S["dv"], dvv)])
        ctx.dv(lambda e: e.tensor_single_scalar(geq[:, :], geq[:, :], 0.0, ALU.is_ge),
               waits=[(S["dv"], S["dv"].v)])
        ctx.dv(lambda e: e.scalar_tensor_tensor(geq[:, :], geq[:, :], -1e30, glog[:, :],
                                                ALU.mult, ALU.add),
               waits=[(S["dv"], S["dv"].v)])
        ctx.dv(lambda e: e.reduce_max(gm2[:, :], geq[:, :], axis=AX.X, negate=True),
               waits=[(S["dv"], S["dv"].v)])
        ctx.dv(lambda e, tc=tc: e.tensor_single_scalar(
            gA[:, tc * 8:tc * 8 + 8], glog[:, :], gm2[:, 0:1], ALU.add),
            waits=[(S["dv"], S["dv"].v)])
        ctx.dv(lambda e, tc=tc: e.tensor_single_scalar(
            gA[:, tc * 8:tc * 8 + 8], gA[:, tc * 8:tc * 8 + 8], 0.0, ALU.is_ge),
            waits=[(S["dv"], S["dv"].v)])
        dvv = S["dv"].v
        ctx.ac(lambda e: e.activation(glog[:, :], glog[:, :], AF.Exp, bias=gm1[:, 0:1]),
               waits=[(S["dv"], dvv)])
        ctx.ac(lambda e: e.activation(gden[:, :], gm2[:, 0:1], AF.Exp, scale=-1.0,
                                      bias=gm1[:, 0:1]))
        ctx._gprev_ac = S["ac"].v
        acv = S["ac"].v
        ctx.dv(lambda e: e.tensor_single_scalar(gden[:, :], gden[:, :], 1.0, ALU.add),
               waits=[(S["ac"], acv)])
        ctx.dv(lambda e: e.reciprocal(gden[:, :], gden[:, :]),
               waits=[(S["dv"], S["dv"].v)])
        dvv = S["dv"].v
        acv2 = S["ac"].v
        ctx.dv(lambda e, tc=tc: e.tensor_mul(gA[:, tc * 8:tc * 8 + 8],
                                             gA[:, tc * 8:tc * 8 + 8], glog[:, :]),
               waits=[(S["dv"], dvv), (S["ac"], acv2)])
        ctx.dv(lambda e, tc=tc: e.tensor_single_scalar(
            gA[:, tc * 8:tc * 8 + 8], gA[:, tc * 8:tc * 8 + 8], gden[:, 0:1], ALU.mult),
            waits=[(S["dv"], S["dv"].v)])
        ctx.dv(lambda e, tc=tc: e.tensor_copy(gtb_s[:, :], gA[:, tc * 8:tc * 8 + 8]))
        dvv = S["dv"].v
        ctx.mm(lambda e, tc=tc: e.transpose(ps_t[0:8, 0:128], gtb_s[:, :], idt[:, :]),
               waits=[(S["dv"], dvv), (S["ac"], getattr(ctx, "_gtev", 0))])
        mmv = S["mm"].v
        ctx.ac(lambda e, tc=tc: e.copy(gTb[0:8, tc * 128:tc * 128 + 128],
                                       ps_t[0:8, 0:128]), waits=[(S["mm"], mmv)])
        ctx._gtev = S["ac"].v
        gu = S["dv"].v
    gdone = (S["dv"], S["dv"].v)
    gdone_ac = (S["ac"], S["ac"].v)

    qw_mm = [0, 0]
    qkv_dv = [0, 0]
    for e_ in range(8):
        j = e_ % 2
        ctx.dma("gpsimd", qwb[j][:, :], qw[e_],
                waits=[(S["mm"], qw_mm[j])] if qw_mm[j] else [])
        wf = ctx.fence("gpsimd")
        for seg in range(4):
            jp = seg % 2
            for kc in range(8):
                ctx.mm(lambda e, j=j, kc=kc, seg=seg, jp=jp: e.matmul(
                    ps_qv[jp][:, :], qwb[j][:, kc * 128:kc * 128 + 128],
                    xnb[kc][:, o0 + seg * 512:o0 + seg * 512 + 512],
                    start=(kc == 0), stop=(kc == 7)),
                    waits=([wf, lndone, (S["dv"], qkv_dv[jp])] if kc == 0 else ()))
            mmv = S["mm"].v
            ctx.dv(lambda e, e_=e_, seg=seg, jp=jp: e.tensor_scalar(
                qtA[e_][:, seg * 512:seg * 512 + 512], ps_qv[jp][:, :],
                qbt[:, e_:e_ + 1], float(128 ** -0.5), ALU.add, ALU.mult),
                waits=[(S["mm"], mmv), consts])
            qkv_dv[jp] = S["dv"].v
        qw_mm[j] = S["mm"].v
    kwb = qwb[0]
    ctx.dma("gpsimd", kwb[:, :], kw, waits=[(S["mm"], qw_mm[0])])
    kwf = ctx.fence("gpsimd")
    for seg in range(8):
        jp = seg % 2
        for kc in range(8):
            ctx.mm(lambda e, kc=kc, seg=seg, jp=jp: e.matmul(
                ps_qv[jp][:, :], kwb[:, kc * 128:kc * 128 + 128],
                xnb[kc][:, seg * 512:seg * 512 + 512],
                start=(kc == 0), stop=(kc == 7)),
                waits=([kwf, (S["dv"], qkv_dv[jp])] if kc == 0 else ()))
        mmv = S["mm"].v
        ctx.dv(lambda e, seg=seg, jp=jp: e.tensor_single_scalar(
            kt[:, seg * 512:seg * 512 + 512], ps_qv[jp][:, :], kbt[:, 0:1], ALU.add),
            waits=[(S["mm"], mmv)])
        qkv_dv[jp] = S["dv"].v
    vwb = qwb[1]
    ctx.dma("gpsimd", vwb[:, :], vw.rearrange("c p q -> p c q") if False else vw,
            waits=[(S["mm"], qw_mm[1])])
    vwf = ctx.fence("gpsimd")
    for tc in range(32):
        jp = tc % 2
        for kc in range(8):
            ctx.mm(lambda e, kc=kc, tc=tc, jp=jp: e.matmul(
                ps_qv[jp][:, 0:128], xnb[kc][:, tc * 128:tc * 128 + 128],
                vwb[:, kc * 128:kc * 128 + 128],
                start=(kc == 0), stop=(kc == 7)),
                waits=([vwf, (S["dv"], qkv_dv[jp])] if kc == 0 else ()))
        mmv = S["mm"].v
        ctx.dv(lambda e, tc=tc, jp=jp: e.tensor_add(
            vtt[:, tc * 128:tc * 128 + 128], ps_qv[jp][:, 0:128], vbb[:, :]),
            waits=[(S["mm"], mmv)])
        qkv_dv[jp] = S["dv"].v
    qkvf = (S["dv"], S["dv"].v)

    reltb = ctx.sbuf("reltb", [128, 132], BF16)
    ctx.dma("gpsimd", reltb[:, :], reltT)
    relf = ctx.fence("gpsimd")
    ow_mm = [0, 0]
    ctx.dma("gpsimd", obb[:, :], obb_d)
    obf = ctx.fence("gpsimd")
    sc_ev = [0, 0]
    sc_ev_dv = [0, 0]
    pbt_mm = [0, 0]
    pbt_ac = 0
    pvps_free = 0
    psb_mm = 0
    nm_ac = 0
    nm_dv = 0
    relq_dv = 0
    relu_mm = [0, 0]
    moa_dv = 0
    z4dma = None
    for qtl in range(16):
        q0g = o0 + qtl * 128
        w0, w1 = q0g - 64, q0g + 192
        w0c, w1c = max(w0, 0), min(w1, TF)
        m0, m1 = w0c - w0, 256 - (w1 - w1c)
        for e_ in range(8):
            j = e_ % 2
            jr = e_ % 2
            ctx.mm(lambda e, e_=e_, qtl=qtl, jr=jr: e.matmul(
                ps_qv[jr][:, 0:132], qtA[e_][:, qtl * 128:qtl * 128 + 128],
                reltb[:, :], start=True, stop=True),
                waits=[qkvf, relf, (S["dv"], relq_dv)])
            mmv = S["mm"].v
            rqw = [(S["mm"], mmv)]
            if getattr(ctx, "_relq_wf", None) is not None:
                rqw.append(ctx._relq_wf)
            ctx.dv(lambda e, jr=jr: e.tensor_copy(relq[:, :], ps_qv[jr][:, 0:132]),
                   waits=rqw)
            relq_dv = S["dv"].v
            ctx.dma("sync", relhbm[j].ap(), relq[:, 0:129], waits=[(S["dv"], relq_dv)])
            rhf = ctx.fence("sync")
            ctx.dma("sync", skw[:, m0:m1], bass.AP(relhbm[j].ap().tensor, m0, [[128, 128], [1, m1 - m0]]),
                    waits=[rhf])
            skf = ctx.fence("sync")
            ctx._relq_wf = skf
            for seg in range(8):
                jp = seg % 2
                s0 = seg * 512
                mw = []
                if sc_ev[jp]:
                    mw.append((S["ac"], sc_ev[jp]))
                if sc_ev_dv[jp]:
                    mw.append((S["dv"], sc_ev_dv[jp]))
                if seg == 0:
                    mw.append(qkvf)
                ctx.mm(lambda e, e_=e_, qtl=qtl, seg=seg, jp=jp: e.matmul(
                    ps_s[jp][:, :], qtA[e_][:, qtl * 128:qtl * 128 + 128],
                    kt[:, seg * 512:seg * 512 + 512], start=True, stop=True),
                    waits=mw)
                mmv = S["mm"].v
                sw = [(S["mm"], mmv)]
                if seg == 0 and nm_dv:
                    sw.append((S["dv"], nm_dv))
                if s0 + 512 <= w0c or s0 >= w1c:
                    col = 0 if s0 + 512 <= w0c else 128
                    sw.append((S["dv"], relq_dv))
                    ctx.dv(lambda e, seg=seg, jp=jp, col=col: e.tensor_single_scalar(
                        sc[:, seg * 512:seg * 512 + 512], ps_s[jp][:, :],
                        relq[:, col:col + 1], ALU.add), waits=sw)
                    sc_ev_dv[jp] = S["dv"].v
                else:
                    ctx.ac(lambda e, seg=seg, jp=jp: e.copy(
                        sc[:, seg * 512:seg * 512 + 512], ps_s[jp][:, :]), waits=sw)
                    sc_ev[jp] = S["ac"].v
                    acv2 = S["ac"].v
                    if s0 < w0c:
                        ctx.dv(lambda e, s0=s0, w0c=w0c: e.tensor_single_scalar(
                            sc[:, s0:w0c], sc[:, s0:w0c], relq[:, 0:1], ALU.add),
                            waits=[(S["ac"], acv2), (S["dv"], relq_dv)])
                        sc_ev_dv[jp] = S["dv"].v
                    if s0 + 512 > w1c:
                        ctx.dv(lambda e, s0=s0, w1c=w1c: e.tensor_single_scalar(
                            sc[:, w1c:s0 + 512], sc[:, w1c:s0 + 512], relq[:, 128:129],
                            ALU.add), waits=[(S["ac"], acv2), (S["dv"], relq_dv)])
                        sc_ev_dv[jp] = S["dv"].v
            ctx.dv(lambda e, m0=m0, m1=m1: e.tensor_mul(
                skw[:, m0:m1], skw[:, m0:m1], bmt[:, m0:m1]), waits=[skf, bmf])
            ctx.dv(lambda e, m0=m0, m1=m1: e.scalar_tensor_tensor(
                skw[:, m0:m1], bmt[:, 256 + m0:256 + m1], relq[:, 0:1], skw[:, m0:m1],
                ALU.mult, ALU.add), waits=[(S["dv"], S["dv"].v)])
            ctx.dv(lambda e, m0=m0, m1=m1: e.scalar_tensor_tensor(
                skw[:, m0:m1], bmt[:, 512 + m0:512 + m1], relq[:, 128:129], skw[:, m0:m1],
                ALU.mult, ALU.add), waits=[(S["dv"], S["dv"].v)])
            ctx.dv(lambda e, m0=m0, m1=m1, w0c=w0c, w1c=w1c: e.tensor_add(
                sc[:, w0c:w1c], sc[:, w0c:w1c], skw[:, m0:m1]),
                waits=[(S["dv"], S["dv"].v)])
            win_dv = S["dv"].v
            acv = S["ac"].v
            rw = [(S["ac"], acv), (S["dv"], win_dv)]
            if nm_ac:
                rw.append((S["ac"], nm_ac))
            ctx.dv(lambda e: e.reduce_max(nmx[:, :], sc[:, :], axis=AX.X, negate=True),
                   waits=rw)
            nm_dv = S["dv"].v
            ew = [(S["dv"], nm_dv)]
            if psb_mm:
                ew.append((S["mm"], psb_mm))
            ctx.ac(lambda e: e.activation(psb[:, :], sc[:, :], AF.Exp,
                                          bias=nmx[:, 0:1], accum_out=zrow[:, 0:1]),
                   waits=ew)
            expv = S["ac"].v
            nm_ac = expv
            ctx.dv(lambda e: e.reciprocal(rz[:, :], zrow[:, 0:1]), waits=[(S["ac"], expv)])
            ctx.dv(lambda e, e_=e_, qtl=qtl: e.tensor_single_scalar(
                rzg[:, :], rz[:, 0:1], gA[:, qtl * 8 + e_:qtl * 8 + e_ + 1], ALU.mult),
                waits=[gdone])
            rzv = S["dv"].v
            for grp in range(4):
                tw = [(S["ac"], expv)]
                if pbt_ac:
                    tw.append((S["ac"], pbt_ac))
                jt = grp % 2
                for s8 in range(8):
                    kk = grp * 8 + s8
                    ctx.mm(lambda e, kk=kk, s8=s8: e.transpose(
                        ps_t[:, s8 * 128:s8 * 128 + 128],
                        psb[:, kk * 128:kk * 128 + 128], idt[:, :]),
                        waits=(tw if s8 == 0 else ()))
                mmv = S["mm"].v
                ctx.ac(lambda e, jt=jt: e.copy(pbt[jt][:, :], ps_t[:, :]),
                       waits=[(S["mm"], mmv), (S["mm"], pbt_mm[jt])])
                pbt_ac = S["ac"].v
                pw = [(S["ac"], pbt_ac)]
                if grp == 0 and pvps_free:
                    pw.append((S["dv"], pvps_free))
                for s8 in range(8):
                    kk = grp * 8 + s8
                    ctx.mm(lambda e, kk=kk, s8=s8, jt=jt: e.matmul(
                        ps_pv[:, :], pbt[jt][:, s8 * 128:s8 * 128 + 128],
                        vtt[:, kk * 128:kk * 128 + 128],
                        start=(kk == 0), stop=(kk == 31)),
                        waits=(pw if s8 == 0 else ()))
                pbt_mm[jt] = S["mm"].v
            psb_mm = S["mm"].v
            mmv = S["mm"].v
            ctx.dv(lambda e: e.tensor_single_scalar(cxb[:, :], ps_pv[:, :], rzg[:, 0:1],
                                                    ALU.mult),
                   waits=[(S["mm"], mmv), (S["dv"], rzv)])
            pvps_free = S["dv"].v
            ctx.mm(lambda e: e.transpose(ps_t[:, 0:128], cxb[:, :], idt[:, :]),
                   waits=[(S["dv"], pvps_free), (S["ac"], pbt_ac)])
            mmv = S["mm"].v
            ctx.ac(lambda e: e.copy(cxT[:, :], ps_t[:, 0:128]),
                   waits=[(S["mm"], mmv), (S["mm"], relu_mm[0])])
            cxTv = S["ac"].v
            pbt_ac = S["ac"].v
            ctx.dma("gpsimd", owb[j][:, :], ow[e_],
                    waits=[(S["mm"], ow_mm[j])] if ow_mm[j] else [])
            owf = ctx.fence("gpsimd")
            for dc in range(8):
                bank0 = (dc < 4)
                mw2 = []
                if dc == 0:
                    mw2 = [owf, (S["ac"], cxTv)]
                    if moa_dv:
                        mw2.append((S["dv"], moa_dv))
                ctx.mm(lambda e, e_=e_, dc=dc, j=j: e.matmul(
                    ps_moa[dc // 4][:, (dc % 4) * 128:(dc % 4) * 128 + 128],
                    owb[j][:, dc * 128:dc * 128 + 128], cxT[:, :],
                    start=(e_ == 0 and dc % 4 == 0), stop=False),
                    waits=mw2)
            ow_mm[j] = S["mm"].v
            relu_mm[0] = S["mm"].v
        for dc in range(8):
            ctx.mm(lambda e, dc=dc, qtl=qtl: e.matmul(
                ps_moa[dc // 4][:, (dc % 4) * 128:(dc % 4) * 128 + 128],
                obb[0:8, dc * 128:dc * 128 + 128], gTb[0:8, qtl * 128:qtl * 128 + 128],
                start=False, stop=True),
                waits=([obf, gdone_ac] if dc == 0 else ()))
        mmv = S["mm"].v
        for dc in range(8):
            ctx.dv(lambda e, dc=dc, qtl=qtl: e.tensor_add(
                z3o[dc][:, qtl * 128:qtl * 128 + 128],
                z3o[dc][:, qtl * 128:qtl * 128 + 128],
                ps_moa[dc // 4][:, (dc % 4) * 128:(dc % 4) * 128 + 128]),
                waits=[(S["mm"], mmv)] if dc == 0 else ())
        moa_dv = S["dv"].v
    z4done = (S["dv"], S["dv"].v)
    for i in range(8):
        ctx.dma("sync", z4t_o.rearrange("(c p) t -> c p t", p=128)[i], z3o[i][:, :],
                waits=[z4done] if i == 0 else [])
    ctx.q["vector"].append(lambda e, v1=S["dgp"].v, v2=S["dsy"].v:
                           (e.wait_ge(S["dgp"].h, v1), e.wait_ge(S["dsy"].h, v2)))
    ctx.finish()
    return nc



def build_ld2():
    nc = bass.Bass("TRN2", target_bir_lowering=False)
    ctx = Ctx(nc)
    S = ctx.S

    z4t = nc.dram_tensor("z4t", [D, TC], BF16, kind="ExternalInput").ap()
    r3t = nc.dram_tensor("r3t", [D, TC], F32, kind="ExternalInput").ap()
    ln2wb = nc.dram_tensor("ln2wb", [2, 128, 8], F32, kind="ExternalInput").ap()
    gatewF = nc.dram_tensor("gatewF", [8, 128, 16], F32, kind="ExternalInput").ap()
    fw1 = nc.dram_tensor("fw1", [16, 128, 8, 512], F32, kind="ExternalInput").ap()
    fb1 = nc.dram_tensor("fb1", [128, 64], F32, kind="ExternalInput").ap()
    fw2 = nc.dram_tensor("fw2", [16, 128, 4, D], F32, kind="ExternalInput").ap()
    fb2 = nc.dram_tensor("fb2", [16, D], F32, kind="ExternalInput").ap()
    idend = nc.dram_tensor("iden16", [128, 128], BF16, kind="ExternalInput").ap()
    ones32d = nc.dram_tensor("ones32", [128, 1], F32, kind="ExternalInput").ap()
    ones16d = nc.dram_tensor("ones16b", [128, 1], BF16, kind="ExternalInput").ap()
    cvecd = nc.dram_tensor("cvec", [128, 2], F32, kind="ExternalInput").ap()
    r4t_o = nc.dram_tensor("r4t", [D, TC], F32, kind="ExternalOutput").ap()

    z4b = [ctx.sbuf(f"z4b{i}", [128, TC], BF16) for i in range(8)]
    ln2b = [ctx.sbuf(f"ln2b{i}", [128, TC], BF16) for i in range(8)]
    acc = ctx.sbuf("acc", [128, 16 * 1024], BF16)
    w1buf = [ctx.sbuf(f"w1buf{j}", [128, 8 * 512], BF16) for j in range(2)]
    w2buf = [ctx.sbuf(f"w2buf{j}", [128, 4 * D], BF16) for j in range(2)]
    hbuf = ctx.sbuf("hbuf", [128, 4 * 512], BF16)
    gF = ctx.sbuf("gF", [128, 16 * 16], F32)
    gFTb = ctx.sbuf("gFTb", [16, TC], BF16)
    gwF = ctx.sbuf("gwF", [128, 8 * 16], BF16)
    fb1s = ctx.sbuf("fb1s", [128, 16 * 4], F32)
    fb2s = ctx.sbuf("fb2s", [16, D], BF16)
    glog = ctx.sbuf("glog", [128, 16], F32)
    gm1 = ctx.sbuf("gm1", [128, 1], F32)
    gm2 = ctx.sbuf("gm2", [128, 1], F32)
    geq = ctx.sbuf("geq", [128, 16], F32)
    gden = ctx.sbuf("gden", [128, 1], F32)
    gtb_s = ctx.sbuf("gtb_s", [128, 16], BF16)
    idt = ctx.sbuf("idt", [128, 128], BF16)
    onesc = ctx.sbuf("onesc", [128, 1], F32)
    ones16 = ctx.sbuf("ones16t", [128, 1], BF16)
    cvec = ctx.sbuf("cvect", [128, 2], F32)
    r1stg = ctx.sbuf("r1stg", [128, TC], F32)
    ps_h = [ctx.psum(f"ps_h{j}", [128, 512], F32) for j in range(2)]
    ps_y = [ctx.psum(f"ps_y{j}", [128, 512], F32) for j in range(2)]
    ps_t = [ctx.psum(f"ps_t{j}", [128, 1024], BF16) for j in range(2)]
    ps_ln = [ctx.psum(f"ps_ln{j}", [1, 512], F32) for j in range(2)]

    for i in range(8):
        ctx.dma("sync", z4b[i][:, :], z4t.rearrange("(c p) t -> c p t", p=128)[i])
    ctx.dma("sync", idt[:, :], idend)
    ctx.dma("sync", onesc[:, :], ones32d)
    ctx.dma("sync", ones16[:, :], ones16d)
    ctx.dma("sync", cvec[:, :], cvecd)
    lnw = ctx.sbuf("lnw", [128, 8], F32)
    lnb = ctx.sbuf("lnb", [128, 8], F32)
    ctx.dma("sync", lnw[:, :], ln2wb[0])
    ctx.dma("sync", lnb[:, :], ln2wb[1])
    ctx.dma("sync", fb1s[:, :], fb1)
    zf = ctx.fence("sync")
    for kc in range(8):
        ctx.dma("gpsimd", gwF[:, kc * 16:kc * 16 + 16], gatewF[kc])
    ctx.dma("gpsimd", fb2s[:, :], fb2)
    gwf = ctx.fence("gpsimd")

    def applyp(i, rstdb, pb, bcf):
        ctx.dv(lambda e, i=i: e.tensor_mul(ln2b[i][:, :], z4b[i][:, :], rstdb),
               waits=[bcf, zf])
        ctx.dv(lambda e, i=i: e.tensor_sub(ln2b[i][:, :], ln2b[i][:, :], pb),
               waits=[(S["dv"], S["dv"].v)])
        ctx.dv(lambda e, i=i: e.tensor_scalar(ln2b[i][:, :], ln2b[i][:, :],
                                              lnw[:, i:i + 1], lnb[:, i:i + 1],
                                              ALU.mult, ALU.add),
               waits=[(S["dv"], S["dv"].v)])

    emit_layernorm(ctx, "ln2", 8, TC, lambda i: z4b[i][:, :], applyp, zf,
                   ones16[:, 0:1], onesc[:, 0:1], zf,
                   ps_ln[0][0:1, :], ps_ln[1][0:1, :], eps_ap=cvec[0:1, 0:1])
    lndone = (S["dv"], S["dv"].v)

    gu = 0
    gt_mm = 0
    for tc in range(16):
        jp = tc % 2
        for kc in range(8):
            ctx.mm(lambda e, tc=tc, kc=kc, jp=jp: e.matmul(
                ps_h[jp][:, 0:16], ln2b[kc][:, tc * 128:tc * 128 + 128],
                gwF[:, kc * 16:kc * 16 + 16], start=(kc == 0), stop=(kc == 7)),
                waits=([lndone, gwf, (S["dv"], gu)] if kc == 0 else ()))
        mmv = S["mm"].v
        ctx.dv(lambda e, jp=jp: e.tensor_copy(glog[:, :], ps_h[jp][:, 0:16]),
               waits=[(S["mm"], mmv), (S["ac"], getattr(ctx, "_gprev_ac", 0))])
        dvv = S["dv"].v
        ctx.dv(lambda e: e.reduce_max(gm1[:, :], glog[:, :], axis=AX.X, negate=True),
               waits=[(S["dv"], dvv)])
        dvv = S["dv"].v
        ctx.dv(lambda e: e.tensor_single_scalar(geq[:, :], glog[:, :], gm1[:, 0:1],
                                                ALU.add), waits=[(S["dv"], dvv)])
        ctx.dv(lambda e: e.tensor_single_scalar(geq[:, :], geq[:, :], 0.0, ALU.is_ge),
               waits=[(S["dv"], S["dv"].v)])
        ctx.dv(lambda e: e.scalar_tensor_tensor(geq[:, :], geq[:, :], -1e30, glog[:, :],
                                                ALU.mult, ALU.add),
               waits=[(S["dv"], S["dv"].v)])
        ctx.dv(lambda e: e.reduce_max(gm2[:, :], geq[:, :], axis=AX.X, negate=True),
               waits=[(S["dv"], S["dv"].v)])
        ctx.dv(lambda e, tc=tc: e.tensor_single_scalar(
            gF[:, tc * 16:tc * 16 + 16], glog[:, :], gm2[:, 0:1], ALU.add),
            waits=[(S["dv"], S["dv"].v)])
        ctx.dv(lambda e, tc=tc: e.tensor_single_scalar(
            gF[:, tc * 16:tc * 16 + 16], gF[:, tc * 16:tc * 16 + 16], 0.0, ALU.is_ge),
            waits=[(S["dv"], S["dv"].v)])
        dvv = S["dv"].v
        ctx.ac(lambda e: e.activation(glog[:, :], glog[:, :], AF.Exp, bias=gm1[:, 0:1]),
               waits=[(S["dv"], dvv)])
        ctx.ac(lambda e: e.activation(gden[:, :], gm2[:, 0:1], AF.Exp, scale=-1.0,
                                      bias=gm1[:, 0:1]))
        ctx._gprev_ac = S["ac"].v
        acv = S["ac"].v
        ctx.dv(lambda e: e.tensor_single_scalar(gden[:, :], gden[:, :], 1.0, ALU.add),
               waits=[(S["ac"], acv)])
        ctx.dv(lambda e: e.reciprocal(gden[:, :], gden[:, :]),
               waits=[(S["dv"], S["dv"].v)])
        dvv = S["dv"].v
        ctx.dv(lambda e, tc=tc: e.tensor_mul(gF[:, tc * 16:tc * 16 + 16],
                                             gF[:, tc * 16:tc * 16 + 16], glog[:, :]),
               waits=[(S["dv"], dvv)])
        ctx.dv(lambda e, tc=tc: e.tensor_single_scalar(
            gF[:, tc * 16:tc * 16 + 16], gF[:, tc * 16:tc * 16 + 16], gden[:, 0:1],
            ALU.mult), waits=[(S["dv"], S["dv"].v)])
        ctx.dv(lambda e, tc=tc: e.tensor_copy(gtb_s[:, :], gF[:, tc * 16:tc * 16 + 16]))
        dvv = S["dv"].v
        ctx.mm(lambda e, tc=tc: e.transpose(ps_t[0][0:16, 0:128], gtb_s[:, :], idt[:, :]),
               waits=[(S["dv"], dvv), (S["ac"], getattr(ctx, "_gtev2", 0))])
        gt_mm = S["mm"].v
        ctx.ac(lambda e, tc=tc: e.copy(gFTb[0:16, tc * 128:tc * 128 + 128],
                                       ps_t[0][0:16, 0:128]), waits=[(S["mm"], gt_mm)])
        ctx._gtev2 = S["ac"].v
        gu = S["dv"].v
    gdone = (S["dv"], S["dv"].v)
    gdone_ac = (S["ac"], S["ac"].v)

    acc_use = 0
    for tc in range(16):
        for ds in range(2):
            jp = (tc * 2 + ds) % 2
            ctx.mm(lambda e, tc=tc, ds=ds, jp=jp: e.matmul(
                ps_y[jp][:, :], gFTb[0:16, tc * 128:tc * 128 + 128],
                fb2s[0:16, ds * 512:ds * 512 + 512], start=True, stop=True),
                waits=[gdone_ac, gwf] if (tc == 0 and ds == 0) else
                      ([(S["dv"], acc_use)] if acc_use else []))
            mmv = S["mm"].v
            ctx.dv(lambda e, tc=tc, ds=ds, jp=jp: e.tensor_copy(
                acc[:, tc * 1024 + ds * 512:tc * 1024 + ds * 512 + 512], ps_y[jp][:, :]),
                waits=[(S["mm"], mmv)])
            acc_use = S["dv"].v

    w1_mm = [0, 0]
    w2_mm = [0, 0]
    hev = [0, 0]
    for e_ in range(16):
        j = e_ % 2
        ctx.dma("gpsimd", w1buf[j][:, :], fw1[e_],
                waits=[(S["mm"], w1_mm[j])] if w1_mm[j] else [])
        w1f = ctx.fence("gpsimd")
        ctx.dma("gpsimd", w2buf[j][:, :], fw2[e_],
                waits=[(S["mm"], w2_mm[j])] if w2_mm[j] else [])
        w2f = ctx.fence("gpsimd")
        for tsub in range(4):
            for hc in range(4):
                jp = hc % 2
                for kc in range(8):
                    ctx.mm(lambda e, e_=e_, hc=hc, kc=kc, tsub=tsub, jp=jp, j=j: e.matmul(
                        ps_h[jp][:, :],
                        w1buf[j][:, kc * 512 + hc * 128:kc * 512 + hc * 128 + 128],
                        ln2b[kc][:, tsub * 512:tsub * 512 + 512],
                        start=(kc == 0), stop=(kc == 7)),
                        waits=([w1f, lndone, (S["ac"], hev[jp])] if kc == 0 else ()))
                mmv = S["mm"].v
                hw = [(S["mm"], mmv)]
                if tsub == 0 and hc < 2 and w2_mm[j]:
                    hw.append((S["mm"], w2_mm[j]))
                ctx.ac(lambda e, e_=e_, hc=hc, jp=jp: e.activation(
                    hbuf[:, hc * 512:hc * 512 + 512], ps_h[jp][:, :], AF.Relu,
                    bias=fb1s[:, e_ * 4 + hc:e_ * 4 + hc + 1]), waits=hw)
                hev[jp] = S["ac"].v
            hdone = (S["ac"], S["ac"].v)
            for tcl in range(4):
                tc = tsub * 4 + tcl
                for ds in range(2):
                    jp = (tcl * 2 + ds) % 2
                    for hc in range(4):
                        ctx.mm(lambda e, hc=hc, tcl=tcl, ds=ds, jp=jp, j=j: e.matmul(
                            ps_y[jp][:, :],
                            hbuf[:, hc * 512 + tcl * 128:hc * 512 + tcl * 128 + 128],
                            w2buf[j][:, hc * 1024 + ds * 512:hc * 1024 + ds * 512 + 512],
                            start=(hc == 0), stop=(hc == 3)),
                            waits=([hdone, w2f, (S["dv"], acc_use)] if hc == 0 else ()))
                    mmv = S["mm"].v
                    ctx.dv(lambda e, e_=e_, tc=tc, ds=ds, jp=jp: e.scalar_tensor_tensor(
                        acc[:, tc * 1024 + ds * 512:tc * 1024 + ds * 512 + 512],
                        ps_y[jp][:, :], gF[:, tc * 16 + e_:tc * 16 + e_ + 1],
                        acc[:, tc * 1024 + ds * 512:tc * 1024 + ds * 512 + 512],
                        ALU.mult, ALU.add),
                        waits=[(S["mm"], mmv), gdone])
                    acc_use = S["dv"].v
            w2_mm[j] = S["mm"].v
        w1_mm[j] = S["mm"].v
    moedone = (S["dv"], S["dv"].v)

    ztev = [0, 0]
    for tc in range(16):
        for kc in range(8):
            jt = kc % 2
            tw = [moedone] if (tc == 0 and kc < 2) else []
            if ztev[jt]:
                tw = [(S["dv"], ztev[jt])]
            ctx.mm(lambda e, tc=tc, kc=kc, jt=jt: e.transpose(
                ps_t[jt][:, 0:128], acc[:, tc * 1024 + kc * 128:tc * 1024 + kc * 128 + 128],
                idt[:, :]), waits=tw)
            mmv = S["mm"].v
            ctx.dv(lambda e, tc=tc, kc=kc, jt=jt: e.tensor_add(
                z4b[kc][:, tc * 128:tc * 128 + 128],
                z4b[kc][:, tc * 128:tc * 128 + 128], ps_t[jt][:, 0:128]),
                waits=[(S["mm"], mmv)])
            ztev[jt] = S["dv"].v
    z5done = (S["dv"], S["dv"].v)

    rsf = [None]

    def applyr2(i, rstdb, pb, bcf):
        lw = [rsf[0]] if rsf[0] is not None else []
        ctx.dma("sync", r1stg[:, :], r3t.rearrange("(c p) t -> c p t", p=128)[i], waits=lw)
        rf = ctx.fence("sync")
        tmpb = acc[:, 0:2048]
        ctx.dv(lambda e, i=i, tmpb=tmpb: e.tensor_mul(tmpb, z4b[i][:, :], rstdb), waits=[bcf])
        ctx.dv(lambda e, tmpb=tmpb: e.tensor_sub(tmpb, tmpb, pb),
               waits=[(S["dv"], S["dv"].v)])
        ctx.dv(lambda e, tmpb=tmpb: e.tensor_add(r1stg[:, :], r1stg[:, :], tmpb),
               waits=[rf, (S["dv"], S["dv"].v)])
        dvv = S["dv"].v
        ctx.dma("sync", r4t_o.rearrange("(c p) t -> c p t", p=128)[i], r1stg[:, :],
                waits=[(S["dv"], dvv)])
        rsf[0] = ctx.fence("sync")

    emit_layernorm(ctx, "lnR", 8, TC, lambda i: z4b[i][:, :], applyr2, z5done,
                   ones16[:, 0:1], onesc[:, 0:1], zf,
                   ps_ln[0][0:1, :], ps_ln[1][0:1, :], eps_ap=cvec[0:1, 0:1])
    ctx.q["vector"].append(lambda e, v1=S["dgp"].v, v2=S["dsy"].v:
                           (e.wait_ge(S["dgp"].h, v1), e.wait_ge(S["dsy"].h, v2)))
    ctx.finish()
    return nc



def _consts():
    iden = np.eye(128, dtype=BF)
    ones32 = np.ones((128, 1), np.float32)
    ones16b = np.ones((128, 1), BF)
    cv = np.zeros((128, 2), np.float32)
    cv[:, 0] = 1e-5
    cv[:, 1] = 1.0
    return iden, ones32, ones16b, cv


def _run(nc, maps, cores=None):
    n = len(maps)
    return run_bass_kernel_spmd(nc, maps, core_ids=list(range(n))).results


def kernel(**P):
    iden, ones32, ones16b, cv = _consts()
    x = np.asarray(P["L0_attn_x"], np.float32)

    nc = build_la1()
    inproj = np.ascontiguousarray(
        P["L0_attn_input_proj"].reshape(8, 128, 32, 128).transpose(2, 1, 0, 3))
    bcwf = np.zeros((32, 128, 64), np.float32)
    bcwf[:, :, 0:16] = P["L0_attn_b_proj"].reshape(32, 128, NST)
    bcwf[:, :, 32:48] = P["L0_attn_c_proj"].reshape(32, 128, NST)
    dtp = np.ascontiguousarray(P["L0_attn_dt_proj"].reshape(32, 128).T)
    phases = np.ascontiguousarray(P["L0_attn_floquet_phases"].reshape(32, 128).T)
    maps = []
    xts = []
    for c in range(NCORES):
        b, h = c >> 1, c & 1
        xs = np.ascontiguousarray(x[b, h * TC:(h + 1) * TC, :].T)
        xts.append(xs)
        pos = np.ascontiguousarray(np.arange(h * TC, (h + 1) * TC, dtype=np.float32)[None, :])
        maps.append(dict(xt=xs, inproj=inproj, bcw=bcwf, dtp=dtp, phases=phases,
                         posrow=pos, ones32=ones32, cvec=cv))
    r1 = _run(nc, maps)
    hdt = [np.asarray(r1[c]["hdt"]) for c in range(NCORES)]
    encdt = [np.asarray(r1[c]["encdt"]) for c in range(NCORES)]
    bcr = [np.asarray(r1[c]["bcrows"]) for c in range(NCORES)]

    encdt_full = np.zeros((32, 128, FULLT), BF)
    bcb = np.zeros((4, 64, 4096), np.float32)
    for c in range(NCORES):
        b, h = c >> 1, c & 1
        cols = slice(b * 4096 + h * TC, b * 4096 + h * TC + TC)
        encdt_full[:, :, cols] = encdt[c]
        bcb[b][:, h * TC:h * TC + TC] = bcr[c]
    alog = np.ascontiguousarray(P["L0_attn_a_log"].astype(np.float32)[None, :])
    nc = build_la2()
    maps = []
    for c in range(NCORES):
        maps.append(dict(encdt_r=np.ascontiguousarray(encdt_full[4 * c:4 * c + 4]),
                         bcb=bcb, dtp_sl=np.ascontiguousarray(dtp[:, 4 * c:4 * c + 4]),
                         alog=alog, idenb=iden, cvec=cv))
    r2 = _run(nc, maps)
    yts = [np.asarray(r2[c]["yt"]) for c in range(NCORES)]

    skipw = np.ascontiguousarray(
        P["L0_attn_skip_proj"].reshape(32, 128, 32, 128).transpose(2, 1, 0, 3))
    outw = np.ascontiguousarray(
        P["L0_attn_output_proj"].reshape(32, 128, 8, 128).transpose(2, 1, 0, 3))
    nc = build_la3()
    maps = []
    for c in range(NCORES):
        b, h = c >> 1, c & 1
        cols = slice(b * 4096 + h * TC, b * 4096 + h * TC + TC)
        ytc = np.zeros((32, 128, TC), BF)
        for g in range(32):
            ytc[g] = yts[g // 4][g % 4][:, cols]
        maps.append(dict(hdt=hdt[c], ytc=ytc, skipw=skipw, outw=outw, xt=xts[c],
                         ones32=ones32, ones16b=ones16b, cvec=cv))
    r3 = _run(nc, maps)
    r1t = [np.asarray(r3[c]["r1t"]) for c in range(NCORES)]
    zt = [np.asarray(r3[c]["zt"]) for c in range(NCORES)]

    wqkv = np.ascontiguousarray(
        P["L0_ffn_w_qkv"].reshape(8, 128, 3, 16, 64).transpose(3, 1, 2, 0, 4))
    wproj = np.ascontiguousarray(
        P["L0_ffn_w_proj"].reshape(8, 128, 8, 128).transpose(2, 1, 0, 3))
    ln1wb = np.ascontiguousarray(np.stack([P["L0_ffn_ln1_w"].reshape(8, 128).T,
                                           P["L0_ffn_ln1_b"].reshape(8, 128).T]))
    ncs = [build_lb1(0), build_lb1(1)]
    res = [None] * NCORES
    for h in range(2):
        ids = [c for c in range(NCORES) if (c & 1) == h]
        maps = []
        for c in ids:
            b = c >> 1
            zfull = np.ascontiguousarray(np.concatenate([zt[2 * b], zt[2 * b + 1]], axis=1))
            maps.append(dict(zfull=zfull, r1t=r1t[c], wqkv=wqkv, wproj=wproj,
                             ln1wb=ln1wb, iden16=iden, ones32=ones32, ones16b=ones16b,
                             cvec=cv))
        rr = _run(ncs[h], maps)
        for i, c in enumerate(ids):
            res[c] = rr[i]
    z1t = [np.asarray(res[c]["z1t"]) for c in range(NCORES)]

    ln2wb = np.ascontiguousarray(np.stack([P["L0_ffn_ln2_w"].reshape(8, 128).T,
                                           P["L0_ffn_ln2_b"].reshape(8, 128).T]))
    gatew = np.ascontiguousarray(P["L0_ffn_gate_w"].reshape(8, 128, 8))
    gateb = np.ascontiguousarray(P["L0_ffn_gate_b"][None, :].astype(np.float32))
    w1r = np.ascontiguousarray(
        P["L0_ffn_w1"].reshape(8, 8, 128, 2, 2048).transpose(0, 3, 2, 1, 4))
    w2r = np.ascontiguousarray(
        P["L0_ffn_w2"].reshape(8, 2, 16, 128, 1024).transpose(0, 1, 3, 2, 4))
    nc = build_lb2()
    maps = [dict(z1t=z1t[c], r1t=r1t[c], ln2wb=ln2wb, gatew=gatew, gateb=gateb,
                 w1r=w1r, w2r=w2r, iden16=iden, ones32=ones32, ones16b=ones16b, cvec=cv)
            for c in range(NCORES)]
    rb2 = _run(nc, maps)
    r2t = [np.asarray(rb2[c]["r2t"]) for c in range(NCORES)]

    wqkv2 = np.ascontiguousarray(
        P["L1_attn_W_qkv"].reshape(16, 8, 128, 3, 64).transpose(0, 2, 3, 1, 4))
    mskc = np.zeros((4, 128, 512), np.float32)
    for r in range(4):
        for p in range(128):
            mskc[r, p, r * 128 + p + 1:] = -1e10
    ncs = [build_lc1(0), build_lc1(1)]
    res = [None] * NCORES
    for h in range(2):
        ids = [c for c in range(NCORES) if (c & 1) == h]
        maps = []
        for c in ids:
            b = c >> 1
            r2full = np.ascontiguousarray(np.concatenate([r2t[2 * b], r2t[2 * b + 1]], axis=1))
            maps.append(dict(r2full=r2full, wqkv2=wqkv2, mskc=mskc, iden16=iden,
                             ones32=ones32, ones16b=ones16b, cvec=cv))
        rr = _run(ncs[h], maps)
        for i, c in enumerate(ids):
            res[c] = rr[i]
    aos = [np.asarray(res[c]["ao"]) for c in range(NCORES)]

    woutf = np.ascontiguousarray(P["L1_attn_W_out"].reshape(1024, 1024).reshape(8, 128, 8, 128))
    nc = build_lc2()
    maps = []
    for c in range(NCORES):
        b, h = c >> 1, c & 1
        AO = np.concatenate([aos[2 * b], aos[2 * b + 1]], axis=1)
        A4 = AO.reshape(16, 64, 64, 64)
        qw_ = slice(h * 32, h * 32 + 32)
        resh = np.ascontiguousarray(A4[:, :, qw_, :]).reshape(8, 128, TC)
        maps.append(dict(aoresh=resh, woutf=woutf, r2t=r2t[c], ones32=ones32,
                         ones16b=ones16b, cvec=cv))
    rc2 = _run(nc, maps)
    r3t = [np.asarray(rc2[c]["r3t"]) for c in range(NCORES)]

    ln1wb2 = np.ascontiguousarray(np.stack([P["L1_ffn_ln1_w"].reshape(8, 128).T,
                                            P["L1_ffn_ln1_b"].reshape(8, 128).T]))
    gatewA = np.ascontiguousarray(P["L1_ffn_attn_gate_w"].reshape(8, 128, 8))
    qw = np.ascontiguousarray(P["L1_ffn_q_w"].reshape(8, 8, 128, 128).transpose(0, 2, 1, 3))
    qbc = np.ascontiguousarray(P["L1_ffn_q_b"].T.astype(np.float32))
    kw = np.ascontiguousarray(P["L1_ffn_k_w"].reshape(8, 128, 128).transpose(1, 0, 2))
    kbc = np.ascontiguousarray(P["L1_ffn_k_b"][:, None].astype(np.float32))
    vw = np.ascontiguousarray(P["L1_ffn_v_w"].reshape(8, 128, 128))
    vbrow = np.ascontiguousarray(P["L1_ffn_v_b"][None, :].astype(np.float32))
    ow = np.ascontiguousarray(P["L1_ffn_o_w"].reshape(8, 128, 8, 128))
    obb_d = np.ascontiguousarray(P["L1_ffn_o_b"].astype(np.float32))
    reltT = np.zeros((128, 132), np.float32)
    reltT[:, 0:129] = P["L1_ffn_rel_table"].T
    bmasks = np.zeros((3, 128, 256), np.float32)
    for p in range(128):
        dd = np.arange(256) - p
        bmasks[0, p] = ((dd > 0) & (dd < 128)).astype(np.float32)
        bmasks[1, p] = (dd <= 0).astype(np.float32)
        bmasks[2, p] = (dd >= 128).astype(np.float32)
    ncs = [build_ld1(0), build_ld1(1)]
    res = [None] * NCORES
    for h in range(2):
        ids = [c for c in range(NCORES) if (c & 1) == h]
        maps = []
        for c in ids:
            b = c >> 1
            r3full = np.ascontiguousarray(np.concatenate([r3t[2 * b], r3t[2 * b + 1]], axis=1))
            maps.append(dict(r3full=r3full, ln1wb=ln1wb2, gatewA=gatewA, qw=qw, qbc=qbc,
                             kw=kw, kbc=kbc, vw=vw, vbrow=vbrow, ow=ow, obb_d=obb_d,
                             reltT=reltT, bmasks=bmasks, iden16=iden, ones32=ones32,
                             ones16b=ones16b, cvec=cv))
        rr = _run(ncs[h], maps)
        for i, c in enumerate(ids):
            res[c] = rr[i]
    z4t = [np.asarray(res[c]["z4t"]) for c in range(NCORES)]

    ln2wb2 = np.ascontiguousarray(np.stack([P["L1_ffn_ln2_w"].reshape(8, 128).T,
                                            P["L1_ffn_ln2_b"].reshape(8, 128).T]))
    gatewF = np.ascontiguousarray(P["L1_ffn_ff_gate_w"].reshape(8, 128, 16))
    fw1 = np.ascontiguousarray(P["L1_ffn_ff_w1"].reshape(16, 8, 128, 512).transpose(0, 2, 1, 3))
    fb1 = np.ascontiguousarray(
        P["L1_ffn_ff_b1"].reshape(16, 4, 128).transpose(2, 0, 1).reshape(128, 64))
    fw2 = np.ascontiguousarray(P["L1_ffn_ff_w2"].reshape(16, 4, 128, 1024).transpose(0, 2, 1, 3))
    fb2 = np.ascontiguousarray(P["L1_ffn_ff_b2"].astype(np.float32))
    nc = build_ld2()
    maps = [dict(z4t=z4t[c], r3t=r3t[c], ln2wb=ln2wb2, gatewF=gatewF, fw1=fw1, fb1=fb1,
                 fw2=fw2, fb2=fb2, iden16=iden, ones32=ones32, ones16b=ones16b, cvec=cv)
            for c in range(NCORES)]
    rd2 = _run(nc, maps)

    out = np.zeros((4, 4096, 1024), np.float32)
    for c in range(NCORES):
        b, h = c >> 1, c & 1
        out[b, h * TC:(h + 1) * TC] = np.asarray(rd2[c]["r4t"]).T
    return out


# version 02
# speedup vs baseline: 1.6898x; 1.6898x over previous
import numpy as np
import ml_dtypes
import concourse.bass as bass
import concourse.mybir as mybir
from concourse.bass_utils import run_bass_kernel_spmd

F32 = mybir.dt.float32
BF16 = mybir.dt.bfloat16
I32 = mybir.dt.int32
AF = mybir.ActivationFunctionType
ALU = mybir.AluOpType
AX = mybir.AxisListType

BF = ml_dtypes.bfloat16
NCORES = 8
TC = 2048
D = 1024
HD = 4096
NST = 16
FULLT = 16384

TWO_PI = float(2.0 * np.pi)
INV_2PI = float(1.0 / (2.0 * np.pi))



class Sem:
    def __init__(self, handle):
        self.h = handle
        self.v = 0

    def inc(self, k):
        self.v += k
        return self.v


class Ctx:
    ENGINES = ("sync", "tensor", "vector", "scalar", "gpsimd")

    def __init__(self, nc):
        self.nc = nc
        self.q = {e: [] for e in self.ENGINES}
        self.sems = {}
        self._exit = []
        for name in ("dsy", "dgp", "dve2", "mm", "dv", "ac", "fsy", "fgp", "fve"):
            self.sem(name)

    def sem(self, name):
        if name not in self.sems:
            cm = self.nc.semaphore(name)
            self.sems[name] = Sem(cm.__enter__())
            self._exit.append(cm)
        return self.sems[name]

    def sbuf(self, name, shape, dtype):
        cm = self.nc.sbuf_tensor(name, shape, dtype)
        t = cm.__enter__()
        self._exit.append(cm)
        return t

    def psum(self, name, shape, dtype):
        cm = self.nc.psum_tensor(name, shape, dtype)
        t = cm.__enter__()
        self._exit.append(cm)
        return t

    def emit(self, eng, fn, waits=(), inc=None):
        waits = tuple(waits)
        if inc is not None:
            inc[0].inc(inc[1])

        def thunk(e, fn=fn, waits=waits, inc=inc):
            for s, v in waits:
                e.wait_ge(s.h, v)
            r = fn(e)
            if inc is not None:
                r.then_inc(inc[0].h, inc[1])
        self.q[eng].append(thunk)
        return inc[0].v if inc is not None else None

    def dma(self, eng, out, in_, waits=()):
        sem = {"sync": self.sems["dsy"], "gpsimd": self.sems["dgp"],
               "vector": self.sems["dve2"]}[eng]
        return self.emit(eng, lambda e, o=out, i=in_: e.dma_start(out=o, in_=i),
                         waits=waits, inc=(sem, 16))

    def fence(self, eng):
        q = {"sync": "dsy", "gpsimd": "dgp", "vector": "dve2"}[eng]
        f = {"sync": "fsy", "gpsimd": "fgp", "vector": "fve"}[eng]
        qs, fs = self.sems[q], self.sems[f]
        v = qs.v
        fs.inc(1)
        fv = fs.v

        def thunk(e, qs=qs, v=v, fs=fs):
            e.wait_ge(qs.h, v)
            e.nop().then_inc(fs.h, 1)
        self.q[eng].append(thunk)
        return (fs, fv)

    def mm(self, fn, waits=(), n=1):
        return self.emit("tensor", fn, waits=waits, inc=(self.sems["mm"], n))

    def dv(self, fn, waits=(), n=1):
        return self.emit("vector", fn, waits=waits, inc=(self.sems["dv"], n))

    def ac(self, fn, waits=(), n=1):
        return self.emit("scalar", fn, waits=waits, inc=(self.sems["ac"], n))

    def finish(self):
        with self.nc.Block() as block:
            for e in self.ENGINES:
                fns = self.q[e]
                getattr(block, e)(lambda eng, fns=fns: [f(eng) for f in fns])
        for cm in reversed(self._exit):
            cm.__exit__(None, None, None)

    @property
    def S(self):
        return self.sems



def emit_layernorm(ctx, pref, nchunks, T, getx, apply_fn, in_wait, ones_x, ones_sq,
                   ones_wait, ps_a, ps_b, eps_ap=None):
    S = ctx.S
    scr = getattr(ctx, "_ln_scratch", None)
    if scr is None or scr["T"] < T:
        rstdb_t = ctx.sbuf(pref + "_rstdb", [128, T], F32)
        pb_t = ctx.sbuf(pref + "_pb", [128, T], F32)
        if T >= 2048:
            rows = [rstdb_t[0:1, k * 512:k * 512 + 512] for k in range(4)]
        else:
            rows = [ctx.sbuf(pref + f"_r{k}", [1, 512], F32)[0:1, :] for k in range(4)]
        scr = dict(T=T, sq=pb_t[:, 0:512], rows=rows,
                   hbm=ctx.nc.dram_tensor(pref + "_bchbm", [2, T], F32),
                   rstdb=rstdb_t, pb=pb_t)
        ctx._ln_scratch = scr
    sq = scr["sq"]
    r_sum, r_var, r_std, r_p = scr["rows"]
    hbm, rstdb, pb = scr["rstdb"] * 0 if False else scr["hbm"], scr["rstdb"], scr["pb"]
    nseg = T // 512
    invD = float(1.0 / (nchunks * 128))

    for seg in range(nseg):
        c0 = seg * 512
        for i in range(nchunks):
            w = [in_wait, (S["mm"], S["mm"].v)]
            ctx.ac(lambda e, x=getx(i), c0=c0: e.activation(sq[:, :], x[:, c0:c0 + 512],
                                                            AF.Square), waits=w)
            acv = S["ac"].v
            mw = [(S["ac"], acv)]
            if seg == 0 and i == 0:
                mw += [in_wait]
            if i == 0:
                mw.append((S["dv"], S["dv"].v))
                mw.append((S["dgp"], S["dgp"].v * 0) if False else (S["ac"], acv))
            ctx.mm(lambda e, x=getx(i), c0=c0, i=i:
                   e.matmul(ps_a, ones_x, x[:, c0:c0 + 512],
                            start=(i == 0), stop=(i == nchunks - 1)), waits=mw)
            ctx.mm(lambda e, i=i:
                   e.matmul(ps_b, ones_sq, sq[:, :], start=(i == 0), stop=(i == nchunks - 1)))
        mmv = S["mm"].v
        evw = [(S["mm"], mmv)]
        if seg > 0:
            evw.append(hbf)
        ctx.dv(lambda e: e.tensor_copy(r_sum, ps_a), waits=evw)
        ctx.dv(lambda e: e.tensor_copy(r_var, ps_b))
        ctx.dv(lambda e: e.tensor_single_scalar(r_sum, r_sum, invD, ALU.mult),
               waits=[(S["dv"], S["dv"].v)])
        ctx.dv(lambda e: e.scalar_tensor_tensor(
            r_std, r_sum, -1.0, r_sum, ALU.mult, ALU.mult),
            waits=[(S["dv"], S["dv"].v)])
        ctx.dv(lambda e: e.scalar_tensor_tensor(
            r_var, r_var, invD, r_std, ALU.mult, ALU.add),
            waits=[(S["dv"], S["dv"].v)])
        dvv = S["dv"].v
        ctx.ac(lambda e: e.activation(r_std, r_var, AF.Ln, bias=eps_ap),
               waits=[(S["dv"], dvv)])
        ctx.ac(lambda e: e.activation(r_std, r_std, AF.Exp, scale=-0.5))
        acv = S["ac"].v
        ctx.dv(lambda e: e.tensor_mul(r_p, r_sum, r_std),
               waits=[(S["ac"], acv)])
        dvv = S["dv"].v
        ctx.dma("gpsimd", hbm[0:1, c0:c0 + 512], r_std, waits=[(S["dv"], dvv)])
        ctx.dma("gpsimd", hbm[1:2, c0:c0 + 512], r_p)
        hbf = ctx.fence("gpsimd")
    ctx.dma("gpsimd", rstdb[0:128, 0:T], hbm[0:1, 0:T].broadcast_to([128, T]),
            waits=[hbf, (S["dv"], S["dv"].v)])
    ctx.dma("gpsimd", pb[0:128, 0:T], hbm[1:2, 0:T].broadcast_to([128, T]))
    bcf = ctx.fence("gpsimd")
    for i in range(nchunks):
        apply_fn(i, rstdb[0:128, 0:T], pb[0:128, 0:T], bcf)
    return rstdb, pb


def ln_apply_simple(ctx, i, getx, getout, rstdb, pb, bcf, tmp, wcol=None, bcol=None,
                    extra_wait=()):
    S = ctx.S
    ctx.dv(lambda e: e.tensor_mul(tmp[:, :], getx(i), rstdb[:, :]),
           waits=[bcf] + list(extra_wait))
    if wcol is None:
        ctx.dv(lambda e: e.tensor_sub(getout(i), tmp[:, :], pb[:, :]),
               waits=[(S["dv"], S["dv"].v)])
    else:
        ctx.dv(lambda e: e.tensor_sub(tmp[:, :], tmp[:, :], pb[:, :]),
               waits=[(S["dv"], S["dv"].v)])
        ctx.dv(lambda e: e.tensor_scalar(getout(i), tmp[:, :], wcol, bcol,
                                         ALU.mult, ALU.add),
               waits=[(S["dv"], S["dv"].v)])



def build_la1():
    nc = bass.Bass("TRN2", target_bir_lowering=False)
    ctx = Ctx(nc)
    S = ctx.S

    xt = nc.dram_tensor("xt", [D, TC], F32, kind="ExternalInput").ap()
    inproj = nc.dram_tensor("inproj", [32, 128, 8, 128], F32, kind="ExternalInput").ap()
    bcw = nc.dram_tensor("bcw", [32, 128, 64], F32, kind="ExternalInput").ap()
    dtp = nc.dram_tensor("dtp", [128, 32], F32, kind="ExternalInput").ap()
    phases = nc.dram_tensor("phases", [128, 32], F32, kind="ExternalInput").ap()
    posrow = nc.dram_tensor("posrow", [1, TC], F32, kind="ExternalInput").ap()
    ones32 = nc.dram_tensor("ones32", [128, 1], F32, kind="ExternalInput").ap()
    cvecd = nc.dram_tensor("cvec", [128, 2], F32, kind="ExternalInput").ap()
    hdt_o = nc.dram_tensor("hdt", [32, 128, TC], BF16, kind="ExternalOutput").ap()
    encdt_o = nc.dram_tensor("encdt", [32, 128, TC], BF16, kind="ExternalOutput").ap()
    bcrows_o = nc.dram_tensor("bcrows", [64, TC], F32, kind="ExternalOutput").ap()
    z0dbg = nc.dram_tensor("z0dbg", [8, 128, TC], BF16, kind="ExternalOutput").ap()
    st0dbg = nc.dram_tensor("st0dbg", [4, TC], F32, kind="ExternalOutput").ap()

    xtt = [ctx.sbuf(f"xtt{i}", [128, TC], F32) for i in range(8)]
    z0 = [ctx.sbuf(f"z0{i}", [128, TC], BF16) for i in range(8)]
    onesc = ctx.sbuf("onesc", [128, 1], F32)
    cvec = ctx.sbuf("cvect", [128, 2], F32)
    dtc = ctx.sbuf("dtc", [128, 32], F32)
    sp1 = ctx.sbuf("sp1", [128, 32], F32)
    phc = ctx.sbuf("phc", [128, 32], F32)
    posb = ctx.sbuf("posb", [128, TC], F32)
    wip = [ctx.sbuf(f"wip{j}", [128, 1024], BF16) for j in range(2)]
    wbc = [ctx.sbuf(f"wbc{j}", [128, 64], BF16) for j in range(2)]
    hdch = ctx.sbuf("hdch", [128, TC], BF16)
    encch = ctx.sbuf("encch", [128, TC], BF16)
    costt = ctx.sbuf("costt", [128, TC], F32)
    red = ctx.sbuf("red", [128, TC], F32)
    redi = ctx.sbuf("redi", [128, TC], I32)
    lntmp = ctx.sbuf("lntmp", [128, TC], F32)
    bcev = ctx.sbuf("bcev", [64, TC], F32)

    ps_hd = [ctx.psum(f"ps_hd{j}", [128, 512], F32) for j in range(2)]
    ps_ln = [ctx.psum(f"ps_ln{j}", [1, 512], F32) for j in range(2)]
    ps_bc = ctx.psum("ps_bc", [64, TC], F32)

    xv = [ctx.dma("sync", xtt[i][:, :], xt.rearrange("(c p) t -> c p t", p=128)[i])
          for i in range(8)]
    ov = ctx.dma("sync", onesc[:, :], ones32)
    ctx.dma("sync", cvec[:, :], cvecd)
    dv_ = ctx.dma("sync", dtc[:, :], dtp)
    phv = ctx.dma("sync", phc[:, :], phases)
    psv = ctx.dma("sync", posb[:, :], posrow.broadcast_to([128, TC]))
    ldall = ctx.fence("sync")

    ctx.ac(lambda e: e.activation(sp1[:, :], dtc[:, :], AF.Abs), waits=[ldall])
    ctx.ac(lambda e: e.activation(sp1[:, :], sp1[:, :], AF.Exp, scale=-1.0))
    ctx.ac(lambda e: e.activation(sp1[:, :], sp1[:, :], AF.Ln, bias=cvec[0:128, 1:2]))
    acv = S["ac"].v
    ctx.dv(lambda e: e.tensor_single_scalar(dtc[:, :], dtc[:, :], 0.0, ALU.max),
           waits=[(S["ac"], acv)])
    ctx.dv(lambda e: e.tensor_add(dtc[:, :], dtc[:, :], sp1[:, :]),
           waits=[(S["dv"], S["dv"].v)])
    spv = S["dv"].v

    def apply0(i, rstdb, pb, bcf):
        ln_apply_simple(ctx, i, lambda k: xtt[k][:, :], lambda k: z0[k][:, :],
                        rstdb, pb, bcf, lntmp)

    emit_layernorm(ctx, "ln0", 8, TC, lambda i: xtt[i][:, :], apply0,
                   ldall, onesc[:, 0:1], onesc[:, 0:1], ldall,
                   ps_ln[0][0:1, :], ps_ln[1][0:1, :], eps_ap=cvec[0:1, 0:1])
    z0done = (S["dv"], S["dv"].v)

    evac_dv = [0, 0]
    wip_mm = [0, 0]
    bc_mm = [0, 0]
    prev_hdst = None
    prev_encst = None
    for g in range(32):
        j = g % 2
        ctx.dma("gpsimd", wip[j][:, :], inproj[g],
                waits=[(S["mm"], wip_mm[j])] if wip_mm[j] else [])
        ctx.dma("gpsimd", wbc[j][:, :], bcw[g],
                waits=[(S["mm"], bc_mm[j])] if bc_mm[j] else [])
        wf = ctx.fence("gpsimd")
        for seg in range(4):
            pj = seg % 2
            mw = [wf, z0done, (S["dv"], evac_dv[pj])]
            for kc in range(8):
                ctx.mm(lambda e, j=j, kc=kc, seg=seg, pj=pj:
                       e.matmul(ps_hd[pj][:, :], wip[j][:, kc * 128:kc * 128 + 128],
                                z0[kc][:, seg * 512:seg * 512 + 512],
                                start=(kc == 0), stop=(kc == 7)),
                       waits=(mw if kc == 0 else ()))
            mmv = S["mm"].v
            ew = [(S["mm"], mmv)]
            if seg == 0 and prev_hdst is not None:
                ew.append(prev_hdst)
            ctx.dv(lambda e, seg=seg, pj=pj: e.tensor_copy(
                hdch[:, seg * 512:seg * 512 + 512], ps_hd[pj][:, :]), waits=ew)
            evac_dv[pj] = S["dv"].v
        wip_mm[j] = S["mm"].v
        hdv = S["dv"].v
        ctx.dma("sync", hdt_o[g], hdch[:, :], waits=[(S["dv"], hdv)])
        prev_hdst = ctx.fence("sync")
        ctx.dv(lambda e, g=g: e.tensor_single_scalar(costt[:, :], posb[:, :],
                                                     phc[:, g:g + 1], ALU.mult),
               waits=[(S["ac"], S["ac"].v)])
        ctx.dv(lambda e: e.tensor_scalar(red[:, :], costt[:, :], INV_2PI, 0.25,
                                         ALU.mult, ALU.add),
               waits=[(S["dv"], S["dv"].v)])
        ctx.dv(lambda e: e.tensor_copy(redi[:, :], red[:, :]),
               waits=[(S["dv"], S["dv"].v)])
        ctx.dv(lambda e: e.tensor_copy(costt[:, :], redi[:, :]),
               waits=[(S["dv"], S["dv"].v)])
        ctx.dv(lambda e: e.tensor_sub(red[:, :], red[:, :], costt[:, :]),
               waits=[(S["dv"], S["dv"].v)])
        rdv = S["dv"].v
        ctx.ac(lambda e: e.activation(costt[:, :], red[:, :], AF.Sin, scale=TWO_PI),
               waits=[(S["dv"], rdv)])
        csv = S["ac"].v
        ew = [(S["ac"], csv)]
        if prev_encst is not None:
            ew.append(prev_encst)
        ctx.dv(lambda e: e.tensor_mul(encch[:, :], hdch[:, :], costt[:, :]), waits=ew)
        encv = S["dv"].v
        for seg in range(4):
            ctx.mm(lambda e, j=j, seg=seg, g=g: e.matmul(
                ps_bc[:, seg * 512:seg * 512 + 512], wbc[j][:, :],
                encch[:, seg * 512:seg * 512 + 512],
                start=(g == 0), stop=(g == 31)),
                waits=([(S["dv"], encv)] if seg == 0 else ()))
        bcmm = S["mm"].v
        bc_mm[j] = bcmm
        ctx.dv(lambda e, g=g: e.tensor_single_scalar(encch[:, :], encch[:, :],
                                                     dtc[:, g:g + 1], ALU.mult),
               waits=[(S["mm"], bcmm)] + ([(S["dv"], spv)] if g == 0 else []))
        ctx.dma("sync", encdt_o[g], encch[:, :], waits=[(S["dv"], S["dv"].v)])
        prev_encst = ctx.fence("sync")
    mmv = S["mm"].v
    ctx.dv(lambda e: e.tensor_copy(bcev[0:16, :], ps_bc[0:16, :]), waits=[(S["mm"], mmv)])
    ctx.dv(lambda e: e.tensor_copy(bcev[32:48, :], ps_bc[32:48, :]))
    ctx.dma("sync", bcrows_o[0:16, :], bcev[0:16, :], waits=[(S["dv"], S["dv"].v)])
    ctx.dma("sync", bcrows_o[32:48, :], bcev[32:48, :])
    ctx.q["vector"].append(lambda e, v1=S["dgp"].v, v2=S["dsy"].v:
                           (e.wait_ge(S["dgp"].h, v1), e.wait_ge(S["dsy"].h, v2)))
    ctx.finish()
    return nc



def _prep_la1_inputs(P):
    x = P["L0_attn_x"]
    inproj = np.ascontiguousarray(
        P["L0_attn_input_proj"].reshape(8, 128, 32, 128).transpose(2, 1, 0, 3))
    bcwf = np.zeros((32, 128, 64), np.float32)
    bcwf[:, :, 0:16] = P["L0_attn_b_proj"].reshape(32, 128, NST)
    bcwf[:, :, 32:48] = P["L0_attn_c_proj"].reshape(32, 128, NST)
    dtp = np.ascontiguousarray(P["L0_attn_dt_proj"].reshape(32, 128).T)
    phases = np.ascontiguousarray(P["L0_attn_floquet_phases"].reshape(32, 128).T)
    ones32 = np.ones((128, 1), np.float32)
    maps = []
    for c in range(NCORES):
        b, h = c >> 1, c & 1
        xs = np.ascontiguousarray(x[b, h * TC:(h + 1) * TC, :].T)
        pos = np.arange(h * TC, (h + 1) * TC, dtype=np.float32)[None, :]
        cv = np.zeros((128, 2), np.float32); cv[:, 0] = 1e-5; cv[:, 1] = 1.0
        maps.append(dict(xt=xs, inproj=inproj, bcw=bcwf, dtp=dtp, phases=phases,
                         posrow=np.ascontiguousarray(pos), ones32=ones32, cvec=cv))
    return maps


def kernel(**inputs):
    raise NotImplementedError("under construction")



def build_la2():
    nc = bass.Bass("TRN2", target_bir_lowering=False)
    ctx = Ctx(nc)
    S = ctx.S

    encdt_r = nc.dram_tensor("encdt_r", [4, 128, FULLT], BF16, kind="ExternalInput").ap()
    bcb = nc.dram_tensor("bcb", [4, 64, 4096], F32, kind="ExternalInput").ap()
    dtp_sl = nc.dram_tensor("dtp_sl", [128, 4], F32, kind="ExternalInput").ap()
    alog = nc.dram_tensor("alog", [1, NST], F32, kind="ExternalInput").ap()
    idend = nc.dram_tensor("idenb", [128, 128], BF16, kind="ExternalInput").ap()
    cvecd = nc.dram_tensor("cvec", [128, 2], F32, kind="ExternalInput").ap()
    yt_o = nc.dram_tensor("yt", [4, 128, FULLT], BF16, kind="ExternalOutput").ap()

    enc = [ctx.sbuf(f"enc{j}", [128, FULLT], BF16) for j in range(2)]
    bb = [ctx.sbuf(f"bb{j}", [128, 4096], BF16) for j in range(2)]
    cb = [ctx.sbuf(f"cb{j}", [128, 4096], BF16) for j in range(2)]
    ubuf = ctx.sbuf("ubuf", [128, 1024], BF16)
    gbuf = [ctx.sbuf(f"gbuf{j}", [128, 1024], BF16) for j in range(2)]
    tbuf = [ctx.sbuf(f"tbuf{j}", [128, 1024], BF16) for j in range(2)]
    ystg = [ctx.sbuf(f"ystg{j}", [128, 4096], BF16) for j in range(2)]
    idt = ctx.sbuf("idt", [128, 128], BF16)
    dtc = ctx.sbuf("dtc", [128, 4], F32)
    sp1 = ctx.sbuf("sp2", [128, 4], F32)
    abc = ctx.sbuf("abc", [128, NST], F32)
    dec = ctx.sbuf("dec", [128, 4 * NST], F32)
    cvec = ctx.sbuf("cvect", [128, 2], F32)
    ps_y = [ctx.psum(f"ps_y{j}", [128, 2048], F32) for j in range(2)]

    ctx.dma("sync", idt[:, :], idend)
    ctx.dma("sync", dtc[:, :], dtp_sl)
    ctx.dma("sync", abc[:, :], alog.broadcast_to([128, NST]))
    ctx.dma("sync", cvec[:, :], cvecd)
    ldf = ctx.fence("sync")

    ctx.ac(lambda e: e.activation(sp1[:, :], dtc[:, :], AF.Abs), waits=[ldf])
    ctx.ac(lambda e: e.activation(sp1[:, :], sp1[:, :], AF.Exp, scale=-1.0))
    ctx.ac(lambda e: e.activation(sp1[:, :], sp1[:, :], AF.Ln, bias=cvec[0:128, 1:2]))
    ctx.ac(lambda e: e.activation(abc[:, :], abc[:, :], AF.Exp))
    acv = S["ac"].v
    ctx.dv(lambda e: e.tensor_single_scalar(dtc[:, :], dtc[:, :], 0.0, ALU.max),
           waits=[(S["ac"], acv)])
    ctx.dv(lambda e: e.tensor_add(dtc[:, :], dtc[:, :], sp1[:, :]),
           waits=[(S["dv"], S["dv"].v)])
    dtcv = S["dv"].v
    for dg in range(4):
        ctx.dv(lambda e, dg=dg: e.tensor_single_scalar(
            dec[:, dg * NST:(dg + 1) * NST], abc[:, :], dtc[:, dg:dg + 1], ALU.mult),
            waits=[(S["dv"], dtcv)] if dg == 0 else [])
    dvv = S["dv"].v
    ctx.ac(lambda e: e.activation(dec[:, :], dec[:, :], AF.Exp, scale=-1.0),
           waits=[(S["dv"], dvv)])
    decdone = (S["ac"], S["ac"].v)

    encf = [None, None]
    enc_lastuse = [0, 0]
    bcuse = [0, 0]
    tmm = [0, 0]
    yst = [None, None]
    ps_free = 0
    for dg in range(4):
        je = dg % 2
        ctx.dma("sync", enc[je][:, :], encdt_r[dg],
                waits=[(S["dv"], enc_lastuse[je])] if enc_lastuse[je] else [])
        encf[je] = ctx.fence("sync")
        for b in range(4):
            for n in range(NST):
                jb = n % 2
                bw = [(S["dv"], bcuse[jb])] if bcuse[jb] else []
                ctx.dma("gpsimd", bb[jb][:, :],
                        bcb[b][n:n + 1, :].broadcast_to([128, 4096]), waits=bw)
                ctx.dma("gpsimd", cb[jb][:, :],
                        bcb[b][32 + n:33 + n, :].broadcast_to([128, 4096]))
                bcf = ctx.fence("gpsimd")
                for tq in range(4):
                    jg, jt = tq % 2, tq % 2
                    c0 = b * 4096 + tq * 1024
                    t0 = tq * 1024
                    w = [bcf, encf[je]] if tq == 0 else []
                    if tq == 0 and n == 0 and b == 0 and dg == 0:
                        w.append(decdone)
                    ctx.dv(lambda e, je=je, jb=jb, c0=c0, t0=t0: e.tensor_mul(
                        ubuf[:, :], enc[je][:, c0:c0 + 1024], bb[jb][:, t0:t0 + 1024]),
                        waits=w)
                    init = 0.0 if tq == 0 else gbuf[(tq - 1) % 2][:, 1023:1024]
                    sw = [(S["dv"], S["dv"].v)] if tq > 0 else []
                    ctx.dv(lambda e, jg=jg, init=init, dg=dg, n=n: e.tensor_tensor_scan(
                        gbuf[jg][:, :],
                        dec[:, dg * NST + n:dg * NST + n + 1].broadcast_to([128, 1024]),
                        ubuf[:, :], init, ALU.mult, ALU.add), waits=sw)
                    tw = [(S["mm"], tmm[jt])] if tmm[jt] else []
                    ctx.dv(lambda e, jg=jg, jt=jt, jb=jb, t0=t0: e.tensor_mul(
                        tbuf[jt][:, :], gbuf[jg][:, :], cb[jb][:, t0:t0 + 1024]),
                        waits=tw)
                    dvv = S["dv"].v
                    for half in range(2):
                        mw = []
                        if half == 0:
                            mw = [(S["dv"], dvv), ldf]
                            if n == 0 and tq == 0 and ps_free:
                                mw.append((S["dv"], ps_free))
                        ctx.mm(lambda e, jt=jt, tq=tq, half=half, n=n: e.matmul(
                            ps_y[tq // 2][:, (tq % 2) * 1024 + half * 512:
                                          (tq % 2) * 1024 + half * 512 + 512],
                            idt[:, :], tbuf[jt][:, half * 512:half * 512 + 512],
                            start=(n == 0), stop=(n == NST - 1)), waits=mw)
                    tmm[jt] = S["mm"].v
                bcuse[jb] = S["dv"].v
            jy = b % 2
            mmv = S["mm"].v
            yw = [(S["mm"], mmv)]
            if yst[jy] is not None:
                yw.append(yst[jy])
            ctx.dv(lambda e, jy=jy: e.tensor_copy(ystg[jy][:, 0:2048], ps_y[0][:, :]),
                   waits=yw)
            ctx.dv(lambda e, jy=jy: e.tensor_copy(ystg[jy][:, 2048:4096], ps_y[1][:, :]))
            ps_free = S["dv"].v
            ctx.dma("sync", yt_o[dg][:, b * 4096:(b + 1) * 4096], ystg[jy][:, :],
                    waits=[(S["dv"], ps_free)])
            yst[jy] = ctx.fence("sync")
        enc_lastuse[je] = S["dv"].v
    ctx.q["vector"].append(lambda e, v1=S["dgp"].v, v2=S["dsy"].v:
                           (e.wait_ge(S["dgp"].h, v1), e.wait_ge(S["dsy"].h, v2)))
    ctx.finish()
    return nc



def build_la3():
    nc = bass.Bass("TRN2", target_bir_lowering=False)
    ctx = Ctx(nc)
    S = ctx.S
    TQ = 512

    hdt = nc.dram_tensor("hdt", [32, 128, TC], BF16, kind="ExternalInput").ap()
    ytc = nc.dram_tensor("ytc", [32, 128, TC], BF16, kind="ExternalInput").ap()
    skipw = nc.dram_tensor("skipw", [32, 128, 32, 128], F32, kind="ExternalInput").ap()
    outw = nc.dram_tensor("outw", [8, 128, 32, 128], F32, kind="ExternalInput").ap()
    xt = nc.dram_tensor("xt", [D, TC], F32, kind="ExternalInput").ap()
    ones32 = nc.dram_tensor("ones32", [128, 1], F32, kind="ExternalInput").ap()
    ones16d = nc.dram_tensor("ones16b", [128, 1], BF16, kind="ExternalInput").ap()
    cvecd = nc.dram_tensor("cvec", [128, 2], F32, kind="ExternalInput").ap()
    r1t_o = nc.dram_tensor("r1t", [D, TC], F32, kind="ExternalOutput").ap()
    zt_o = nc.dram_tensor("zt", [D, TC], F32, kind="ExternalOutput").ap()

    hdq = ctx.sbuf("hdq", [128, 32 * TQ], BF16)
    outhd = ctx.sbuf("outhd", [128, 32 * TQ], BF16)
    aout = ctx.sbuf("aout", [128, 8 * TQ], BF16)
    xq = [ctx.sbuf(f"xq{i}", [128, TQ], F32) for i in range(8)]
    wsk = [ctx.sbuf(f"wsk{j}", [128, 4096], BF16) for j in range(2)]
    yec = [ctx.sbuf(f"yec{j}", [128, TQ], BF16) for j in range(2)]
    onesc = ctx.sbuf("onesc", [128, 1], F32)
    ones16 = ctx.sbuf("ones16t", [128, 1], BF16)
    cvec = ctx.sbuf("cvect", [128, 2], F32)
    lntmp = ctx.sbuf("lntmp", [128, TQ], F32)
    zstg = [ctx.sbuf(f"zstg{j}", [128, TQ], F32) for j in range(2)]
    ps = [ctx.psum(f"ps{j}", [128, 512], F32) for j in range(2)]
    ps_ln = [ctx.psum(f"ps_ln{j}", [1, 512], F32) for j in range(2)]

    ctx.dma("sync", onesc[:, :], ones32)
    ctx.dma("sync", ones16[:, :], ones16d)
    ctx.dma("sync", cvec[:, :], cvecd)
    consts = ctx.fence("sync")

    qend_mm = 0
    qend_dv = 0
    qend_ac = 0
    zst_f = [None, None]
    for q in range(4):
        t0 = q * TQ
        lw = [(S["mm"], qend_mm), (S["dv"], qend_dv), (S["ac"], qend_ac)] if q else []
        for g in range(32):
            ctx.dma("sync", hdq[:, g * TQ:(g + 1) * TQ], hdt[g][:, t0:t0 + TQ],
                    waits=lw if g == 0 else [])
        for i in range(8):
            ctx.dma("sync", xq[i][:, :], xt.rearrange("(c p) t -> c p t", p=128)[i][:, t0:t0 + TQ])
        hdf = ctx.fence("sync")
        wsk_mm = [0, 0]
        yv = [None, None]
        for ec in range(32):
            j = ec % 2
            ctx.dma("gpsimd", wsk[j][:, :], skipw[ec],
                    waits=[(S["mm"], wsk_mm[j])] if wsk_mm[j] else [])
            wf = ctx.fence("gpsimd")
            ctx.dma("sync", yec[j][:, :], ytc[ec][:, t0:t0 + TQ],
                    waits=[(S["dv"], yv[j])] if yv[j] is not None else [])
            yf = ctx.fence("sync")
            jp = ec % 2
            for kc in range(32):
                ctx.mm(lambda e, j=j, kc=kc, jp=jp: e.matmul(
                    ps[jp][:, :], wsk[j][:, kc * 128:kc * 128 + 128],
                    hdq[:, kc * TQ:(kc + 1) * TQ],
                    start=(kc == 0), stop=(kc == 31)),
                    waits=([wf, hdf, (S["dv"], qend_dv)] if kc == 0 else ()))
            mmv = S["mm"].v
            wsk_mm[j] = mmv
            ctx.dv(lambda e, ec=ec, j=j, jp=jp: e.scalar_tensor_tensor(
                outhd[:, ec * TQ:(ec + 1) * TQ], ps[jp][:, :], 1.0, yec[j][:, :],
                ALU.mult, ALU.add), waits=[(S["mm"], mmv), yf])
            yv[j] = S["dv"].v
        ow_mm = [0, 0]
        for do in range(8):
            j = do % 2
            ctx.dma("gpsimd", wsk[j][:, :], outw[do],
                    waits=[(S["mm"], max(ow_mm[j], wsk_mm[j]))])
            wf = ctx.fence("gpsimd")
            jp = do % 2
            dvv = S["dv"].v
            for ec in range(32):
                ctx.mm(lambda e, j=j, ec=ec, jp=jp: e.matmul(
                    ps[jp][:, :], wsk[j][:, ec * 128:ec * 128 + 128],
                    outhd[:, ec * TQ:(ec + 1) * TQ],
                    start=(ec == 0), stop=(ec == 31)),
                    waits=([wf, (S["dv"], dvv)] if ec == 0 else ()))
            mmv = S["mm"].v
            ow_mm[j] = mmv
            aw = [(S["mm"], mmv)]
            if do == 0 and qend_ac:
                aw.append((S["ac"], qend_ac))
            ctx.dv(lambda e, do=do, jp=jp: e.tensor_copy(
                aout[:, do * TQ:(do + 1) * TQ], ps[jp][:, :]), waits=aw)
        aoutv = (S["dv"], S["dv"].v)

        def apply1(i, rstdb, pb, bcf):
            ctx.dv(lambda e, i=i: e.tensor_mul(lntmp[:, :], aout[:, i * TQ:(i + 1) * TQ],
                                               rstdb), waits=[bcf])
            ctx.dv(lambda e, i=i: e.tensor_sub(lntmp[:, :], lntmp[:, :], pb))
            ctx.dv(lambda e, i=i: e.tensor_add(xq[i][:, :], xq[i][:, :], lntmp[:, :]))

        emit_layernorm(ctx, f"lnA{q}", 8, TQ, lambda i: aout[:, i * TQ:(i + 1) * TQ],
                       apply1, aoutv, ones16[:, 0:1], onesc[:, 0:1], consts,
                       ps_ln[0][0:1, :], ps_ln[1][0:1, :], eps_ap=cvec[0:1, 0:1])
        r1v = (S["dv"], S["dv"].v)
        for i in range(8):
            ctx.dma("sync", r1t_o.rearrange("(c p) t -> c p t", p=128)[i][:, t0:t0 + TQ],
                    xq[i][:, :], waits=[r1v] if i == 0 else [])
        r1stf = ctx.fence("sync")

        def apply2(i, rstdb, pb, bcf):
            j = i % 2
            w = [bcf]
            if zst_f[j] is not None:
                w.append(zst_f[j])
            ctx.dv(lambda e, i=i, j=j: e.tensor_mul(zstg[j][:, :], xq[i][:, :], rstdb),
                   waits=w)
            ctx.dv(lambda e, i=i, j=j: e.tensor_sub(zstg[j][:, :], zstg[j][:, :], pb),
                   waits=[(S["dv"], S["dv"].v)])
            zv = S["dv"].v
            ctx.dma("sync", zt_o.rearrange("(c p) t -> c p t", p=128)[i][:, t0:t0 + TQ],
                    zstg[j][:, :], waits=[(S["dv"], zv)])
            zst_f[j] = ctx.fence("sync")

        emit_layernorm(ctx, f"lnB{q}", 8, TQ, lambda i: xq[i][:, :],
                       apply2, r1v, onesc[:, 0:1], onesc[:, 0:1], consts,
                       ps_ln[0][0:1, :], ps_ln[1][0:1, :], eps_ap=cvec[0:1, 0:1])
        qend_mm = S["mm"].v
        qend_dv = S["dv"].v
    ctx.q["vector"].append(lambda e, v1=S["dgp"].v, v2=S["dsy"].v:
                           (e.wait_ge(S["dgp"].h, v1), e.wait_ge(S["dsy"].h, v2)))
    ctx.finish()
    return nc



def build_lb1(own_half):
    nc = bass.Bass("TRN2", target_bir_lowering=False)
    ctx = Ctx(nc)
    S = ctx.S
    TF = 2 * TC
    o0 = own_half * TC

    zfull = nc.dram_tensor("zfull", [D, TF], F32, kind="ExternalInput").ap()
    wqkv = nc.dram_tensor("wqkv", [16, 128, 3, 8, 64], F32, kind="ExternalInput").ap()
    wproj = nc.dram_tensor("wproj", [8, 128, 8, 128], F32, kind="ExternalInput").ap()
    ln1wb = nc.dram_tensor("ln1wb", [2, 128, 8], F32, kind="ExternalInput").ap()
    idend = nc.dram_tensor("iden16", [128, 128], BF16, kind="ExternalInput").ap()
    ones32d = nc.dram_tensor("ones32", [128, 1], F32, kind="ExternalInput").ap()
    ones16d = nc.dram_tensor("ones16b", [128, 1], BF16, kind="ExternalInput").ap()
    cvecd = nc.dram_tensor("cvec", [128, 2], F32, kind="ExternalInput").ap()
    z1t_o = nc.dram_tensor("z1t", [D, TC], BF16, kind="ExternalOutput").ap()

    zb = [ctx.sbuf(f"zb{i}", [128, TF], BF16) for i in range(8)]
    pvt = [ctx.sbuf(f"pvt{i}", [128, TC], BF16) for i in range(8)]
    sc = ctx.sbuf("sc", [128, TF], F32)
    psb = ctx.sbuf("psb", [128, TF], BF16)
    pbt = [ctx.sbuf(f"pbt{j}", [128, 1024], BF16) for j in range(2)]
    qt = [ctx.sbuf(f"qt{j}", [64, TC], BF16) for j in range(2)]
    kt = [ctx.sbuf(f"kt{j}", [64, TF], BF16) for j in range(2)]
    vt = [ctx.sbuf(f"vt{j}", [128, (TF // 128) * 64], BF16) for j in range(2)]
    wq = [ctx.sbuf(f"wq{j}", [128, 3 * 512], BF16) for j in range(2)]
    wpj = [ctx.sbuf(f"wpj{j}", [128, 1024], BF16) for j in range(2)]
    z1s = [ctx.sbuf(f"z1s{j}", [128, 512], BF16) for j in range(2)]
    zstage = [ctx.sbuf(f"zst{j}", [128, 512], F32) for j in range(2)]
    idt = ctx.sbuf("idt", [128, 128], BF16)
    onesc = ctx.sbuf("onesc", [128, 1], F32)
    ones16 = ctx.sbuf("ones16t", [128, 1], BF16)
    cvec = ctx.sbuf("cvect", [128, 2], F32)
    lnw = ctx.sbuf("lnw", [128, 8], F32)
    lnb = ctx.sbuf("lnb", [128, 8], F32)
    nmx = ctx.sbuf("nmx", [128, 1], F32)
    zrow = ctx.sbuf("zrow", [128, 1], F32)
    rz = ctx.sbuf("rz", [128, 1], F32)
    pvq = ctx.sbuf("pvq", [128, 64], BF16)

    ps_s = [ctx.psum(f"ps_s{j}", [128, 512], F32) for j in range(2)]
    ps_qv = [ctx.psum(f"ps_qv{j}", [128, 512], F32) for j in range(2)]
    ps_pv = ctx.psum("ps_pv", [128, 64], F32)
    ps_t = [ctx.psum(f"ps_t{j}", [128, 1024], BF16) for j in range(2)]

    for i in range(8):
        ctx.dma("gpsimd", zb[i][:, :], zfull.rearrange("(c p) t -> c p t", p=128)[i])
    ctx.dma("sync", idt[:, :], idend)
    ctx.dma("sync", onesc[:, :], ones32d)
    ctx.dma("sync", ones16[:, :], ones16d)
    ctx.dma("sync", cvec[:, :], cvecd)
    ctx.dma("sync", lnw[:, :], ln1wb[0])
    ctx.dma("sync", lnb[:, :], ln1wb[1])
    consts = ctx.fence("sync")
    zf = ctx.fence("gpsimd")

    for part in range(2):
        c0 = part * TC

        def applyp(i, rstdb, pb, bcf, c0=c0):
            ctx.dv(lambda e, i=i, c0=c0: e.tensor_mul(
                sc[:, 0:TC], zb[i][:, c0:c0 + TC], rstdb), waits=[bcf, consts])
            ctx.dv(lambda e: e.tensor_sub(sc[:, 0:TC], sc[:, 0:TC], pb))
            ctx.dv(lambda e, i=i, c0=c0: e.tensor_scalar(
                zb[i][:, c0:c0 + TC], sc[:, 0:TC], lnw[:, i:i + 1], lnb[:, i:i + 1],
                ALU.mult, ALU.add))

        emit_layernorm(ctx, f"ln1_{part}", 8, TC, lambda i, c0=c0: zb[i][:, c0:c0 + TC],
                       applyp, zf, ones16[:, 0:1], onesc[:, 0:1], consts,
                       ps_s[0][0:1, 0:512], ps_s[1][0:1, 0:512], eps_ap=cvec[0:1, 0:1])
    lndone = (S["dv"], S["dv"].v)

    ntc = TF // 128
    wq_mm = [0, 0]
    qkv_dv = [0, 0]
    head_attn_mm = [0, 0]
    sc_ev = [0, 0]
    pbt_mm = [0, 0]
    pbt_ac = [0, 0]
    pvps_free = 0
    psb_mm = 0
    nm_ac = 0
    pvt_mm = 0
    for n in range(16):
        j = n % 2
        ctx.dma("gpsimd", wq[j][:, :], wqkv[n],
                waits=[(S["mm"], wq_mm[j])] if wq_mm[j] else [])
        wf = ctx.fence("gpsimd")
        for seg in range(4):
            jp = seg % 2
            for kc in range(8):
                ctx.mm(lambda e, j=j, kc=kc, seg=seg, jp=jp: e.matmul(
                    ps_qv[jp][0:64, :], wq[j][:, kc * 64:kc * 64 + 64],
                    zb[kc][:, o0 + seg * 512:o0 + seg * 512 + 512],
                    start=(kc == 0), stop=(kc == 7)),
                    waits=([wf, lndone, (S["dv"], qkv_dv[jp])] if kc == 0 else ()))
            mmv = S["mm"].v
            ew = [(S["mm"], mmv)]
            if seg < 2 and head_attn_mm[j]:
                ew.append((S["mm"], head_attn_mm[j]))
            ctx.dv(lambda e, j=j, seg=seg, jp=jp: e.tensor_single_scalar(
                qt[j][:, seg * 512:seg * 512 + 512], ps_qv[jp][0:64, :], 0.125, ALU.mult),
                waits=ew)
            qkv_dv[jp] = S["dv"].v
        for seg in range(8):
            jp = seg % 2
            for kc in range(8):
                ctx.mm(lambda e, j=j, kc=kc, seg=seg, jp=jp: e.matmul(
                    ps_qv[jp][0:64, :], wq[j][:, 512 + kc * 64:512 + kc * 64 + 64],
                    zb[kc][:, seg * 512:seg * 512 + 512],
                    start=(kc == 0), stop=(kc == 7)),
                    waits=([(S["dv"], qkv_dv[jp])] if kc == 0 else ()))
            mmv = S["mm"].v
            ctx.dv(lambda e, j=j, seg=seg, jp=jp: e.tensor_copy(
                kt[j][:, seg * 512:seg * 512 + 512], ps_qv[jp][0:64, :]),
                waits=[(S["mm"], mmv)])
            qkv_dv[jp] = S["dv"].v
        for tc4 in range(ntc // 4):
            jp = tc4 % 2
            for sub in range(4):
                tc = tc4 * 4 + sub
                for kc in range(8):
                    ctx.mm(lambda e, j=j, kc=kc, tc=tc, sub=sub, jp=jp: e.matmul(
                        ps_qv[jp][:, sub * 64:sub * 64 + 64],
                        zb[kc][:, tc * 128:tc * 128 + 128],
                        wq[j][:, 1024 + kc * 64:1024 + kc * 64 + 64],
                        start=(kc == 0 and sub == 0), stop=(kc == 7 and sub == 3)),
                        waits=([(S["dv"], qkv_dv[jp])] if (kc == 0 and sub == 0) else ()))
            mmv = S["mm"].v
            ctx.dv(lambda e, j=j, tc4=tc4, jp=jp: e.tensor_copy(
                vt[j][:, tc4 * 256:tc4 * 256 + 256], ps_qv[jp][:, 0:256]),
                waits=[(S["mm"], mmv)])
            qkv_dv[jp] = S["dv"].v
        qkvf = (S["dv"], S["dv"].v)
        wq_mm[j] = S["mm"].v

        for qtl in range(16):
            for seg in range(8):
                jp = seg % 2
                mw = [(S["ac"], sc_ev[jp])] if sc_ev[jp] else []
                if seg == 0:
                    mw.append(qkvf)
                ctx.mm(lambda e, j=j, qtl=qtl, seg=seg, jp=jp: e.matmul(
                    ps_s[jp][:, :], qt[j][:, qtl * 128:qtl * 128 + 128],
                    kt[j][:, seg * 512:seg * 512 + 512], start=True, stop=True),
                    waits=mw)
                mmv = S["mm"].v
                sw = [(S["mm"], mmv)]
                if seg == 0 and nm_ac:
                    sw.append((S["dv"], nm_dv))
                ctx.ac(lambda e, seg=seg, jp=jp: e.copy(
                    sc[:, seg * 512:seg * 512 + 512], ps_s[jp][:, :]), waits=sw)
                sc_ev[jp] = S["ac"].v
            acv = S["ac"].v
            rw = [(S["ac"], acv)]
            if nm_ac:
                rw.append((S["ac"], nm_ac))
            ctx.dv(lambda e: e.reduce_max(nmx[:, :], sc[:, :], axis=AX.X, negate=True),
                   waits=rw)
            nm_dv = S["dv"].v
            ew = [(S["dv"], nm_dv)]
            if psb_mm:
                ew.append((S["mm"], psb_mm))
            ctx.ac(lambda e: e.activation(psb[:, :], sc[:, :], AF.Exp,
                                          bias=nmx[:, 0:1], accum_out=zrow[:, 0:1]),
                   waits=ew)
            expv = S["ac"].v
            nm_ac = expv
            ctx.dv(lambda e: e.reciprocal(rz[:, :], zrow[:, 0:1]), waits=[(S["ac"], expv)])
            rzv = S["dv"].v
            for grp in range(4):
                jt = grp % 2
                tw = [(S["ac"], expv)]
                if pbt_ac[jt]:
                    tw.append((S["ac"], pbt_ac[jt]))
                for s8 in range(8):
                    kk = grp * 8 + s8
                    ctx.mm(lambda e, kk=kk, s8=s8, jt=jt: e.transpose(
                        ps_t[jt][:, s8 * 128:s8 * 128 + 128],
                        psb[:, kk * 128:kk * 128 + 128], idt[:, :]),
                        waits=(tw if s8 == 0 else ()))
                mmv = S["mm"].v
                ctx.ac(lambda e, jt=jt: e.copy(pbt[jt][:, :], ps_t[jt][:, :]),
                       waits=[(S["mm"], mmv)])
                pbt_ac[jt] = S["ac"].v
                pw = [(S["ac"], pbt_ac[jt])]
                if grp == 0 and pvps_free:
                    pw.append((S["dv"], pvps_free))
                for s8 in range(8):
                    kk = grp * 8 + s8
                    ctx.mm(lambda e, kk=kk, s8=s8, j=j, jt=jt: e.matmul(
                        ps_pv[:, :], pbt[jt][:, s8 * 128:s8 * 128 + 128],
                        vt[j][:, kk * 64:kk * 64 + 64],
                        start=(kk == 0), stop=(kk == 31)),
                        waits=(pw if s8 == 0 else ()))
                pbt_mm[jt] = S["mm"].v
            psb_mm = S["mm"].v
            mmv = S["mm"].v
            ctx.dv(lambda e: e.tensor_single_scalar(pvq[:, :], ps_pv[:, :], rz[:, 0:1],
                                                    ALU.mult),
                   waits=[(S["mm"], mmv), (S["dv"], rzv)])
            pvps_free = S["dv"].v
            ctx.mm(lambda e: e.transpose(ps_t[0][0:64, 0:128], pvq[:, :], idt[:, :]),
                   waits=[(S["dv"], pvps_free), (S["ac"], pbt_ac[0]),
                          (S["mm"], pbt_mm[0])])
            mmv = S["mm"].v
            ctx.ac(lambda e, n=n, qtl=qtl: e.copy(
                pvt[n // 2][(n % 2) * 64:(n % 2) * 64 + 64, qtl * 128:qtl * 128 + 128],
                ps_t[0][0:64, 0:128]), waits=[(S["mm"], mmv)])
            pbt_ac[0] = S["ac"].v
            pbt_mm[0] = S["mm"].v + 0
        head_attn_mm[j] = S["mm"].v

    pvtdone = (S["ac"], S["ac"].v)
    wp_mm = [0, 0]
    z1use = [None, None]
    zsf = [None, None]
    for do in range(8):
        j = do % 2
        ctx.dma("gpsimd", wpj[j][:, :], wproj[do],
                waits=[(S["mm"], wp_mm[j])] if wp_mm[j] else [])
        wf = ctx.fence("gpsimd")
        for tseg in range(4):
            jp = tseg % 2
            ctx.dma("sync", zstage[jp][:, :],
                    zfull.rearrange("(c p) t -> c p t", p=128)[do][:, o0 + tseg * 512:o0 + tseg * 512 + 512],
                    waits=[zsf[jp]] if zsf[jp] is not None else [])
            zstf = ctx.fence("sync")
            for kc in range(8):
                ctx.mm(lambda e, j=j, kc=kc, tseg=tseg, jp=jp: e.matmul(
                    ps_s[jp][:, :], wpj[j][:, kc * 128:kc * 128 + 128],
                    pvt[kc][:, tseg * 512:tseg * 512 + 512],
                    start=(kc == 0), stop=(kc == 7)),
                    waits=([wf, pvtdone, (S["dv"], z1use[jp])] if kc == 0 and z1use[jp] is not None
                           else [wf, pvtdone] if kc == 0 else ()))
            mmv = S["mm"].v
            zw = [(S["mm"], mmv), zstf]
            if zsf[jp] is not None:
                zw.append(zsf[jp])
            ctx.dv(lambda e, jp=jp: e.scalar_tensor_tensor(
                z1s[jp][:, :], ps_s[jp][:, :], 1.0, zstage[jp][:, :], ALU.mult, ALU.add),
                waits=zw)
            z1use[jp] = S["dv"].v
            ctx.dma("sync", z1t_o.rearrange("(c p) t -> c p t", p=128)[do][:, tseg * 512:tseg * 512 + 512],
                    z1s[jp][:, :], waits=[(S["dv"], z1use[jp])])
            zsf[jp] = ctx.fence("sync")
        wp_mm[j] = S["mm"].v
    ctx.q["vector"].append(lambda e, v1=S["dgp"].v, v2=S["dsy"].v:
                           (e.wait_ge(S["dgp"].h, v1), e.wait_ge(S["dsy"].h, v2)))
    ctx.finish()
    return nc



def build_lb2():
    nc = bass.Bass("TRN2", target_bir_lowering=False)
    ctx = Ctx(nc)
    S = ctx.S

    z1t = nc.dram_tensor("z1t", [D, TC], BF16, kind="ExternalInput").ap()
    r1t = nc.dram_tensor("r1t", [D, TC], F32, kind="ExternalInput").ap()
    ln2wb = nc.dram_tensor("ln2wb", [2, 128, 8], F32, kind="ExternalInput").ap()
    gatew = nc.dram_tensor("gatew", [8, 128, 8], F32, kind="ExternalInput").ap()
    gateb = nc.dram_tensor("gateb", [1, 8], F32, kind="ExternalInput").ap()
    w1r = nc.dram_tensor("w1r", [8, 2, 128, 8, TC], F32, kind="ExternalInput").ap()
    w2r = nc.dram_tensor("w2r", [8, 2, 128, 16, D], F32, kind="ExternalInput").ap()
    idend = nc.dram_tensor("iden16", [128, 128], BF16, kind="ExternalInput").ap()
    ones32d = nc.dram_tensor("ones32", [128, 1], F32, kind="ExternalInput").ap()
    ones16d = nc.dram_tensor("ones16b", [128, 1], BF16, kind="ExternalInput").ap()
    cvecd = nc.dram_tensor("cvec", [128, 2], F32, kind="ExternalInput").ap()
    r2t_o = nc.dram_tensor("r2t", [D, TC], F32, kind="ExternalOutput").ap()

    z1b = [ctx.sbuf(f"z1b{i}", [128, TC], BF16) for i in range(8)]
    ln2b = [ctx.sbuf(f"ln2b{i}", [128, TC], BF16) for i in range(8)]
    acc = ctx.sbuf("acc", [128, 16 * 1024], BF16)
    w1buf = ctx.sbuf("w1buf", [128, 8 * TC], BF16)
    w2buf = ctx.sbuf("w2buf", [128, 16 * 1024], BF16)
    hbuf = ctx.sbuf("hbuf", [128, 16 * 512], BF16)
    msk = ctx.sbuf("msk", [128, 16 * 8], F32)
    gwb = ctx.sbuf("gwb", [128, 8 * 8], BF16)
    gbb = ctx.sbuf("gbb", [128, 8], F32)
    glog = ctx.sbuf("glog", [128, 8], F32)
    gm1 = ctx.sbuf("gm1", [128, 1], F32)
    lnw = ctx.sbuf("lnw", [128, 8], F32)
    lnb = ctx.sbuf("lnb", [128, 8], F32)
    idt = ctx.sbuf("idt", [128, 128], BF16)
    onesc = ctx.sbuf("onesc", [128, 1], F32)
    ones16 = ctx.sbuf("ones16t", [128, 1], BF16)
    cvec = ctx.sbuf("cvect", [128, 2], F32)
    r1stg = [ctx.sbuf("r1stg0", [128, TC], F32)] * 2
    ps_h = [ctx.psum(f"ps_h{j}", [128, 512], F32) for j in range(2)]
    ps_y = [ctx.psum(f"ps_y{j}", [128, 512], F32) for j in range(2)]
    ps_t = [ctx.psum(f"ps_t{j}", [128, 1024], BF16) for j in range(2)]
    ps_ln = [ctx.psum(f"ps_ln{j}", [1, 512], F32) for j in range(2)]

    for i in range(8):
        ctx.dma("sync", z1b[i][:, :], z1t.rearrange("(c p) t -> c p t", p=128)[i])
    ctx.dma("sync", idt[:, :], idend)
    ctx.dma("sync", onesc[:, :], ones32d)
    ctx.dma("sync", ones16[:, :], ones16d)
    ctx.dma("sync", cvec[:, :], cvecd)
    ctx.dma("sync", lnw[:, :], ln2wb[0])
    ctx.dma("sync", lnb[:, :], ln2wb[1])
    zf = ctx.fence("sync")
    for kc in range(8):
        ctx.dma("gpsimd", gwb[:, kc * 8:kc * 8 + 8], gatew[kc])
    ctx.dma("gpsimd", gbb[:, :], gateb.broadcast_to([128, 8]))
    gwf = ctx.fence("gpsimd")

    def applyp(i, rstdb, pb, bcf):
        ctx.dv(lambda e, i=i: e.tensor_mul(ln2b[i][:, :], z1b[i][:, :], rstdb),
               waits=[bcf, zf])
        ctx.dv(lambda e, i=i: e.tensor_sub(ln2b[i][:, :], ln2b[i][:, :], pb),
               waits=[(S["dv"], S["dv"].v)])
        ctx.dv(lambda e, i=i: e.tensor_scalar(ln2b[i][:, :], ln2b[i][:, :],
                                              lnw[:, i:i + 1], lnb[:, i:i + 1],
                                              ALU.mult, ALU.add),
               waits=[(S["dv"], S["dv"].v)])

    emit_layernorm(ctx, "ln2", 8, TC, lambda i: z1b[i][:, :], applyp, zf,
                   ones16[:, 0:1], onesc[:, 0:1], zf,
                   ps_ln[0][0:1, :], ps_ln[1][0:1, :], eps_ap=cvec[0:1, 0:1])
    lndone = (S["dv"], S["dv"].v)

    gu = 0
    for tc in range(16):
        jp = tc % 2
        for kc in range(8):
            ctx.mm(lambda e, tc=tc, kc=kc, jp=jp: e.matmul(
                ps_h[jp][:, 0:8], ln2b[kc][:, tc * 128:tc * 128 + 128],
                gwb[:, kc * 8:kc * 8 + 8], start=(kc == 0), stop=(kc == 7)),
                waits=([lndone, gwf, (S["dv"], gu)] if kc == 0 else ()))
        mmv = S["mm"].v
        ctx.dv(lambda e, tc=tc, jp=jp: e.tensor_add(glog[:, :], ps_h[jp][:, 0:8],
                                                    gbb[:, :]), waits=[(S["mm"], mmv)])
        dvv = S["dv"].v
        ctx.dv(lambda e: e.reduce_max(gm1[:, :], glog[:, :], axis=AX.X),
               waits=[(S["dv"], dvv)])
        dvv = S["dv"].v
        ctx.dv(lambda e, tc=tc: e.tensor_single_scalar(
            msk[:, tc * 8:tc * 8 + 8], glog[:, :], gm1[:, 0:1], ALU.is_ge),
            waits=[(S["dv"], dvv)])
        gu = S["dv"].v
    gdone = (S["dv"], S["dv"].v)

    ctx.dv(lambda e: e.memset(acc[:, :], 0.0))
    accz = S["dv"].v
    w1_mm = 0
    w2_mm = 0
    acc_use = 0
    for e_ in range(8):
        for half in range(2):
            ctx.dma("gpsimd", w1buf[:, :], w1r[e_, half],
                    waits=[(S["mm"], w1_mm)] if w1_mm else [])
            w1f = ctx.fence("gpsimd")
            ctx.dma("gpsimd", w2buf[:, :], w2r[e_, half],
                    waits=[(S["mm"], w2_mm)] if w2_mm else [])
            w2f = ctx.fence("gpsimd")
            for tsub in range(4):
                for hc in range(16):
                    jp = hc % 2
                    for kc in range(8):
                        ctx.mm(lambda e, hc=hc, kc=kc, tsub=tsub, jp=jp: e.matmul(
                            ps_h[jp][:, :],
                            w1buf[:, kc * TC + hc * 128:kc * TC + hc * 128 + 128],
                            ln2b[kc][:, tsub * 512:tsub * 512 + 512],
                            start=(kc == 0), stop=(kc == 7)),
                            waits=([w1f, lndone, (S["ac"], getattr(ctx, "_hev", [0, 0])[jp])]
                                   if kc == 0 else ()))
                    mmv = S["mm"].v
                    hw = [(S["mm"], mmv)]
                    if tsub == 0 and hc < 2 and w2_mm:
                        hw.append((S["mm"], w2_mm))
                    ctx.ac(lambda e, hc=hc, jp=jp: e.activation(
                        hbuf[:, hc * 512:hc * 512 + 512], ps_h[jp][:, :],
                        AF.Gelu_apprx_tanh), waits=hw)
                    if not hasattr(ctx, "_hev"):
                        ctx._hev = [0, 0]
                    ctx._hev[jp] = S["ac"].v
                hdone = (S["ac"], S["ac"].v)
                for tcl in range(4):
                    tc = tsub * 4 + tcl
                    for ds in range(2):
                        jp = (tcl * 2 + ds) % 2
                        for hc in range(16):
                            ctx.mm(lambda e, hc=hc, tcl=tcl, ds=ds, jp=jp: e.matmul(
                                ps_y[jp][:, :],
                                hbuf[:, hc * 512 + tcl * 128:hc * 512 + tcl * 128 + 128],
                                w2buf[:, hc * 1024 + ds * 512:hc * 1024 + ds * 512 + 512],
                                start=(hc == 0), stop=(hc == 15)),
                                waits=([hdone, w2f, (S["dv"], acc_use)] if hc == 0 else ()))
                        mmv = S["mm"].v
                        ctx.dv(lambda e, e_=e_, tc=tc, ds=ds, jp=jp: e.scalar_tensor_tensor(
                            acc[:, tc * 1024 + ds * 512:tc * 1024 + ds * 512 + 512],
                            ps_y[jp][:, :], msk[:, tc * 8 + e_:tc * 8 + e_ + 1],
                            acc[:, tc * 1024 + ds * 512:tc * 1024 + ds * 512 + 512],
                            ALU.mult, ALU.add),
                            waits=[(S["mm"], mmv), gdone] + ([(S["dv"], accz)] if e_ == 0 else []))
                        acc_use = S["dv"].v
                w2_mm = S["mm"].v
            w1_mm = S["mm"].v
    moedone = (S["dv"], S["dv"].v)

    z2u = 0
    for tc in range(16):
        for kc in range(8):
            jt = kc % 2
            tw = [moedone]
            if getattr(ctx, "_ztev", [0, 0])[jt]:
                tw = [(S["dv"], ctx._ztev[jt])]
            ctx.mm(lambda e, tc=tc, kc=kc, jt=jt: e.transpose(
                ps_t[jt][:, 0:128], acc[:, tc * 1024 + kc * 128:tc * 1024 + kc * 128 + 128],
                idt[:, :]), waits=tw)
            mmv = S["mm"].v
            ctx.dv(lambda e, tc=tc, kc=kc, jt=jt: e.tensor_add(
                z1b[kc][:, tc * 128:tc * 128 + 128],
                z1b[kc][:, tc * 128:tc * 128 + 128], ps_t[jt][:, 0:128]),
                waits=[(S["mm"], mmv)])
            if not hasattr(ctx, "_ztev"):
                ctx._ztev = [0, 0]
            ctx._ztev[jt] = S["dv"].v
    z2done = (S["dv"], S["dv"].v)

    rsf = [None, None]

    def applyr2(i, rstdb, pb, bcf):
        lw = [rsf[0]] if rsf[0] is not None else []
        ctx.dma("sync", r1stg[0][:, :], r1t.rearrange("(c p) t -> c p t", p=128)[i], waits=lw)
        rf = ctx.fence("sync")
        tmpb = acc[:, 0:2048]
        ctx.dv(lambda e, i=i, tmpb=tmpb: e.tensor_mul(tmpb, z1b[i][:, :], rstdb), waits=[bcf])
        ctx.dv(lambda e, tmpb=tmpb: e.tensor_sub(tmpb, tmpb, pb))
        ctx.dv(lambda e, tmpb=tmpb: e.tensor_add(r1stg[0][:, :], r1stg[0][:, :], tmpb),
               waits=[rf])
        dvv = S["dv"].v
        ctx.dma("sync", r2t_o.rearrange("(c p) t -> c p t", p=128)[i], r1stg[0][:, :],
                waits=[(S["dv"], dvv)])
        rsf[0] = ctx.fence("sync")

    emit_layernorm(ctx, "lnR", 8, TC, lambda i: z1b[i][:, :], applyr2, z2done,
                   ones16[:, 0:1], onesc[:, 0:1], zf,
                   ps_ln[0][0:1, :], ps_ln[1][0:1, :], eps_ap=cvec[0:1, 0:1])
    ctx.q["vector"].append(lambda e, v1=S["dgp"].v, v2=S["dsy"].v:
                           (e.wait_ge(S["dgp"].h, v1), e.wait_ge(S["dsy"].h, v2)))
    ctx.finish()
    return nc



def build_lc1():
    nc = bass.Bass("TRN2", target_bir_lowering=False)
    ctx = Ctx(nc)
    S = ctx.S
    TF = 2 * TC
    o0 = 0
    KMAXT = TF

    r2full = nc.dram_tensor("r2full", [D, TF], F32, kind="ExternalInput").ap()
    wqkv2 = nc.dram_tensor("wqkv2", [16, 128, 3, 8, 64], F32, kind="ExternalInput").ap()
    mskcd = nc.dram_tensor("mskc", [4, 128, 512], F32, kind="ExternalInput").ap()
    maskPd = nc.dram_tensor("maskP", [128, 1], F32, kind="ExternalInput").ap()
    idend = nc.dram_tensor("iden16", [128, 128], BF16, kind="ExternalInput").ap()
    ones32d = nc.dram_tensor("ones32", [128, 1], F32, kind="ExternalInput").ap()
    ones16d = nc.dram_tensor("ones16b", [128, 1], BF16, kind="ExternalInput").ap()
    cvecd = nc.dram_tensor("cvec", [128, 2], F32, kind="ExternalInput").ap()
    ao_o = nc.dram_tensor("ao", [16, TC, 64], BF16, kind="ExternalOutput").ap()

    nkc = KMAXT // 128
    zb = [ctx.sbuf(f"zb{i}", [128, TF], BF16) for i in range(8)]
    sc = ctx.sbuf("sc", [128, KMAXT], F32)
    psb = ctx.sbuf("psb", [128, KMAXT], BF16)
    pbt = [ctx.sbuf(f"pbt{j}", [128, 1024], BF16) for j in range(2)]
    qt = [ctx.sbuf(f"qt{j}", [64, TC], BF16) for j in range(2)]
    kt = [ctx.sbuf(f"kt{j}", [64, KMAXT], BF16) for j in range(2)]
    vt = [ctx.sbuf(f"vt{j}", [128, nkc * 64], BF16) for j in range(2)]
    wq = [ctx.sbuf(f"wq{j}", [128, 3 * 512], BF16) for j in range(2)]
    mskt = ctx.sbuf("mskt", [128, 4 * 512], F32)
    maskP = ctx.sbuf("maskPt", [128, 1], F32)
    idt = ctx.sbuf("idt", [128, 128], BF16)
    onesc = ctx.sbuf("onesc", [128, 1], F32)
    ones16 = ctx.sbuf("ones16t", [128, 1], BF16)
    cvec = ctx.sbuf("cvect", [128, 2], F32)
    nmx = ctx.sbuf("nmx", [128, 1], F32)
    zrow = ctx.sbuf("zrow", [128, 1], F32)
    rz = ctx.sbuf("rz", [128, 1], F32)
    pvq = [ctx.sbuf(f"pvq{j}", [128, 64], BF16) for j in range(2)]
    ps_s = [ctx.psum(f"ps_s{j}", [128, 512], F32) for j in range(2)]
    ps_qv = [ctx.psum(f"ps_qv{j}", [128, 512], F32) for j in range(2)]
    ps_pv = ctx.psum("ps_pv", [128, 64], F32)
    ps_t = [ctx.psum(f"ps_t{j}", [128, 1024], BF16) for j in range(2)]

    for i in range(8):
        ctx.dma("gpsimd", zb[i][:, :], r2full.rearrange("(c p) t -> c p t", p=128)[i])
    zf = ctx.fence("gpsimd")
    ctx.dma("sync", idt[:, :], idend)
    ctx.dma("sync", onesc[:, :], ones32d)
    ctx.dma("sync", ones16[:, :], ones16d)
    ctx.dma("sync", cvec[:, :], cvecd)
    for r in range(4):
        ctx.dma("sync", mskt[:, r * 512:r * 512 + 512], mskcd[r])
    ctx.dma("sync", maskP[:, :], maskPd)
    consts = ctx.fence("sync")

    for part in range(2):
        c0 = part * TC

        def applyp(i, rstdb, pb, bcf, c0=c0):
            ctx.dv(lambda e, i=i, c0=c0: e.tensor_mul(
                sc[:, 0:TC], zb[i][:, c0:c0 + TC], rstdb), waits=[bcf, consts])
            ctx.dv(lambda e, i=i, c0=c0: e.tensor_sub(
                zb[i][:, c0:c0 + TC], sc[:, 0:TC], pb), waits=[(S["dv"], S["dv"].v)])

        emit_layernorm(ctx, f"ln0_{part}", 8, TC, lambda i, c0=c0: zb[i][:, c0:c0 + TC],
                       applyp, zf, ones16[:, 0:1], onesc[:, 0:1], consts,
                       ps_s[0][0:1, 0:512], ps_s[1][0:1, 0:512], eps_ap=cvec[0:1, 0:1])
    lndone = (S["dv"], S["dv"].v)

    wq_mm = [0, 0]
    qkv_dv = [0, 0]
    head_attn_mm = [0, 0]
    sc_ev = [0, 0]
    sc_ev_dv = [0, 0]
    pbt_mm = [0, 0]
    pbt_ac = [0, 0]
    pvps_free = 0
    psb_mm = 0
    nm_ac = 0
    nm_dv = 0
    pvq_dma = [None, None]
    for n in range(16):
        j = n % 2
        ctx.dma("gpsimd", wq[j][:, :], wqkv2[n],
                waits=[(S["mm"], wq_mm[j])] if wq_mm[j] else [])
        wf = ctx.fence("gpsimd")
        for seg in range(4):
            jp = seg % 2
            for kc in range(8):
                ctx.mm(lambda e, j=j, kc=kc, seg=seg, jp=jp: e.matmul(
                    ps_qv[jp][0:64, :], wq[j][:, kc * 64:kc * 64 + 64],
                    zb[kc][:, o0 + seg * 512:o0 + seg * 512 + 512],
                    start=(kc == 0), stop=(kc == 7)),
                    waits=([wf, lndone, (S["dv"], qkv_dv[jp])] if kc == 0 else ()))
            mmv = S["mm"].v
            ew = [(S["mm"], mmv)]
            if seg < 2 and head_attn_mm[j]:
                ew.append((S["mm"], head_attn_mm[j]))
            ctx.dv(lambda e, j=j, seg=seg, jp=jp: e.tensor_single_scalar(
                qt[j][:, seg * 512:seg * 512 + 512], ps_qv[jp][0:64, :], 0.125, ALU.mult),
                waits=ew)
            qkv_dv[jp] = S["dv"].v
        for seg in range(KMAXT // 512):
            jp = seg % 2
            for kc in range(8):
                ctx.mm(lambda e, j=j, kc=kc, seg=seg, jp=jp: e.matmul(
                    ps_qv[jp][0:64, :], wq[j][:, 512 + kc * 64:512 + kc * 64 + 64],
                    zb[kc][:, seg * 512:seg * 512 + 512],
                    start=(kc == 0), stop=(kc == 7)),
                    waits=([(S["dv"], qkv_dv[jp])] if kc == 0 else ()))
            mmv = S["mm"].v
            ctx.dv(lambda e, j=j, seg=seg, jp=jp: e.tensor_copy(
                kt[j][:, seg * 512:seg * 512 + 512], ps_qv[jp][0:64, :]),
                waits=[(S["mm"], mmv)])
            qkv_dv[jp] = S["dv"].v
        for tc4 in range(nkc // 4):
            jp = tc4 % 2
            for sub in range(4):
                tc = tc4 * 4 + sub
                for kc in range(8):
                    ctx.mm(lambda e, j=j, kc=kc, tc=tc, sub=sub, jp=jp: e.matmul(
                        ps_qv[jp][:, sub * 64:sub * 64 + 64],
                        zb[kc][:, tc * 128:tc * 128 + 128],
                        wq[j][:, 1024 + kc * 64:1024 + kc * 64 + 64],
                        start=(kc == 0 and sub == 0), stop=(kc == 7 and sub == 3)),
                        waits=([(S["dv"], qkv_dv[jp])] if (kc == 0 and sub == 0) else ()))
            mmv = S["mm"].v
            ctx.dv(lambda e, j=j, tc4=tc4, jp=jp: e.tensor_copy(
                vt[j][:, tc4 * 256:tc4 * 256 + 256], ps_qv[jp][:, 0:256]),
                waits=[(S["mm"], mmv)])
            qkv_dv[jp] = S["dv"].v
        qkvf = (S["dv"], S["dv"].v)
        wq_mm[j] = S["mm"].v

        for qtl in range(16):
            q0g = qtl * 128
            nsego = (q0g + 128 + 511) // 512
            nseg = nsego + 4
            kw = nseg * 512
            seglist = list(range(nsego)) + [4, 5, 6, 7]
            for si, seg in enumerate(seglist):
                jp = seg % 2
                mw = [(S["ac"], sc_ev[jp])] if sc_ev[jp] else []
                if sc_ev_dv[jp]:
                    mw.append((S["dv"], sc_ev_dv[jp]))
                if si == 0:
                    mw.append(qkvf)
                ctx.mm(lambda e, j=j, qtl=qtl, seg=seg, jp=jp: e.matmul(
                    ps_s[jp][:, :], qt[j][:, qtl * 128:qtl * 128 + 128],
                    kt[j][:, seg * 512:seg * 512 + 512], start=True, stop=True),
                    waits=mw)
                mmv = S["mm"].v
                sw = [(S["mm"], mmv)]
                if si == 0 and nm_dv:
                    sw.append((S["dv"], nm_dv))
                if seg >= 4:
                    ctx.dv(lambda e, seg=seg, jp=jp: e.tensor_single_scalar(
                        sc[:, seg * 512:seg * 512 + 512], ps_s[jp][:, :],
                        maskP[:, 0:1], ALU.add), waits=sw + [consts])
                    sc_ev_dv[jp] = S["dv"].v
                elif seg == nsego - 1:
                    r = (q0g - seg * 512) // 128
                    ctx.dv(lambda e, seg=seg, jp=jp, r=r: e.scalar_tensor_tensor(
                        sc[:, seg * 512:seg * 512 + 512], ps_s[jp][:, :], 1.0,
                        mskt[:, r * 512:r * 512 + 512], ALU.mult, ALU.add), waits=sw)
                    sc_ev_dv[jp] = S["dv"].v
                else:
                    ctx.ac(lambda e, seg=seg, jp=jp: e.copy(
                        sc[:, seg * 512:seg * 512 + 512], ps_s[jp][:, :]), waits=sw)
                    sc_ev[jp] = S["ac"].v
            hole0, hole1 = nsego * 512, 2048
            if hole1 > hole0:
                ctx.dv(lambda e, hole0=hole0, hole1=hole1: e.memset(
                    sc[:, hole0:hole1], -1e30))
            acv = S["ac"].v
            dvv0 = S["dv"].v
            rw = [(S["ac"], acv), (S["dv"], dvv0)]
            if nm_ac:
                rw.append((S["ac"], nm_ac))
            ctx.dv(lambda e: e.reduce_max(nmx[:, :], sc[:, :], axis=AX.X,
                                          negate=True), waits=rw)
            nm_dv = S["dv"].v
            ew = [(S["dv"], nm_dv)]
            if psb_mm:
                ew.append((S["mm"], psb_mm))
            ctx.ac(lambda e: e.activation(psb[:, :], sc[:, :], AF.Exp,
                                          bias=nmx[:, 0:1], accum_out=zrow[:, 0:1]),
                   waits=ew)
            expv = S["ac"].v
            nm_ac = expv
            ctx.dv(lambda e: e.reciprocal(rz[:, :], zrow[:, 0:1]), waits=[(S["ac"], expv)])
            rzv = S["dv"].v
            for grp in range(4):
                jt = grp % 2
                tw = [(S["ac"], expv)]
                if pbt_ac[jt]:
                    tw.append((S["ac"], pbt_ac[jt]))
                for s8 in range(8):
                    kk = grp * 8 + s8
                    ctx.mm(lambda e, kk=kk, s8=s8, jt=jt: e.transpose(
                        ps_t[jt][:, s8 * 128:s8 * 128 + 128],
                        psb[:, kk * 128:kk * 128 + 128], idt[:, :]),
                        waits=(tw if s8 == 0 else ()))
                mmv = S["mm"].v
                ctx.ac(lambda e, jt=jt: e.copy(pbt[jt][:, :], ps_t[jt][:, :]),
                       waits=[(S["mm"], mmv), (S["mm"], pbt_mm[jt])])
                pbt_ac[jt] = S["ac"].v
                pw = [(S["ac"], pbt_ac[jt])]
                if grp == 0 and pvps_free:
                    pw.append((S["dv"], pvps_free))
                for s8 in range(8):
                    kk = grp * 8 + s8
                    ctx.mm(lambda e, kk=kk, s8=s8, j=j, jt=jt: e.matmul(
                        ps_pv[:, :], pbt[jt][:, s8 * 128:s8 * 128 + 128],
                        vt[j][:, kk * 64:kk * 64 + 64],
                        start=(kk == 0), stop=(kk == 31)),
                        waits=(pw if s8 == 0 else ()))
                pbt_mm[jt] = S["mm"].v
            psb_mm = S["mm"].v
            mmv = S["mm"].v
            jq = qtl % 2
            pq = [(S["mm"], mmv), (S["dv"], rzv)]
            if pvq_dma[jq] is not None:
                pq.append(pvq_dma[jq])
            ctx.dv(lambda e, jq=jq: e.tensor_single_scalar(
                pvq[jq][:, :], ps_pv[:, :], rz[:, 0:1], ALU.mult), waits=pq)
            pvps_free = S["dv"].v
            ctx.dma("sync", ao_o[n][qtl * 128:qtl * 128 + 128, :], pvq[jq][:, :],
                    waits=[(S["dv"], pvps_free)])
            pvq_dma[jq] = ctx.fence("sync")
        head_attn_mm[j] = S["mm"].v
    ctx.q["vector"].append(lambda e, v1=S["dgp"].v, v2=S["dsy"].v:
                           (e.wait_ge(S["dgp"].h, v1), e.wait_ge(S["dsy"].h, v2)))
    ctx.finish()
    return nc



def build_lc2():
    nc = bass.Bass("TRN2", target_bir_lowering=False)
    ctx = Ctx(nc)
    S = ctx.S

    aoresh = nc.dram_tensor("aoresh", [8, 128, TC], BF16, kind="ExternalInput").ap()
    woutf = nc.dram_tensor("woutf", [8, 128, 8, 128], F32, kind="ExternalInput").ap()
    r2t = nc.dram_tensor("r2t", [D, TC], F32, kind="ExternalInput").ap()
    ones32d = nc.dram_tensor("ones32", [128, 1], F32, kind="ExternalInput").ap()
    ones16d = nc.dram_tensor("ones16b", [128, 1], BF16, kind="ExternalInput").ap()
    cvecd = nc.dram_tensor("cvec", [128, 2], F32, kind="ExternalInput").ap()
    r3t_o = nc.dram_tensor("r3t", [D, TC], F32, kind="ExternalOutput").ap()

    aob = [ctx.sbuf(f"aob{i}", [128, TC], BF16) for i in range(8)]
    wob = [ctx.sbuf(f"wob{j}", [128, 1024], BF16) for j in range(2)]
    wsumc = ctx.sbuf("wsumc", [128, 8], F32)
    onesc = ctx.sbuf("onesc", [128, 1], F32)
    ones16 = ctx.sbuf("ones16t", [128, 1], BF16)
    cvec = ctx.sbuf("cvect", [128, 2], F32)
    lntmp = ctx.sbuf("lntmp", [128, TC], F32)
    rstg = ctx.sbuf("rstg", [128, TC], F32)
    ps = [ctx.psum(f"ps{j}", [128, 512], F32) for j in range(2)]
    ps_ln = [ctx.psum(f"ps_ln{j}", [1, 512], F32) for j in range(2)]

    for i in range(8):
        ctx.dma("sync", aob[i][:, :], aoresh[i])
    ctx.dma("sync", onesc[:, :], ones32d)
    ctx.dma("sync", ones16[:, :], ones16d)
    ctx.dma("sync", cvec[:, :], cvecd)
    aof = ctx.fence("sync")

    wu = 0
    wobf = [None, None]
    for rc in range(8):
        j = rc % 2
        w = [(S["mm"], wu)] if wu else []
        ctx.dma("gpsimd", wob[j][:, :], woutf[rc], waits=w)
        wobf[j] = ctx.fence("gpsimd")
        for dc in range(8):
            ctx.mm(lambda e, j=j, dc=dc, rc=rc: e.matmul(
                ps[0][0:128, dc:dc + 1], wob[j][:, dc * 128:dc * 128 + 128],
                ones16[:, 0:1], start=(rc == 0 and dc == 0), stop=(rc == 7 and dc == 7)),
                waits=([wobf[j], aof] if dc == 0 else ()))
        wu = S["mm"].v
    mmv = S["mm"].v
    ctx.dv(lambda e: e.tensor_copy(wsumc[:, :], ps[0][0:128, 0:8]), waits=[(S["mm"], mmv)])
    wsv = S["dv"].v

    for i in range(8):
        ctx.dv(lambda e, i=i: e.tensor_single_scalar(
            aob[i][:, :], aob[i][:, :], wsumc[:, i:i + 1], ALU.mult),
            waits=[(S["dv"], wsv), aof])
    sdone = (S["dv"], S["dv"].v)

    rsf = [None]

    def applyr(i, rstdb, pb, bcf):
        lw = [rsf[0]] if rsf[0] is not None else []
        ctx.dma("sync", rstg[:, :], r2t.rearrange("(c p) t -> c p t", p=128)[i], waits=lw)
        rf = ctx.fence("sync")
        ctx.dv(lambda e, i=i: e.tensor_mul(lntmp[:, :], aob[i][:, :], rstdb), waits=[bcf])
        ctx.dv(lambda e: e.tensor_sub(lntmp[:, :], lntmp[:, :], pb),
               waits=[(S["dv"], S["dv"].v)])
        ctx.dv(lambda e: e.tensor_add(rstg[:, :], rstg[:, :], lntmp[:, :]),
               waits=[rf, (S["dv"], S["dv"].v)])
        dvv = S["dv"].v
        ctx.dma("sync", r3t_o.rearrange("(c p) t -> c p t", p=128)[i], rstg[:, :],
                waits=[(S["dv"], dvv)])
        rsf[0] = ctx.fence("sync")

    emit_layernorm(ctx, "lnC", 8, TC, lambda i: aob[i][:, :], applyr, sdone,
                   ones16[:, 0:1], onesc[:, 0:1], aof,
                   ps_ln[0][0:1, :], ps_ln[1][0:1, :], eps_ap=cvec[0:1, 0:1])
    ctx.q["vector"].append(lambda e, v1=S["dgp"].v, v2=S["dsy"].v:
                           (e.wait_ge(S["dgp"].h, v1), e.wait_ge(S["dsy"].h, v2)))
    ctx.finish()
    return nc



def build_ld1():
    nc = bass.Bass("TRN2", target_bir_lowering=False)
    ctx = Ctx(nc)
    S = ctx.S
    TF = 2 * TC
    o0 = 0

    r3full = nc.dram_tensor("r3full", [D, TF], F32, kind="ExternalInput").ap()
    wrapmd = nc.dram_tensor("wrapm", [2, 128, 64], BF16, kind="ExternalInput").ap()
    ln1wb = nc.dram_tensor("ln1wb", [2, 128, 8], F32, kind="ExternalInput").ap()
    gatewA = nc.dram_tensor("gatewA", [8, 128, 8], F32, kind="ExternalInput").ap()
    qw = nc.dram_tensor("qw", [8, 128, 8, 128], F32, kind="ExternalInput").ap()
    qbc = nc.dram_tensor("qbc", [128, 8], F32, kind="ExternalInput").ap()
    kw = nc.dram_tensor("kw", [128, 8, 128], F32, kind="ExternalInput").ap()
    kbc = nc.dram_tensor("kbc", [128, 1], F32, kind="ExternalInput").ap()
    vw = nc.dram_tensor("vw", [8, 128, 128], F32, kind="ExternalInput").ap()
    vbrow = nc.dram_tensor("vbrow", [1, 128], F32, kind="ExternalInput").ap()
    ow = nc.dram_tensor("ow", [8, 128, 8, 128], F32, kind="ExternalInput").ap()
    obb_d = nc.dram_tensor("obb_d", [8, D], F32, kind="ExternalInput").ap()
    reltT = nc.dram_tensor("reltT", [128, 132], F32, kind="ExternalInput").ap()
    bmasks = nc.dram_tensor("bmasks", [3, 128, 256], F32, kind="ExternalInput").ap()
    idend = nc.dram_tensor("iden16", [128, 128], BF16, kind="ExternalInput").ap()
    ones32d = nc.dram_tensor("ones32", [128, 1], F32, kind="ExternalInput").ap()
    ones16d = nc.dram_tensor("ones16b", [128, 1], BF16, kind="ExternalInput").ap()
    cvecd = nc.dram_tensor("cvec", [128, 2], F32, kind="ExternalInput").ap()
    z4t_o = nc.dram_tensor("z4t", [D, TC], BF16, kind="ExternalOutput").ap()
    relhbm = [nc.dram_tensor(f"relhbm{j}", [128, 129], F32) for j in range(2)]

    xnb = [ctx.sbuf(f"xnb{i}", [128, TF], BF16) for i in range(8)]
    z3o = [ctx.sbuf(f"z3o{i}", [128, TC], BF16) for i in range(8)]
    qtA = [ctx.sbuf(f"qtA{e}", [128, TC], BF16) for e in range(8)]
    kt = ctx.sbuf("kt", [128, TF], BF16)
    vtt = ctx.sbuf("vtt", [128, TF], BF16)
    sc = ctx.sbuf("sc", [128, TF], F32)
    psb = ctx.sbuf("psb", [128, TF], BF16)
    pbt = [ctx.sbuf(f"pbt{j}", [128, 1024], BF16) for j in range(2)]
    gA = ctx.sbuf("gA", [128, 16 * 8], F32)
    gTb = ctx.sbuf("gTb", [8, TC], BF16)
    relq = ctx.sbuf("relq", [128, 132], F32)
    skw = ctx.sbuf("skw", [128, 256], F32)
    wrapm = ctx.sbuf("wrapmt", [128, 2 * 64], BF16)
    bmt = ctx.sbuf("bmt", [128, 3 * 256], BF16)
    qwb = [ctx.sbuf(f"qwb{j}", [128, 1024], BF16) for j in range(2)]
    owb = [ctx.sbuf(f"owb{j}", [128, 1024], BF16) for j in range(2)]
    obb = ctx.sbuf("obb", [8, D], BF16)
    gwA = ctx.sbuf("gwA", [128, 64], BF16)
    qbt = ctx.sbuf("qbt", [128, 8], F32)
    kbt = ctx.sbuf("kbt", [128, 1], F32)
    vbb = ctx.sbuf("vbb", [128, 128], BF16)
    glog = ctx.sbuf("glog", [128, 8], F32)
    gm1 = ctx.sbuf("gm1", [128, 1], F32)
    gm2 = ctx.sbuf("gm2", [128, 1], F32)
    geq = ctx.sbuf("geq", [128, 8], F32)
    gden = ctx.sbuf("gden", [128, 1], F32)
    idt = ctx.sbuf("idt", [128, 128], BF16)
    onesc = ctx.sbuf("onesc", [128, 1], F32)
    ones16 = ctx.sbuf("ones16t", [128, 1], BF16)
    cvec = ctx.sbuf("cvect", [128, 2], F32)
    nmx = ctx.sbuf("nmx", [128, 1], F32)
    zrow = ctx.sbuf("zrow", [128, 1], F32)
    rz = ctx.sbuf("rz", [128, 1], F32)
    rzg = ctx.sbuf("rzg", [128, 1], F32)
    cxb = ctx.sbuf("cxb", [128, 128], BF16)
    cxT = ctx.sbuf("cxT", [128, 128], BF16)
    gtb_s = ctx.sbuf("gtb_s", [128, 8], BF16)

    ps_s = [ctx.psum(f"ps_s{j}", [128, 512], F32) for j in range(2)]
    ps_qv = [ctx.psum(f"ps_qv{j}", [128, 512], F32) for j in range(2)]
    ps_pv = ctx.psum("ps_pv", [128, 128], F32)
    ps_moa = [ctx.psum(f"ps_moa{j}", [128, 512], F32) for j in range(2)]
    ps_t = ctx.psum("ps_t", [128, 1024], BF16)

    for i in range(8):
        ctx.dma("gpsimd", xnb[i][:, :], r3full.rearrange("(c p) t -> c p t", p=128)[i])
    for kc in range(8):
        ctx.dma("gpsimd", gwA[:, kc * 8:kc * 8 + 8], gatewA[kc])
    zf = ctx.fence("gpsimd")
    ctx.dma("sync", idt[:, :], idend)
    ctx.dma("sync", onesc[:, :], ones32d)
    ctx.dma("sync", ones16[:, :], ones16d)
    ctx.dma("sync", cvec[:, :], cvecd)
    ctx.dma("sync", qbt[:, :], qbc)
    ctx.dma("sync", kbt[:, :], kbc)
    ctx.dma("gpsimd", vbb[:, :], vbrow.broadcast_to([128, 128]))
    ctx.dma("sync", wrapm[:, 0:64], wrapmd[0])
    ctx.dma("sync", wrapm[:, 64:128], wrapmd[1])
    consts = ctx.fence("sync")
    for r in range(3):
        ctx.dma("gpsimd", bmt[:, r * 256:r * 256 + 256], bmasks[r])
    bmf = ctx.fence("gpsimd")

    for part in range(2):
        c0 = part * TC

        def applyp(i, rstdb, pb, bcf, c0=c0):
            ctx.dv(lambda e, i=i, c0=c0: e.tensor_mul(
                sc[:, 0:TC], xnb[i][:, c0:c0 + TC], rstdb), waits=[bcf, consts])
            ctx.dv(lambda e, i=i, c0=c0: e.tensor_sub(
                xnb[i][:, c0:c0 + TC], sc[:, 0:TC], pb), waits=[(S["dv"], S["dv"].v)])

        emit_layernorm(ctx, f"lnz{part}", 8, TC, lambda i, c0=c0: xnb[i][:, c0:c0 + TC],
                       applyp, zf, ones16[:, 0:1], onesc[:, 0:1], consts,
                       ps_s[0][0:1, 0:512], ps_s[1][0:1, 0:512], eps_ap=cvec[0:1, 0:1])
    z3done = (S["dv"], S["dv"].v)
    for i in range(8):
        ctx.dv(lambda e, i=i: e.tensor_copy(z3o[i][:, :], xnb[i][:, o0:o0 + TC]),
               waits=[z3done] if i == 0 else [])
    z3ocp = (S["dv"], S["dv"].v)

    lnw = ctx.sbuf("lnw", [128, 8], F32)
    lnb = ctx.sbuf("lnb", [128, 8], F32)
    ctx.dma("sync", lnw[:, :], ln1wb[0])
    ctx.dma("sync", lnb[:, :], ln1wb[1])
    lnwf = ctx.fence("sync")
    for part in range(2):
        c0 = part * TC

        def applyq(i, rstdb, pb, bcf, c0=c0):
            ctx.dv(lambda e, i=i, c0=c0: e.tensor_mul(
                sc[:, 0:TC], xnb[i][:, c0:c0 + TC], rstdb), waits=[bcf, lnwf, z3ocp])
            ctx.dv(lambda e: e.tensor_sub(sc[:, 0:TC], sc[:, 0:TC], pb),
                   waits=[(S["dv"], S["dv"].v)])
            ctx.dv(lambda e, i=i, c0=c0: e.tensor_scalar(
                xnb[i][:, c0:c0 + TC], sc[:, 0:TC], lnw[:, i:i + 1], lnb[:, i:i + 1],
                ALU.mult, ALU.add), waits=[(S["dv"], S["dv"].v)])

        emit_layernorm(ctx, f"lnx{part}", 8, TC, lambda i, c0=c0: xnb[i][:, c0:c0 + TC],
                       applyq, z3ocp, ones16[:, 0:1], onesc[:, 0:1], consts,
                       ps_s[0][0:1, 0:512], ps_s[1][0:1, 0:512], eps_ap=cvec[0:1, 0:1])
    lndone = (S["dv"], S["dv"].v)

    gu = 0
    for tc in range(16):
        jp = tc % 2
        for kc in range(8):
            ctx.mm(lambda e, tc=tc, kc=kc, jp=jp: e.matmul(
                ps_qv[jp][:, 0:8], xnb[kc][:, o0 + tc * 128:o0 + tc * 128 + 128],
                gwA[:, kc * 8:kc * 8 + 8], start=(kc == 0), stop=(kc == 7)),
                waits=([lndone, zf, (S["dv"], gu)] if kc == 0 else ()))
        mmv = S["mm"].v
        ctx.dv(lambda e, jp=jp: e.tensor_copy(glog[:, :], ps_qv[jp][:, 0:8]),
               waits=[(S["mm"], mmv), (S["ac"], getattr(ctx, "_gprev_ac", 0))])
        dvv = S["dv"].v
        ctx.dv(lambda e: e.reduce_max(gm1[:, :], glog[:, :], axis=AX.X, negate=True),
               waits=[(S["dv"], dvv)])
        dvv = S["dv"].v
        ctx.dv(lambda e: e.tensor_single_scalar(geq[:, :], glog[:, :], gm1[:, 0:1],
                                                ALU.add), waits=[(S["dv"], dvv)])
        ctx.dv(lambda e: e.tensor_single_scalar(geq[:, :], geq[:, :], 0.0, ALU.is_ge),
               waits=[(S["dv"], S["dv"].v)])
        ctx.dv(lambda e: e.scalar_tensor_tensor(geq[:, :], geq[:, :], -1e30, glog[:, :],
                                                ALU.mult, ALU.add),
               waits=[(S["dv"], S["dv"].v)])
        ctx.dv(lambda e: e.reduce_max(gm2[:, :], geq[:, :], axis=AX.X, negate=True),
               waits=[(S["dv"], S["dv"].v)])
        ctx.dv(lambda e, tc=tc: e.tensor_single_scalar(
            gA[:, tc * 8:tc * 8 + 8], glog[:, :], gm2[:, 0:1], ALU.add),
            waits=[(S["dv"], S["dv"].v)])
        ctx.dv(lambda e, tc=tc: e.tensor_single_scalar(
            gA[:, tc * 8:tc * 8 + 8], gA[:, tc * 8:tc * 8 + 8], 0.0, ALU.is_ge),
            waits=[(S["dv"], S["dv"].v)])
        dvv = S["dv"].v
        ctx.ac(lambda e: e.activation(glog[:, :], glog[:, :], AF.Exp, bias=gm1[:, 0:1]),
               waits=[(S["dv"], dvv)])
        ctx.ac(lambda e: e.activation(gden[:, :], gm2[:, 0:1], AF.Exp, scale=-1.0,
                                      bias=gm1[:, 0:1]))
        ctx._gprev_ac = S["ac"].v
        acv = S["ac"].v
        ctx.dv(lambda e: e.tensor_single_scalar(gden[:, :], gden[:, :], 1.0, ALU.add),
               waits=[(S["ac"], acv)])
        ctx.dv(lambda e: e.reciprocal(gden[:, :], gden[:, :]),
               waits=[(S["dv"], S["dv"].v)])
        dvv = S["dv"].v
        acv2 = S["ac"].v
        ctx.dv(lambda e, tc=tc: e.tensor_mul(gA[:, tc * 8:tc * 8 + 8],
                                             gA[:, tc * 8:tc * 8 + 8], glog[:, :]),
               waits=[(S["dv"], dvv), (S["ac"], acv2)])
        ctx.dv(lambda e, tc=tc: e.tensor_single_scalar(
            gA[:, tc * 8:tc * 8 + 8], gA[:, tc * 8:tc * 8 + 8], gden[:, 0:1], ALU.mult),
            waits=[(S["dv"], S["dv"].v)])
        ctx.dv(lambda e, tc=tc: e.tensor_copy(gtb_s[:, :], gA[:, tc * 8:tc * 8 + 8]))
        dvv = S["dv"].v
        ctx.mm(lambda e, tc=tc: e.transpose(ps_t[0:8, 0:128], gtb_s[:, :], idt[:, :]),
               waits=[(S["dv"], dvv), (S["ac"], getattr(ctx, "_gtev", 0))])
        mmv = S["mm"].v
        ctx.ac(lambda e, tc=tc: e.copy(gTb[0:8, tc * 128:tc * 128 + 128],
                                       ps_t[0:8, 0:128]), waits=[(S["mm"], mmv)])
        ctx._gtev = S["ac"].v
        gu = S["dv"].v
    gdone = (S["dv"], S["dv"].v)
    gdone_ac = (S["ac"], S["ac"].v)

    qw_mm = [0, 0]
    qkv_dv = [0, 0]
    for e_ in range(8):
        j = e_ % 2
        ctx.dma("gpsimd", qwb[j][:, :], qw[e_],
                waits=[(S["mm"], qw_mm[j])] if qw_mm[j] else [])
        wf = ctx.fence("gpsimd")
        for seg in range(4):
            jp = seg % 2
            for kc in range(8):
                ctx.mm(lambda e, j=j, kc=kc, seg=seg, jp=jp: e.matmul(
                    ps_qv[jp][:, :], qwb[j][:, kc * 128:kc * 128 + 128],
                    xnb[kc][:, o0 + seg * 512:o0 + seg * 512 + 512],
                    start=(kc == 0), stop=(kc == 7)),
                    waits=([wf, lndone, (S["dv"], qkv_dv[jp])] if kc == 0 else ()))
            mmv = S["mm"].v
            ctx.dv(lambda e, e_=e_, seg=seg, jp=jp: e.tensor_scalar(
                qtA[e_][:, seg * 512:seg * 512 + 512], ps_qv[jp][:, :],
                qbt[:, e_:e_ + 1], float(128 ** -0.5), ALU.add, ALU.mult),
                waits=[(S["mm"], mmv), consts])
            qkv_dv[jp] = S["dv"].v
        qw_mm[j] = S["mm"].v
    kwb = qwb[0]
    ctx.dma("gpsimd", kwb[:, :], kw, waits=[(S["mm"], qw_mm[0])])
    kwf = ctx.fence("gpsimd")
    for seg in range(8):
        jp = seg % 2
        for kc in range(8):
            ctx.mm(lambda e, kc=kc, seg=seg, jp=jp: e.matmul(
                ps_qv[jp][:, :], kwb[:, kc * 128:kc * 128 + 128],
                xnb[kc][:, seg * 512:seg * 512 + 512],
                start=(kc == 0), stop=(kc == 7)),
                waits=([kwf, (S["dv"], qkv_dv[jp])] if kc == 0 else ()))
        mmv = S["mm"].v
        ctx.dv(lambda e, seg=seg, jp=jp: e.tensor_single_scalar(
            kt[:, seg * 512:seg * 512 + 512], ps_qv[jp][:, :], kbt[:, 0:1], ALU.add),
            waits=[(S["mm"], mmv)])
        qkv_dv[jp] = S["dv"].v
    vwb = qwb[1]
    ctx.dma("gpsimd", vwb[:, :], vw.rearrange("c p q -> p c q") if False else vw,
            waits=[(S["mm"], qw_mm[1])])
    vwf = ctx.fence("gpsimd")
    for tc in range(32):
        jp = tc % 2
        for kc in range(8):
            ctx.mm(lambda e, kc=kc, tc=tc, jp=jp: e.matmul(
                ps_qv[jp][:, 0:128], xnb[kc][:, tc * 128:tc * 128 + 128],
                vwb[:, kc * 128:kc * 128 + 128],
                start=(kc == 0), stop=(kc == 7)),
                waits=([vwf, (S["dv"], qkv_dv[jp])] if kc == 0 else ()))
        mmv = S["mm"].v
        ctx.dv(lambda e, tc=tc, jp=jp: e.tensor_add(
            vtt[:, tc * 128:tc * 128 + 128], ps_qv[jp][:, 0:128], vbb[:, :]),
            waits=[(S["mm"], mmv)])
        qkv_dv[jp] = S["dv"].v
    qkvf = (S["dv"], S["dv"].v)

    reltb = ctx.sbuf("reltb", [128, 132], BF16)
    ctx.dma("gpsimd", reltb[:, :], reltT)
    relf = ctx.fence("gpsimd")
    ow_mm = [0, 0]
    ctx.dma("gpsimd", obb[:, :], obb_d)
    obf = ctx.fence("gpsimd")
    sc_ev = [0, 0]
    sc_ev_dv = [0, 0]
    pbt_mm = [0, 0]
    pbt_ac = 0
    pvps_free = 0
    psb_mm = 0
    nm_ac = 0
    nm_dv = 0
    relq_dv = 0
    relu_mm = [0, 0]
    moa_dv = 0
    z4dma = None
    for qtl in range(16):
        q0g = qtl * 128
        w0, w1 = q0g - 64, q0g + 192
        w0c, w1c = max(w0, 0), min(w1, TC)
        m0, m1 = w0c - w0, 256 - (w1 - w1c)
        for e_ in range(8):
            j = e_ % 2
            jr = e_ % 2
            ctx.mm(lambda e, e_=e_, qtl=qtl, jr=jr: e.matmul(
                ps_qv[jr][:, 0:132], qtA[e_][:, qtl * 128:qtl * 128 + 128],
                reltb[:, :], start=True, stop=True),
                waits=[qkvf, relf, (S["dv"], relq_dv)])
            mmv = S["mm"].v
            rqw = [(S["mm"], mmv)]
            if getattr(ctx, "_relq_wf", None) is not None:
                rqw.append(ctx._relq_wf)
            ctx.dv(lambda e, jr=jr: e.tensor_copy(relq[:, :], ps_qv[jr][:, 0:132]),
                   waits=rqw)
            relq_dv = S["dv"].v
            ctx.dma("sync", relhbm[j].ap(), relq[:, 0:129], waits=[(S["dv"], relq_dv)])
            rhf = ctx.fence("sync")
            ctx.dma("sync", skw[:, m0:m1], bass.AP(relhbm[j].ap().tensor, m0, [[128, 128], [1, m1 - m0]]),
                    waits=[rhf])
            skf = ctx.fence("sync")
            ctx._relq_wf = skf
            for seg in range(8):
                jp = seg % 2
                s0 = seg * 512
                mw = []
                if sc_ev[jp]:
                    mw.append((S["ac"], sc_ev[jp]))
                if sc_ev_dv[jp]:
                    mw.append((S["dv"], sc_ev_dv[jp]))
                if seg == 0:
                    mw.append(qkvf)
                ctx.mm(lambda e, e_=e_, qtl=qtl, seg=seg, jp=jp: e.matmul(
                    ps_s[jp][:, :], qtA[e_][:, qtl * 128:qtl * 128 + 128],
                    kt[:, seg * 512:seg * 512 + 512], start=True, stop=True),
                    waits=mw)
                mmv = S["mm"].v
                sw = [(S["mm"], mmv)]
                if seg == 0 and nm_dv:
                    sw.append((S["dv"], nm_dv))
                if seg >= 4 or s0 + 512 <= w0c or s0 >= w1c:
                    col = 129 if seg >= 4 else (0 if s0 + 512 <= w0c else 128)
                    sw.append((S["dv"], relq_dv))
                    ctx.dv(lambda e, seg=seg, jp=jp, col=col: e.tensor_single_scalar(
                        sc[:, seg * 512:seg * 512 + 512], ps_s[jp][:, :],
                        relq[:, col:col + 1], ALU.add), waits=sw)
                    sc_ev_dv[jp] = S["dv"].v
                else:
                    ctx.ac(lambda e, seg=seg, jp=jp: e.copy(
                        sc[:, seg * 512:seg * 512 + 512], ps_s[jp][:, :]), waits=sw)
                    sc_ev[jp] = S["ac"].v
                    acv2 = S["ac"].v
                    if s0 < w0c:
                        ctx.dv(lambda e, s0=s0, w0c=w0c: e.tensor_single_scalar(
                            sc[:, s0:w0c], sc[:, s0:w0c], relq[:, 0:1], ALU.add),
                            waits=[(S["ac"], acv2), (S["dv"], relq_dv)])
                        sc_ev_dv[jp] = S["dv"].v
                    if s0 + 512 > w1c:
                        ctx.dv(lambda e, s0=s0, w1c=w1c: e.tensor_single_scalar(
                            sc[:, w1c:s0 + 512], sc[:, w1c:s0 + 512], relq[:, 128:129],
                            ALU.add), waits=[(S["ac"], acv2), (S["dv"], relq_dv)])
                        sc_ev_dv[jp] = S["dv"].v
            ctx.dv(lambda e, m0=m0, m1=m1: e.tensor_mul(
                skw[:, m0:m1], skw[:, m0:m1], bmt[:, m0:m1]), waits=[skf, bmf])
            ctx.dv(lambda e, m0=m0, m1=m1: e.scalar_tensor_tensor(
                skw[:, m0:m1], bmt[:, 256 + m0:256 + m1], relq[:, 0:1], skw[:, m0:m1],
                ALU.mult, ALU.add), waits=[(S["dv"], S["dv"].v)])
            ctx.dv(lambda e, m0=m0, m1=m1: e.scalar_tensor_tensor(
                skw[:, m0:m1], bmt[:, 512 + m0:512 + m1], relq[:, 128:129], skw[:, m0:m1],
                ALU.mult, ALU.add), waits=[(S["dv"], S["dv"].v)])
            ctx.dv(lambda e, m0=m0, m1=m1, w0c=w0c, w1c=w1c: e.tensor_add(
                sc[:, w0c:w1c], sc[:, w0c:w1c], skw[:, m0:m1]),
                waits=[(S["dv"], S["dv"].v)])
            if qtl == 0:
                ctx.dma("sync", skw[:, 0:64],
                        bass.AP(relhbm[j].ap().tensor, 0, [[128, 128], [1, 64]]),
                        waits=[rhf, (S["dv"], S["dv"].v)])
                skf2 = ctx.fence("sync")
                ctx.dv(lambda e: e.tensor_single_scalar(
                    skw[:, 0:64], skw[:, 0:64], relq[:, 129:130], ALU.subtract),
                    waits=[skf2, (S["dv"], relq_dv)])
                ctx.dv(lambda e: e.tensor_mul(skw[:, 0:64], skw[:, 0:64], wrapm[:, 0:64]),
                       waits=[(S["dv"], S["dv"].v), consts])
                ctx.dv(lambda e: e.tensor_add(sc[:, 4032:4096], sc[:, 4032:4096],
                                              skw[:, 0:64]),
                       waits=[(S["dv"], S["dv"].v)])
            if qtl == 15:
                ctx.dma("sync", skw[:, 0:64],
                        bass.AP(relhbm[j].ap().tensor, 192, [[128, 128], [1, 64]]),
                        waits=[rhf, (S["dv"], S["dv"].v)])
                skf2 = ctx.fence("sync")
                ctx.dv(lambda e: e.tensor_single_scalar(
                    skw[:, 0:64], skw[:, 0:64], relq[:, 129:130], ALU.subtract),
                    waits=[skf2, (S["dv"], relq_dv)])
                ctx.dv(lambda e: e.tensor_mul(skw[:, 0:64], skw[:, 0:64],
                                              wrapm[:, 64:128]),
                       waits=[(S["dv"], S["dv"].v), consts])
                ctx.dv(lambda e: e.tensor_add(sc[:, 2048:2112], sc[:, 2048:2112],
                                              skw[:, 0:64]),
                       waits=[(S["dv"], S["dv"].v)])
            win_dv = S["dv"].v
            acv = S["ac"].v
            rw = [(S["ac"], acv), (S["dv"], win_dv)]
            if nm_ac:
                rw.append((S["ac"], nm_ac))
            ctx.dv(lambda e: e.reduce_max(nmx[:, :], sc[:, :], axis=AX.X, negate=True),
                   waits=rw)
            nm_dv = S["dv"].v
            ew = [(S["dv"], nm_dv)]
            if psb_mm:
                ew.append((S["mm"], psb_mm))
            ctx.ac(lambda e: e.activation(psb[:, :], sc[:, :], AF.Exp,
                                          bias=nmx[:, 0:1], accum_out=zrow[:, 0:1]),
                   waits=ew)
            expv = S["ac"].v
            nm_ac = expv
            ctx.dv(lambda e: e.reciprocal(rz[:, :], zrow[:, 0:1]), waits=[(S["ac"], expv)])
            ctx.dv(lambda e, e_=e_, qtl=qtl: e.tensor_single_scalar(
                rzg[:, :], rz[:, 0:1], gA[:, qtl * 8 + e_:qtl * 8 + e_ + 1], ALU.mult),
                waits=[gdone])
            rzv = S["dv"].v
            for grp in range(4):
                tw = [(S["ac"], expv)]
                if pbt_ac:
                    tw.append((S["ac"], pbt_ac))
                jt = grp % 2
                for s8 in range(8):
                    kk = grp * 8 + s8
                    ctx.mm(lambda e, kk=kk, s8=s8: e.transpose(
                        ps_t[:, s8 * 128:s8 * 128 + 128],
                        psb[:, kk * 128:kk * 128 + 128], idt[:, :]),
                        waits=(tw if s8 == 0 else ()))
                mmv = S["mm"].v
                ctx.ac(lambda e, jt=jt: e.copy(pbt[jt][:, :], ps_t[:, :]),
                       waits=[(S["mm"], mmv), (S["mm"], pbt_mm[jt])])
                pbt_ac = S["ac"].v
                pw = [(S["ac"], pbt_ac)]
                if grp == 0 and pvps_free:
                    pw.append((S["dv"], pvps_free))
                for s8 in range(8):
                    kk = grp * 8 + s8
                    ctx.mm(lambda e, kk=kk, s8=s8, jt=jt: e.matmul(
                        ps_pv[:, :], pbt[jt][:, s8 * 128:s8 * 128 + 128],
                        vtt[:, kk * 128:kk * 128 + 128],
                        start=(kk == 0), stop=(kk == 31)),
                        waits=(pw if s8 == 0 else ()))
                pbt_mm[jt] = S["mm"].v
            psb_mm = S["mm"].v
            mmv = S["mm"].v
            ctx.dv(lambda e: e.tensor_single_scalar(cxb[:, :], ps_pv[:, :], rzg[:, 0:1],
                                                    ALU.mult),
                   waits=[(S["mm"], mmv), (S["dv"], rzv)])
            pvps_free = S["dv"].v
            ctx.mm(lambda e: e.transpose(ps_t[:, 0:128], cxb[:, :], idt[:, :]),
                   waits=[(S["dv"], pvps_free), (S["ac"], pbt_ac)])
            mmv = S["mm"].v
            ctx.ac(lambda e: e.copy(cxT[:, :], ps_t[:, 0:128]),
                   waits=[(S["mm"], mmv), (S["mm"], relu_mm[0])])
            cxTv = S["ac"].v
            pbt_ac = S["ac"].v
            ctx.dma("gpsimd", owb[j][:, :], ow[e_],
                    waits=[(S["mm"], ow_mm[j])] if ow_mm[j] else [])
            owf = ctx.fence("gpsimd")
            for dc in range(8):
                bank0 = (dc < 4)
                mw2 = []
                if dc == 0:
                    mw2 = [owf, (S["ac"], cxTv)]
                    if moa_dv:
                        mw2.append((S["dv"], moa_dv))
                ctx.mm(lambda e, e_=e_, dc=dc, j=j: e.matmul(
                    ps_moa[dc // 4][:, (dc % 4) * 128:(dc % 4) * 128 + 128],
                    owb[j][:, dc * 128:dc * 128 + 128], cxT[:, :],
                    start=(e_ == 0 and dc % 4 == 0), stop=False),
                    waits=mw2)
            ow_mm[j] = S["mm"].v
            relu_mm[0] = S["mm"].v
        for dc in range(8):
            ctx.mm(lambda e, dc=dc, qtl=qtl: e.matmul(
                ps_moa[dc // 4][:, (dc % 4) * 128:(dc % 4) * 128 + 128],
                obb[0:8, dc * 128:dc * 128 + 128], gTb[0:8, qtl * 128:qtl * 128 + 128],
                start=False, stop=True),
                waits=([obf, gdone_ac] if dc == 0 else ()))
        mmv = S["mm"].v
        for dc in range(8):
            ctx.dv(lambda e, dc=dc, qtl=qtl: e.tensor_add(
                z3o[dc][:, qtl * 128:qtl * 128 + 128],
                z3o[dc][:, qtl * 128:qtl * 128 + 128],
                ps_moa[dc // 4][:, (dc % 4) * 128:(dc % 4) * 128 + 128]),
                waits=[(S["mm"], mmv)] if dc == 0 else ())
        moa_dv = S["dv"].v
    z4done = (S["dv"], S["dv"].v)
    for i in range(8):
        ctx.dma("sync", z4t_o.rearrange("(c p) t -> c p t", p=128)[i], z3o[i][:, :],
                waits=[z4done] if i == 0 else [])
    ctx.q["vector"].append(lambda e, v1=S["dgp"].v, v2=S["dsy"].v:
                           (e.wait_ge(S["dgp"].h, v1), e.wait_ge(S["dsy"].h, v2)))
    ctx.finish()
    return nc



def build_ld2():
    nc = bass.Bass("TRN2", target_bir_lowering=False)
    ctx = Ctx(nc)
    S = ctx.S

    z4t = nc.dram_tensor("z4t", [D, TC], BF16, kind="ExternalInput").ap()
    r3t = nc.dram_tensor("r3t", [D, TC], F32, kind="ExternalInput").ap()
    ln2wb = nc.dram_tensor("ln2wb", [2, 128, 8], F32, kind="ExternalInput").ap()
    gatewF = nc.dram_tensor("gatewF", [8, 128, 16], F32, kind="ExternalInput").ap()
    fw1 = nc.dram_tensor("fw1", [16, 128, 8, 512], F32, kind="ExternalInput").ap()
    fb1 = nc.dram_tensor("fb1", [128, 64], F32, kind="ExternalInput").ap()
    fw2 = nc.dram_tensor("fw2", [16, 128, 4, D], F32, kind="ExternalInput").ap()
    fb2 = nc.dram_tensor("fb2", [16, D], F32, kind="ExternalInput").ap()
    idend = nc.dram_tensor("iden16", [128, 128], BF16, kind="ExternalInput").ap()
    ones32d = nc.dram_tensor("ones32", [128, 1], F32, kind="ExternalInput").ap()
    ones16d = nc.dram_tensor("ones16b", [128, 1], BF16, kind="ExternalInput").ap()
    cvecd = nc.dram_tensor("cvec", [128, 2], F32, kind="ExternalInput").ap()
    r4t_o = nc.dram_tensor("r4t", [D, TC], F32, kind="ExternalOutput").ap()

    z4b = [ctx.sbuf(f"z4b{i}", [128, TC], BF16) for i in range(8)]
    ln2b = [ctx.sbuf(f"ln2b{i}", [128, TC], BF16) for i in range(8)]
    acc = ctx.sbuf("acc", [128, 16 * 1024], BF16)
    w1buf = [ctx.sbuf(f"w1buf{j}", [128, 8 * 512], BF16) for j in range(2)]
    w2buf = [ctx.sbuf(f"w2buf{j}", [128, 4 * D], BF16) for j in range(2)]
    hbuf = ctx.sbuf("hbuf", [128, 4 * 512], BF16)
    gF = ctx.sbuf("gF", [128, 16 * 16], F32)
    gFTb = ctx.sbuf("gFTb", [16, TC], BF16)
    gwF = ctx.sbuf("gwF", [128, 8 * 16], BF16)
    fb1s = ctx.sbuf("fb1s", [128, 16 * 4], F32)
    fb2s = ctx.sbuf("fb2s", [16, D], BF16)
    glog = ctx.sbuf("glog", [128, 16], F32)
    gm1 = ctx.sbuf("gm1", [128, 1], F32)
    gm2 = ctx.sbuf("gm2", [128, 1], F32)
    geq = ctx.sbuf("geq", [128, 16], F32)
    gden = ctx.sbuf("gden", [128, 1], F32)
    gtb_s = ctx.sbuf("gtb_s", [128, 16], BF16)
    idt = ctx.sbuf("idt", [128, 128], BF16)
    onesc = ctx.sbuf("onesc", [128, 1], F32)
    ones16 = ctx.sbuf("ones16t", [128, 1], BF16)
    cvec = ctx.sbuf("cvect", [128, 2], F32)
    r1stg = ctx.sbuf("r1stg", [128, TC], F32)
    ps_h = [ctx.psum(f"ps_h{j}", [128, 512], F32) for j in range(2)]
    ps_y = [ctx.psum(f"ps_y{j}", [128, 512], F32) for j in range(2)]
    ps_t = [ctx.psum(f"ps_t{j}", [128, 1024], BF16) for j in range(2)]
    ps_ln = [ctx.psum(f"ps_ln{j}", [1, 512], F32) for j in range(2)]

    for i in range(8):
        ctx.dma("sync", z4b[i][:, :], z4t.rearrange("(c p) t -> c p t", p=128)[i])
    ctx.dma("sync", idt[:, :], idend)
    ctx.dma("sync", onesc[:, :], ones32d)
    ctx.dma("sync", ones16[:, :], ones16d)
    ctx.dma("sync", cvec[:, :], cvecd)
    lnw = ctx.sbuf("lnw", [128, 8], F32)
    lnb = ctx.sbuf("lnb", [128, 8], F32)
    ctx.dma("sync", lnw[:, :], ln2wb[0])
    ctx.dma("sync", lnb[:, :], ln2wb[1])
    ctx.dma("sync", fb1s[:, :], fb1)
    zf = ctx.fence("sync")
    for kc in range(8):
        ctx.dma("gpsimd", gwF[:, kc * 16:kc * 16 + 16], gatewF[kc])
    ctx.dma("gpsimd", fb2s[:, :], fb2)
    gwf = ctx.fence("gpsimd")

    def applyp(i, rstdb, pb, bcf):
        ctx.dv(lambda e, i=i: e.tensor_mul(ln2b[i][:, :], z4b[i][:, :], rstdb),
               waits=[bcf, zf])
        ctx.dv(lambda e, i=i: e.tensor_sub(ln2b[i][:, :], ln2b[i][:, :], pb),
               waits=[(S["dv"], S["dv"].v)])
        ctx.dv(lambda e, i=i: e.tensor_scalar(ln2b[i][:, :], ln2b[i][:, :],
                                              lnw[:, i:i + 1], lnb[:, i:i + 1],
                                              ALU.mult, ALU.add),
               waits=[(S["dv"], S["dv"].v)])

    emit_layernorm(ctx, "ln2", 8, TC, lambda i: z4b[i][:, :], applyp, zf,
                   ones16[:, 0:1], onesc[:, 0:1], zf,
                   ps_ln[0][0:1, :], ps_ln[1][0:1, :], eps_ap=cvec[0:1, 0:1])
    lndone = (S["dv"], S["dv"].v)

    gu = 0
    gt_mm = 0
    for tc in range(16):
        jp = tc % 2
        for kc in range(8):
            ctx.mm(lambda e, tc=tc, kc=kc, jp=jp: e.matmul(
                ps_h[jp][:, 0:16], ln2b[kc][:, tc * 128:tc * 128 + 128],
                gwF[:, kc * 16:kc * 16 + 16], start=(kc == 0), stop=(kc == 7)),
                waits=([lndone, gwf, (S["dv"], gu)] if kc == 0 else ()))
        mmv = S["mm"].v
        ctx.dv(lambda e, jp=jp: e.tensor_copy(glog[:, :], ps_h[jp][:, 0:16]),
               waits=[(S["mm"], mmv), (S["ac"], getattr(ctx, "_gprev_ac", 0))])
        dvv = S["dv"].v
        ctx.dv(lambda e: e.reduce_max(gm1[:, :], glog[:, :], axis=AX.X, negate=True),
               waits=[(S["dv"], dvv)])
        dvv = S["dv"].v
        ctx.dv(lambda e: e.tensor_single_scalar(geq[:, :], glog[:, :], gm1[:, 0:1],
                                                ALU.add), waits=[(S["dv"], dvv)])
        ctx.dv(lambda e: e.tensor_single_scalar(geq[:, :], geq[:, :], 0.0, ALU.is_ge),
               waits=[(S["dv"], S["dv"].v)])
        ctx.dv(lambda e: e.scalar_tensor_tensor(geq[:, :], geq[:, :], -1e30, glog[:, :],
                                                ALU.mult, ALU.add),
               waits=[(S["dv"], S["dv"].v)])
        ctx.dv(lambda e: e.reduce_max(gm2[:, :], geq[:, :], axis=AX.X, negate=True),
               waits=[(S["dv"], S["dv"].v)])
        ctx.dv(lambda e, tc=tc: e.tensor_single_scalar(
            gF[:, tc * 16:tc * 16 + 16], glog[:, :], gm2[:, 0:1], ALU.add),
            waits=[(S["dv"], S["dv"].v)])
        ctx.dv(lambda e, tc=tc: e.tensor_single_scalar(
            gF[:, tc * 16:tc * 16 + 16], gF[:, tc * 16:tc * 16 + 16], 0.0, ALU.is_ge),
            waits=[(S["dv"], S["dv"].v)])
        dvv = S["dv"].v
        ctx.ac(lambda e: e.activation(glog[:, :], glog[:, :], AF.Exp, bias=gm1[:, 0:1]),
               waits=[(S["dv"], dvv)])
        ctx.ac(lambda e: e.activation(gden[:, :], gm2[:, 0:1], AF.Exp, scale=-1.0,
                                      bias=gm1[:, 0:1]))
        ctx._gprev_ac = S["ac"].v
        acv = S["ac"].v
        ctx.dv(lambda e: e.tensor_single_scalar(gden[:, :], gden[:, :], 1.0, ALU.add),
               waits=[(S["ac"], acv)])
        ctx.dv(lambda e: e.reciprocal(gden[:, :], gden[:, :]),
               waits=[(S["dv"], S["dv"].v)])
        dvv = S["dv"].v
        ctx.dv(lambda e, tc=tc: e.tensor_mul(gF[:, tc * 16:tc * 16 + 16],
                                             gF[:, tc * 16:tc * 16 + 16], glog[:, :]),
               waits=[(S["dv"], dvv)])
        ctx.dv(lambda e, tc=tc: e.tensor_single_scalar(
            gF[:, tc * 16:tc * 16 + 16], gF[:, tc * 16:tc * 16 + 16], gden[:, 0:1],
            ALU.mult), waits=[(S["dv"], S["dv"].v)])
        ctx.dv(lambda e, tc=tc: e.tensor_copy(gtb_s[:, :], gF[:, tc * 16:tc * 16 + 16]))
        dvv = S["dv"].v
        ctx.mm(lambda e, tc=tc: e.transpose(ps_t[0][0:16, 0:128], gtb_s[:, :], idt[:, :]),
               waits=[(S["dv"], dvv), (S["ac"], getattr(ctx, "_gtev2", 0))])
        gt_mm = S["mm"].v
        ctx.ac(lambda e, tc=tc: e.copy(gFTb[0:16, tc * 128:tc * 128 + 128],
                                       ps_t[0][0:16, 0:128]), waits=[(S["mm"], gt_mm)])
        ctx._gtev2 = S["ac"].v
        gu = S["dv"].v
    gdone = (S["dv"], S["dv"].v)
    gdone_ac = (S["ac"], S["ac"].v)

    acc_use = 0
    for tc in range(16):
        for ds in range(2):
            jp = (tc * 2 + ds) % 2
            ctx.mm(lambda e, tc=tc, ds=ds, jp=jp: e.matmul(
                ps_y[jp][:, :], gFTb[0:16, tc * 128:tc * 128 + 128],
                fb2s[0:16, ds * 512:ds * 512 + 512], start=True, stop=True),
                waits=[gdone_ac, gwf] if (tc == 0 and ds == 0) else
                      ([(S["dv"], acc_use)] if acc_use else []))
            mmv = S["mm"].v
            ctx.dv(lambda e, tc=tc, ds=ds, jp=jp: e.tensor_copy(
                acc[:, tc * 1024 + ds * 512:tc * 1024 + ds * 512 + 512], ps_y[jp][:, :]),
                waits=[(S["mm"], mmv)])
            acc_use = S["dv"].v

    w1_mm = [0, 0]
    w2_mm = [0, 0]
    hev = [0, 0]
    for e_ in range(16):
        j = e_ % 2
        ctx.dma("gpsimd", w1buf[j][:, :], fw1[e_],
                waits=[(S["mm"], w1_mm[j])] if w1_mm[j] else [])
        w1f = ctx.fence("gpsimd")
        ctx.dma("gpsimd", w2buf[j][:, :], fw2[e_],
                waits=[(S["mm"], w2_mm[j])] if w2_mm[j] else [])
        w2f = ctx.fence("gpsimd")
        for tsub in range(4):
            for hc in range(4):
                jp = hc % 2
                for kc in range(8):
                    ctx.mm(lambda e, e_=e_, hc=hc, kc=kc, tsub=tsub, jp=jp, j=j: e.matmul(
                        ps_h[jp][:, :],
                        w1buf[j][:, kc * 512 + hc * 128:kc * 512 + hc * 128 + 128],
                        ln2b[kc][:, tsub * 512:tsub * 512 + 512],
                        start=(kc == 0), stop=(kc == 7)),
                        waits=([w1f, lndone, (S["ac"], hev[jp])] if kc == 0 else ()))
                mmv = S["mm"].v
                hw = [(S["mm"], mmv)]
                if tsub == 0 and hc < 2 and w2_mm[j]:
                    hw.append((S["mm"], w2_mm[j]))
                ctx.ac(lambda e, e_=e_, hc=hc, jp=jp: e.activation(
                    hbuf[:, hc * 512:hc * 512 + 512], ps_h[jp][:, :], AF.Relu,
                    bias=fb1s[:, e_ * 4 + hc:e_ * 4 + hc + 1]), waits=hw)
                hev[jp] = S["ac"].v
            hdone = (S["ac"], S["ac"].v)
            for tcl in range(4):
                tc = tsub * 4 + tcl
                for ds in range(2):
                    jp = (tcl * 2 + ds) % 2
                    for hc in range(4):
                        ctx.mm(lambda e, hc=hc, tcl=tcl, ds=ds, jp=jp, j=j: e.matmul(
                            ps_y[jp][:, :],
                            hbuf[:, hc * 512 + tcl * 128:hc * 512 + tcl * 128 + 128],
                            w2buf[j][:, hc * 1024 + ds * 512:hc * 1024 + ds * 512 + 512],
                            start=(hc == 0), stop=(hc == 3)),
                            waits=([hdone, w2f, (S["dv"], acc_use)] if hc == 0 else ()))
                    mmv = S["mm"].v
                    ctx.dv(lambda e, e_=e_, tc=tc, ds=ds, jp=jp: e.scalar_tensor_tensor(
                        acc[:, tc * 1024 + ds * 512:tc * 1024 + ds * 512 + 512],
                        ps_y[jp][:, :], gF[:, tc * 16 + e_:tc * 16 + e_ + 1],
                        acc[:, tc * 1024 + ds * 512:tc * 1024 + ds * 512 + 512],
                        ALU.mult, ALU.add),
                        waits=[(S["mm"], mmv), gdone])
                    acc_use = S["dv"].v
            w2_mm[j] = S["mm"].v
        w1_mm[j] = S["mm"].v
    moedone = (S["dv"], S["dv"].v)

    ztev = [0, 0]
    for tc in range(16):
        for kc in range(8):
            jt = kc % 2
            tw = [moedone] if (tc == 0 and kc < 2) else []
            if ztev[jt]:
                tw = [(S["dv"], ztev[jt])]
            ctx.mm(lambda e, tc=tc, kc=kc, jt=jt: e.transpose(
                ps_t[jt][:, 0:128], acc[:, tc * 1024 + kc * 128:tc * 1024 + kc * 128 + 128],
                idt[:, :]), waits=tw)
            mmv = S["mm"].v
            ctx.dv(lambda e, tc=tc, kc=kc, jt=jt: e.tensor_add(
                z4b[kc][:, tc * 128:tc * 128 + 128],
                z4b[kc][:, tc * 128:tc * 128 + 128], ps_t[jt][:, 0:128]),
                waits=[(S["mm"], mmv)])
            ztev[jt] = S["dv"].v
    z5done = (S["dv"], S["dv"].v)

    rsf = [None]

    def applyr2(i, rstdb, pb, bcf):
        lw = [rsf[0]] if rsf[0] is not None else []
        ctx.dma("sync", r1stg[:, :], r3t.rearrange("(c p) t -> c p t", p=128)[i], waits=lw)
        rf = ctx.fence("sync")
        tmpb = acc[:, 0:2048]
        ctx.dv(lambda e, i=i, tmpb=tmpb: e.tensor_mul(tmpb, z4b[i][:, :], rstdb), waits=[bcf])
        ctx.dv(lambda e, tmpb=tmpb: e.tensor_sub(tmpb, tmpb, pb),
               waits=[(S["dv"], S["dv"].v)])
        ctx.dv(lambda e, tmpb=tmpb: e.tensor_add(r1stg[:, :], r1stg[:, :], tmpb),
               waits=[rf, (S["dv"], S["dv"].v)])
        dvv = S["dv"].v
        ctx.dma("sync", r4t_o.rearrange("(c p) t -> c p t", p=128)[i], r1stg[:, :],
                waits=[(S["dv"], dvv)])
        rsf[0] = ctx.fence("sync")

    emit_layernorm(ctx, "lnR", 8, TC, lambda i: z4b[i][:, :], applyr2, z5done,
                   ones16[:, 0:1], onesc[:, 0:1], zf,
                   ps_ln[0][0:1, :], ps_ln[1][0:1, :], eps_ap=cvec[0:1, 0:1])
    ctx.q["vector"].append(lambda e, v1=S["dgp"].v, v2=S["dsy"].v:
                           (e.wait_ge(S["dgp"].h, v1), e.wait_ge(S["dsy"].h, v2)))
    ctx.finish()
    return nc



def _consts():
    iden = np.eye(128, dtype=BF)
    ones32 = np.ones((128, 1), np.float32)
    ones16b = np.ones((128, 1), BF)
    cv = np.zeros((128, 2), np.float32)
    cv[:, 0] = 1e-5
    cv[:, 1] = 1.0
    return iden, ones32, ones16b, cv


def _run(nc, maps, cores=None):
    n = len(maps)
    return run_bass_kernel_spmd(nc, maps, core_ids=list(range(n))).results


def kernel(**P):
    iden, ones32, ones16b, cv = _consts()
    x = np.asarray(P["L0_attn_x"], np.float32)

    nc = build_la1()
    inproj = np.ascontiguousarray(
        P["L0_attn_input_proj"].reshape(8, 128, 32, 128).transpose(2, 1, 0, 3))
    bcwf = np.zeros((32, 128, 64), np.float32)
    bcwf[:, :, 0:16] = P["L0_attn_b_proj"].reshape(32, 128, NST)
    bcwf[:, :, 32:48] = P["L0_attn_c_proj"].reshape(32, 128, NST)
    dtp = np.ascontiguousarray(P["L0_attn_dt_proj"].reshape(32, 128).T)
    phases = np.ascontiguousarray(P["L0_attn_floquet_phases"].reshape(32, 128).T)
    maps = []
    xts = []
    for c in range(NCORES):
        b, h = c >> 1, c & 1
        xs = np.ascontiguousarray(x[b, h * TC:(h + 1) * TC, :].T)
        xts.append(xs)
        pos = np.ascontiguousarray(np.arange(h * TC, (h + 1) * TC, dtype=np.float32)[None, :])
        maps.append(dict(xt=xs, inproj=inproj, bcw=bcwf, dtp=dtp, phases=phases,
                         posrow=pos, ones32=ones32, cvec=cv))
    r1 = _run(nc, maps)
    hdt = [np.asarray(r1[c]["hdt"]) for c in range(NCORES)]
    encdt = [np.asarray(r1[c]["encdt"]) for c in range(NCORES)]
    bcr = [np.asarray(r1[c]["bcrows"]) for c in range(NCORES)]

    encdt_full = np.zeros((32, 128, FULLT), BF)
    bcb = np.zeros((4, 64, 4096), np.float32)
    for c in range(NCORES):
        b, h = c >> 1, c & 1
        cols = slice(b * 4096 + h * TC, b * 4096 + h * TC + TC)
        encdt_full[:, :, cols] = encdt[c]
        bcb[b][:, h * TC:h * TC + TC] = bcr[c]
    alog = np.ascontiguousarray(P["L0_attn_a_log"].astype(np.float32)[None, :])
    nc = build_la2()
    maps = []
    for c in range(NCORES):
        maps.append(dict(encdt_r=np.ascontiguousarray(encdt_full[4 * c:4 * c + 4]),
                         bcb=bcb, dtp_sl=np.ascontiguousarray(dtp[:, 4 * c:4 * c + 4]),
                         alog=alog, idenb=iden, cvec=cv))
    r2 = _run(nc, maps)
    yts = [np.asarray(r2[c]["yt"]) for c in range(NCORES)]

    skipw = np.ascontiguousarray(
        P["L0_attn_skip_proj"].reshape(32, 128, 32, 128).transpose(2, 1, 0, 3))
    outw = np.ascontiguousarray(
        P["L0_attn_output_proj"].reshape(32, 128, 8, 128).transpose(2, 1, 0, 3))
    nc = build_la3()
    maps = []
    for c in range(NCORES):
        b, h = c >> 1, c & 1
        cols = slice(b * 4096 + h * TC, b * 4096 + h * TC + TC)
        ytc = np.zeros((32, 128, TC), BF)
        for g in range(32):
            ytc[g] = yts[g // 4][g % 4][:, cols]
        maps.append(dict(hdt=hdt[c], ytc=ytc, skipw=skipw, outw=outw, xt=xts[c],
                         ones32=ones32, ones16b=ones16b, cvec=cv))
    r3 = _run(nc, maps)
    r1t = [np.asarray(r3[c]["r1t"]) for c in range(NCORES)]
    zt = [np.asarray(r3[c]["zt"]) for c in range(NCORES)]

    wqkv = np.ascontiguousarray(
        P["L0_ffn_w_qkv"].reshape(8, 128, 3, 16, 64).transpose(3, 1, 2, 0, 4))
    wproj = np.ascontiguousarray(
        P["L0_ffn_w_proj"].reshape(8, 128, 8, 128).transpose(2, 1, 0, 3))
    ln1wb = np.ascontiguousarray(np.stack([P["L0_ffn_ln1_w"].reshape(8, 128).T,
                                           P["L0_ffn_ln1_b"].reshape(8, 128).T]))
    nc = build_lb1(0)
    maps = []
    for c in range(NCORES):
        b, h = c >> 1, c & 1
        zfull = np.ascontiguousarray(np.concatenate([zt[c], zt[c ^ 1]], axis=1))
        maps.append(dict(zfull=zfull, r1t=r1t[c], wqkv=wqkv, wproj=wproj,
                         ln1wb=ln1wb, iden16=iden, ones32=ones32, ones16b=ones16b,
                         cvec=cv))
    res = _run(nc, maps)
    z1t = [np.asarray(res[c]["z1t"]) for c in range(NCORES)]

    ln2wb = np.ascontiguousarray(np.stack([P["L0_ffn_ln2_w"].reshape(8, 128).T,
                                           P["L0_ffn_ln2_b"].reshape(8, 128).T]))
    gatew = np.ascontiguousarray(P["L0_ffn_gate_w"].reshape(8, 128, 8))
    gateb = np.ascontiguousarray(P["L0_ffn_gate_b"][None, :].astype(np.float32))
    w1r = np.ascontiguousarray(
        P["L0_ffn_w1"].reshape(8, 8, 128, 2, 2048).transpose(0, 3, 2, 1, 4))
    w2r = np.ascontiguousarray(
        P["L0_ffn_w2"].reshape(8, 2, 16, 128, 1024).transpose(0, 1, 3, 2, 4))
    nc = build_lb2()
    maps = [dict(z1t=z1t[c], r1t=r1t[c], ln2wb=ln2wb, gatew=gatew, gateb=gateb,
                 w1r=w1r, w2r=w2r, iden16=iden, ones32=ones32, ones16b=ones16b, cvec=cv)
            for c in range(NCORES)]
    rb2 = _run(nc, maps)
    r2t = [np.asarray(rb2[c]["r2t"]) for c in range(NCORES)]

    wqkv2 = np.ascontiguousarray(
        P["L1_attn_W_qkv"].reshape(16, 8, 128, 3, 64).transpose(0, 2, 3, 1, 4))
    mskc = np.zeros((4, 128, 512), np.float32)
    for r in range(4):
        for p in range(128):
            mskc[r, p, r * 128 + p + 1:] = -1e10
    nc = build_lc1()
    maps = []
    for c in range(NCORES):
        h = c & 1
        r2full = np.ascontiguousarray(np.concatenate([r2t[c], r2t[c ^ 1]], axis=1))
        mp = np.full((128, 1), 0.0 if h == 1 else -1e10, np.float32)
        maps.append(dict(r2full=r2full, wqkv2=wqkv2, mskc=mskc, maskP=mp, iden16=iden,
                         ones32=ones32, ones16b=ones16b, cvec=cv))
    res = _run(nc, maps)
    aos = [np.asarray(res[c]["ao"]) for c in range(NCORES)]

    woutf = np.ascontiguousarray(P["L1_attn_W_out"].reshape(1024, 1024).reshape(8, 128, 8, 128))
    nc = build_lc2()
    maps = []
    for c in range(NCORES):
        b, h = c >> 1, c & 1
        AO = np.concatenate([aos[2 * b], aos[2 * b + 1]], axis=1)
        A4 = AO.reshape(16, 64, 64, 64)
        qw_ = slice(h * 32, h * 32 + 32)
        resh = np.ascontiguousarray(A4[:, :, qw_, :]).reshape(8, 128, TC)
        maps.append(dict(aoresh=resh, woutf=woutf, r2t=r2t[c], ones32=ones32,
                         ones16b=ones16b, cvec=cv))
    rc2 = _run(nc, maps)
    r3t = [np.asarray(rc2[c]["r3t"]) for c in range(NCORES)]

    ln1wb2 = np.ascontiguousarray(np.stack([P["L1_ffn_ln1_w"].reshape(8, 128).T,
                                            P["L1_ffn_ln1_b"].reshape(8, 128).T]))
    gatewA = np.ascontiguousarray(P["L1_ffn_attn_gate_w"].reshape(8, 128, 8))
    qw = np.ascontiguousarray(P["L1_ffn_q_w"].reshape(8, 8, 128, 128).transpose(0, 2, 1, 3))
    qbc = np.ascontiguousarray(P["L1_ffn_q_b"].T.astype(np.float32))
    kw = np.ascontiguousarray(P["L1_ffn_k_w"].reshape(8, 128, 128).transpose(1, 0, 2))
    kbc = np.ascontiguousarray(P["L1_ffn_k_b"][:, None].astype(np.float32))
    vw = np.ascontiguousarray(P["L1_ffn_v_w"].reshape(8, 128, 128))
    vbrow = np.ascontiguousarray(P["L1_ffn_v_b"][None, :].astype(np.float32))
    ow = np.ascontiguousarray(P["L1_ffn_o_w"].reshape(8, 128, 8, 128))
    obb_d = np.ascontiguousarray(P["L1_ffn_o_b"].astype(np.float32))
    reltT = np.zeros((128, 132), np.float32)
    reltT[:, 0:129] = P["L1_ffn_rel_table"].T
    bmasks = np.zeros((3, 128, 256), np.float32)
    for p in range(128):
        dd = np.arange(256) - p
        bmasks[0, p] = ((dd > 0) & (dd < 128)).astype(np.float32)
        bmasks[1, p] = (dd <= 0).astype(np.float32)
        bmasks[2, p] = (dd >= 128).astype(np.float32)
    nc = build_ld1()
    maps = []
    for c in range(NCORES):
        h = c & 1
        r3full = np.ascontiguousarray(np.concatenate([r3t[c], r3t[c ^ 1]], axis=1))
        reltc = reltT.copy()
        reltc[:, 129] = reltT[:, 0] if h == 1 else reltT[:, 128]
        wrapm_c = np.zeros((2, 128, 64), BF)
        if h == 1:
            for p in range(128):
                wrapm_c[0, p, p + 1:] = 1.0
        else:
            for p in range(128):
                jmax = p - 64
                if jmax > 0:
                    wrapm_c[1, p, 0:min(jmax, 64)] = 1.0
        maps.append(dict(r3full=r3full, wrapm=wrapm_c, ln1wb=ln1wb2, gatewA=gatewA,
                         qw=qw, qbc=qbc, kw=kw, kbc=kbc, vw=vw, vbrow=vbrow, ow=ow,
                         obb_d=obb_d, reltT=reltc, bmasks=bmasks, iden16=iden,
                         ones32=ones32, ones16b=ones16b, cvec=cv))
    res = _run(nc, maps)
    z4t = [np.asarray(res[c]["z4t"]) for c in range(NCORES)]

    ln2wb2 = np.ascontiguousarray(np.stack([P["L1_ffn_ln2_w"].reshape(8, 128).T,
                                            P["L1_ffn_ln2_b"].reshape(8, 128).T]))
    gatewF = np.ascontiguousarray(P["L1_ffn_ff_gate_w"].reshape(8, 128, 16))
    fw1 = np.ascontiguousarray(P["L1_ffn_ff_w1"].reshape(16, 8, 128, 512).transpose(0, 2, 1, 3))
    fb1 = np.ascontiguousarray(
        P["L1_ffn_ff_b1"].reshape(16, 4, 128).transpose(2, 0, 1).reshape(128, 64))
    fw2 = np.ascontiguousarray(P["L1_ffn_ff_w2"].reshape(16, 4, 128, 1024).transpose(0, 2, 1, 3))
    fb2 = np.ascontiguousarray(P["L1_ffn_ff_b2"].astype(np.float32))
    nc = build_ld2()
    maps = [dict(z4t=z4t[c], r3t=r3t[c], ln2wb=ln2wb2, gatewF=gatewF, fw1=fw1, fb1=fb1,
                 fw2=fw2, fb2=fb2, iden16=iden, ones32=ones32, ones16b=ones16b, cvec=cv)
            for c in range(NCORES)]
    rd2 = _run(nc, maps)

    out = np.zeros((4, 4096, 1024), np.float32)
    for c in range(NCORES):
        b, h = c >> 1, c & 1
        out[b, h * TC:(h + 1) * TC] = np.asarray(rd2[c]["r4t"]).T
    return out
